# version 08
# baseline (speedup 1.0000x reference)
import contextlib
import numpy as np
import ml_dtypes
import concourse.bass as bass
import concourse.mybir as mybir
from concourse.bass_utils import run_bass_kernel_spmd

F32 = mybir.dt.float32
BF16 = mybir.dt.bfloat16
U8 = mybir.dt.uint8
OP = mybir.AluOpType
AF = mybir.ActivationFunctionType
AX = mybir.AxisListType

D = 2048
SEQ = 2048
NH = 16
IFF = 8192
NC = D // 128
INV_SQRT_DH = float(1.0 / np.sqrt(np.float64(128.0)))
LN2C = float(np.log(2.0))
INV_LN2 = float(1.0 / np.log(2.0))

PHASES = 5
DEBUG = None
QW = 512


class Sem:
    __slots__ = ("name", "count", "handle")

    def __init__(self, name):
        self.name, self.count, self.handle = name, 0, None

    def inc(self, n=1):
        self.count += n
        return self.count


class Prog:
    ENGINES = ("sync", "tensor", "vector", "scalar", "gpsimd")

    def __init__(self):
        self.ops = []
        self.sems = {}

    def sem(self, name):
        if name not in self.sems:
            self.sems[name] = Sem(name)
        return self.sems[name]

    def op(self, engine, fn, waits=(), inc=None, sep=False):
        if inc is not None:
            inc[0].inc(inc[1])
        self.ops.append((engine, tuple(waits), inc, fn, sep))


class Stream:

    def __init__(self, prog, name, engine, nbuf):
        self.prog, self.engine, self.nbuf = prog, engine, nbuf
        self.sems = [prog.sem(f"{name}_{i}") for i in range(nbuf)]
        self.idx = 0

    def load(self, fn, waits=()):
        sem = self.sems[self.idx % self.nbuf]
        self.idx += 1
        self.prog.op(self.engine, fn, waits=waits, inc=(sem, 16))
        return sem, sem.count

    def totals(self):
        return [(sem, sem.count) for sem in self.sems]


def emit_program(nc, prog):
    with contextlib.ExitStack() as stack:
        for s in prog.sems.values():
            s.handle = stack.enter_context(nc.semaphore(s.name))
        block = stack.enter_context(nc.Block())
        per_engine = {e: [] for e in Prog.ENGINES}
        for rec in prog.ops:
            per_engine[rec[0]].append(rec)

        def runner(engname):
            def run(eng):
                marks = {}
                for _, waits, inc, fn, sep in per_engine[engname]:
                    for sem, val in waits:
                        if val > 0 and marks.get(sem.name, -1) < val:
                            eng.wait_ge(sem.handle, val)
                            marks[sem.name] = val
                    if sep:
                        eng.drain()
                    ins = fn(eng)
                    if inc is not None:
                        sem, amount = inc
                        ins.then_inc(sem.handle, amount)
            return run

        for e in Prog.ENGINES:
            getattr(block, e)(runner(e))


class Overlay:
    def __init__(self, nc, base_bytes, size_bytes, tag):
        self.nc, self.base, self.size, self.tag = nc, base_bytes, size_bytes, tag
        self.cur = base_bytes
        self.n = 0

    def reset(self):
        self.cur = self.base

    def alloc(self, name, shape, dt=F32):
        esz = {F32: 4, BF16: 2, U8: 1}[dt]
        nbytes = int(np.prod(shape[1:])) * esz
        off = (self.cur + 31) & ~31
        self.cur = off + nbytes
        assert self.cur <= self.base + self.size, \
            f"overlay {self.tag} overflow at {name}: {self.cur - self.base} > {self.size}"
        self.n += 1
        return self.nc.alloc_sbuf_tensor_at(f"{self.tag}{self.n}_{name}", shape, dt,
                                            offset=off).ap()


def build_nc(debug=None, phases=PHASES):
    nc = bass.Bass("TRN2", target_bir_lowering=False, debug=False)
    p = Prog()

    xT = nc.dram_tensor("xT", [D, SEQ], F32, kind="ExternalInput").ap()
    wqkv = nc.dram_tensor("wqkv", [48, D, 128], F32, kind="ExternalInput").ap()
    wo = nc.dram_tensor("wo", [16, D, 128], F32, kind="ExternalInput").ap()
    wg_h = nc.dram_tensor("wg", [IFF, D], F32, kind="ExternalInput").ap()
    wu_h = nc.dram_tensor("wu", [IFF, D], F32, kind="ExternalInput").ap()
    wd_h = nc.dram_tensor("wd", [D, IFF], F32, kind="ExternalInput").ap()
    ccol_h = nc.dram_tensor("ccol", [128, NC], F32, kind="ExternalInput").ap()
    wmod6 = nc.dram_tensor("wmod6", [6, D, D], F32, kind="ExternalInput").ap()

    outT = nc.dram_tensor("outT", [D, SEQ], F32, kind="ExternalOutput").ap()
    dbg = nc.dram_tensor("dbg", [D, SEQ], F32, kind="ExternalOutput").ap() if debug else None

    h_scr = nc.dram_tensor("h_scr", [D, SEQ], F32).ap()
    val_scr = nc.dram_tensor("val_scr", [D, SEQ], F32).ap()
    x1_scr = nc.dram_tensor("x1_scr", [D, SEQ], F32).ap()
    wgqT_scr = nc.dram_tensor("wgqT_scr", [D, IFF], BF16).ap()
    h2q_dbg = nc.dram_tensor("h2q_dbg", [4, 128, 16 * 512], BF16).ap()
    iq_dbg = nc.dram_tensor("iq_dbg", [4, 128, 64 * 512], BF16).ap()
    wuqT_scr = nc.dram_tensor("wuqT_scr", [D, IFF], BF16).ap()
    wdqT_scr = nc.dram_tensor("wdqT_scr", [16, IFF, 128], BF16).ap()

    sb = lambda name, shape, dt=F32: nc.alloc_sbuf_tensor(name, shape, dt).ap()
    ones_col = sb("ones_col", [128, 1])
    flagc = sb("flagc", [128, 1])
    c_round = sb("c_round", [128, 1])
    c_velt_s = sb("c_velt_s", [128, 1])
    c_fix3 = sb("c_fix3", [128, 1])
    c_velt4 = sb("c_velt4", [128, 1])
    c_fix1 = sb("c_fix1", [128, 1])
    c_velt2 = sb("c_velt2", [128, 1])
    ident = sb("ident", [128, 128])
    identb = sb("identb", [128, 128], BF16)
    sc_col = sb("sc_col", [128, NC])
    mod_cols = sb("mod_cols", [128, 6 * NC])
    a1_col = sb("a1_col", [128, NC])
    b1_col = sb("b1_col", [128, NC])
    ga_col = sb("ga_col", [128, NC])
    gb_col = sb("gb_col", [128, NC])
    m2_col = sb("m2_col", [128, NC])
    r2_col = sb("r2_col", [128, NC])
    gcol_scr = sb("gcol_scr", [128, NC])
    negm_col = sb("negm_col", [128, 1])
    zrec_col = sb("zrec_col", [128, 1])
    zsum_col = sb("zsum_col", [128, 1])
    prow = sb("prow", [1, 4480])
    slotA = prow[:, 0:2048]
    slotB = prow[:, 2048:4096]
    ones128 = prow[:, 4096:4224]
    id11 = prow[:, 4224:4225]
    eps11 = prow[:, 4225:4226]

    TOP = 229344
    OV_MAIN = Overlay(nc, TOP - 168 * 1024, 168 * 1024, "M")
    OV_Q = Overlay(nc, TOP - 182 * 1024, 14 * 1024, "Q")

    ps = nc.alloc_psum_tensor("ps", [128, 4096], F32).ap()

    def bank(i, cols=512, off=0):
        return ps[:, i * 512 + off:i * 512 + off + cols]

    def bank_row(i, cols=512):
        return ps[0:1, i * 512:i * 512 + cols]

    s_wld = p.sem("s_wld")
    s_xld = p.sem("s_xld")
    s_st = p.sem("s_st")
    s_sld = p.sem("s_sld")
    s_mm = p.sem("s_mm")
    s_v = p.sem("s_v")
    s_s = p.sem("s_s")
    s_g = p.sem("s_g")
    s_out = p.sem("s_out")

    def init_consts():
        p.op("vector", lambda v: v.memset(ones_col, 1.0))
        p.op("vector", lambda v: v.memset(prow, 0.0))
        p.op("vector", lambda v: v.memset(ones128, 1.0), sep=True)
        p.op("vector", lambda v: v.memset(id11, 1.0))
        p.op("vector", lambda v: v.memset(eps11, 1e-5))
        p.op("vector", lambda v: v.tensor_scalar(
            out=flagc, in0=ones_col, scalar1=float(2.0 ** -30), scalar2=None,
            op0=OP.add))
        p.op("vector", lambda v: v.drain())
        p.op("vector", lambda v: v.tensor_tensor(
            out=flagc, in0=flagc, in1=ones_col, op=OP.is_equal))
        p.op("vector", lambda v: v.drain())

        def mkconst(dst, c32, c64):
            p.op("vector", lambda v, dst=dst: v.tensor_scalar(
                out=dst, in0=flagc, scalar1=-1.0, scalar2=1.0,
                op0=OP.mult, op1=OP.add))
            p.op("vector", lambda v, dst=dst, c64=c64: v.tensor_scalar(
                out=dst, in0=dst, scalar1=float(c64), scalar2=None, op0=OP.mult),
                sep=True)
            p.op("vector", lambda v, dst=dst, c32=c32: v.scalar_tensor_tensor(
                out=dst, in0=flagc, scalar=float(c32), in1=dst,
                op0=OP.mult, op1=OP.add), sep=True)

        mkconst(c_round, 1.5 * 2.0 ** 23, 1.5 * 2.0 ** 52)
        mkconst(c_velt_s, 2.0 ** 22 + 1, 2.0 ** 51 + 1)
        mkconst(c_fix3, 1.5 * 2.0 ** 20, 1.5 * 2.0 ** 49)
        mkconst(c_velt4, 2.0 ** 20 + 1, 2.0 ** 49 + 1)
        mkconst(c_fix1, 1.5 * 2.0 ** 22, 1.5 * 2.0 ** 51)
        mkconst(c_velt2, 2.0 ** 22 + 1, 2.0 ** 51 + 1)
        p.op("vector", lambda v: v.drain(), inc=(s_v, 1))
        p.op("gpsimd", lambda g: g.memset(ident, 0.0))
        p.op("gpsimd", lambda g: g.affine_select(
            out=ident, in_=ident, compare_op=OP.not_equal, fill=1.0, base=0,
            pattern=[[-1, 128]], channel_multiplier=1), sep=True)
        p.op("gpsimd", lambda g: g.memset(identb, 0.0))
        p.op("gpsimd", lambda g: g.affine_select(
            out=identb, in_=identb, compare_op=OP.not_equal, fill=1.0, base=0,
            pattern=[[-1, 128]], channel_multiplier=1), inc=(s_g, 1), sep=True)

    def phase_mod():
        slabs = [OV_MAIN.alloc("slab0", [128, 2048]),
                 OV_MAIN.alloc("slab1", [128, 2048])]
        p.op("sync", lambda e: e.dma_start(out=sc_col, in_=ccol_h), inc=(s_sld, 16))
        t_cld = s_sld.count
        p.op("scalar", lambda e: e.activation(
            out=sc_col, in_=sc_col, func=AF.Silu, bias=0.0, scale=1.0),
            waits=[(s_sld, t_cld), (s_v, 1)], inc=(s_s, 1))
        t_silu = s_s.count

        t_evac_v = 0
        t_tr_mm = 0
        slab_stream = Stream(p, "sw", "sync", 2)
        for seg in range(6):
            for kc in range(NC):
                i = seg * NC + kc
                waits = [(s_mm, seg_slab_mm[i - 2])] if i >= 2 else []
                lsem, t_ld = slab_stream.load(
                    lambda e, seg=seg, kc=kc, i=i: e.dma_start(
                        out=slabs[i % 2], in_=wmod6[seg, kc * 128:(kc + 1) * 128, :]),
                    waits=waits)
                mm_w = [(lsem, t_ld), (s_s, t_silu)]
                if kc == 0 and seg > 0:
                    mm_w.append((s_v, t_evac_v))
                for piece in range(4):
                    p.op("tensor", lambda e, kc=kc, piece=piece, i=i: e.matmul(
                        bank_row(piece), sc_col[:, kc:kc + 1],
                        slabs[i % 2][:, piece * 512:(piece + 1) * 512],
                        start=(kc == 0), stop=(kc == NC - 1)),
                        waits=mm_w if piece == 0 else [],
                        inc=(s_mm, 1) if piece == 3 else None)
                seg_slab_mm.append(s_mm.count)
            t_seg_mm = s_mm.count
            p.op("vector", lambda v: v.tensor_copy(out=slotA, in_=ps[0:1, 0:2048]),
                 waits=[(s_mm, t_seg_mm), (s_mm, t_tr_mm)], inc=(s_v, 1))
            t_evac_v = s_v.count
            for j in range(16):
                jj = seg * 16 + j
                p.op("tensor", lambda e, j=j, jj=jj: e.matmul(
                    bank(4, 96)[:, jj:jj + 1], slotA[:, j * 128:(j + 1) * 128],
                    id11, start=(jj == 0), stop=(jj == 95)),
                    waits=[(s_v, t_evac_v), (s_g, 1)] if j == 0 else [],
                    inc=(s_mm, 1) if j == 15 else None)
            t_tr_mm = s_mm.count

        p.op("vector", lambda v: v.tensor_copy(out=mod_cols, in_=bank(4, 96)),
             waits=[(s_mm, t_tr_mm)], inc=(s_v, 1))
        t_cols_v = s_v.count

        p.op("vector", lambda v: v.tensor_scalar(
            out=a1_col, in0=mod_cols[:, NC:2 * NC], scalar1=1.0, scalar2=None,
            op0=OP.add), waits=[(s_v, t_cols_v)])
        p.op("vector", lambda v: v.tensor_scalar(
            out=b1_col, in0=mod_cols[:, 4 * NC:5 * NC], scalar1=1.0, scalar2=None,
            op0=OP.add))

        def gtransform(dst, src):
            p.op("vector", lambda v, dst=dst, src=src: v.tensor_scalar(
                out=dst, in0=src, scalar1=0.0, scalar2=2.0,
                op0=OP.is_ge, op1=OP.mult))
            p.op("vector", lambda v, dst=dst: v.tensor_scalar(
                out=dst, in0=dst, scalar1=-1.0, scalar2=None, op0=OP.add), sep=True)
            p.op("vector", lambda v, dst=dst, src=src: v.tensor_tensor(
                out=gcol_scr, in0=dst, in1=src, op=OP.mult), sep=True)
            p.op("vector", lambda v: v.tensor_scalar(
                out=gcol_scr, in0=gcol_scr, scalar1=0.25, scalar2=None, op0=OP.max),
                sep=True)
            p.op("vector", lambda v, dst=dst: v.tensor_tensor(
                out=dst, in0=dst, in1=gcol_scr, op=OP.mult), sep=True)

        gtransform(ga_col, mod_cols[:, 2 * NC:3 * NC])
        gtransform(gb_col, mod_cols[:, 5 * NC:6 * NC])
        p.op("vector", lambda v: v.drain(), inc=(s_v, 1))
        return s_v.count, t_tr_mm, slab_stream.totals()

    def phase_ln1_h(mod_v, mod_mm, slab_done):
        xb = [OV_MAIN.alloc("xb0", [128, 2048]), OV_MAIN.alloc("xb1", [128, 2048])]
        sqb = OV_MAIN.alloc("sqb", [128, 2048])
        hb = [OV_MAIN.alloc("hb0", [128, 2048]), OV_MAIN.alloc("hb1", [128, 2048])]
        repa = OV_MAIN.alloc("repa", [128, 2048])
        repb = OV_MAIN.alloc("repb", [128, 2048])

        chunk_mm = []
        xs = Stream(p, "xs", "gpsimd", 2)
        for kc in range(NC):
            waits = [(s_mm, chunk_mm[kc - 2])] if kc >= 2 else \
                (slab_done + [(s_mm, mod_mm)])
            lsem, t_ld = xs.load(lambda g, kc=kc: g.dma_start(
                out=xb[kc % 2], in_=xT[kc * 128:(kc + 1) * 128, :]),
                waits=waits)
            sq_w = [(lsem, t_ld)]
            if kc >= 1:
                sq_w.append((s_mm, chunk_mm[kc - 1]))
            p.op("scalar", lambda e, kc=kc: e.activation(
                out=sqb, in_=xb[kc % 2], func=AF.Square, bias=0.0, scale=1.0),
                waits=sq_w, inc=(s_s, 1))
            t_sq = s_s.count
            mm_w = [(lsem, t_ld), (s_v, 1)]
            if kc == 0:
                mm_w += [(s_mm, mod_mm), (s_v, mod_v)]
            for piece in range(4):
                p.op("tensor", lambda e, kc=kc, piece=piece: e.matmul(
                    bank_row(piece), ones_col,
                    xb[kc % 2][:, piece * 512:(piece + 1) * 512],
                    start=(kc == 0), stop=(kc == NC - 1)),
                    waits=mm_w if piece == 0 else [])
            for piece in range(4):
                p.op("tensor", lambda e, kc=kc, piece=piece: e.matmul(
                    bank_row(4 + piece), ones_col,
                    sqb[:, piece * 512:(piece + 1) * 512],
                    start=(kc == 0), stop=(kc == NC - 1)),
                    waits=[(s_s, t_sq)] if piece == 0 else [],
                    inc=(s_mm, 1) if piece == 3 else None)
            chunk_mm.append(s_mm.count)
        t_stats = s_mm.count

        p.op("vector", lambda v: v.tensor_scalar(
            out=slotB, in0=ps[0:1, 0:2048], scalar1=float(1.0 / D),
            scalar2=None, op0=OP.mult), waits=[(s_mm, t_stats)])
        p.op("vector", lambda v: v.tensor_scalar(
            out=ps[0:1, 2048:4096], in0=ps[0:1, 2048:4096], scalar1=float(1.0 / D),
            scalar2=None, op0=OP.mult))
        p.op("vector", lambda v: v.tensor_tensor(
            out=slotA, in0=slotB, in1=slotB, op=OP.mult), sep=True)
        p.op("vector", lambda v: v.tensor_tensor(
            out=ps[0:1, 2048:4096], in0=ps[0:1, 2048:4096], in1=slotA,
            op=OP.subtract), inc=(s_v, 1), sep=True)
        t_var = s_v.count
        p.op("scalar", lambda e: e.activation(
            out=slotA, in_=ps[0:1, 2048:4096], func=AF.Sqrt,
            bias=eps11, scale=1.0),
            waits=[(s_v, t_var)], inc=(s_s, 1))
        t_rstd = s_s.count
        p.op("vector", lambda v: v.reciprocal(out=slotA, in_=slotA),
             waits=[(s_s, t_rstd)])
        p.op("vector", lambda v: v.scalar_tensor_tensor(
            out=slotB, in0=slotB, scalar=-1.0, in1=slotA,
            op0=OP.mult, op1=OP.mult), inc=(s_v, 1), sep=True)
        t_rows = s_v.count

        evac_hist = []
        for k in range(8):
            which, piece = divmod(k, 4)
            src = slotA if which == 0 else slotB
            w = [(s_v, t_rows)]
            if k >= 2:
                w.append((s_v, evac_hist[k - 2]))
            p.op("tensor", lambda e, src=src, piece=piece, k=k: e.matmul(
                bank(k % 2), ones128, src[:, piece * 512:(piece + 1) * 512],
                start=True, stop=True),
                waits=w, inc=(s_mm, 1))
            t_k = s_mm.count
            rdst = repa if which == 0 else repb
            p.op("vector", lambda v, rdst=rdst, piece=piece, k=k: v.tensor_copy(
                out=rdst[:, piece * 512:(piece + 1) * 512], in_=bank(k % 2)),
                waits=[(s_mm, t_k)], inc=(s_v, 1))
            evac_hist.append(s_v.count)
        t_reps = s_v.count

        t1h, t2h, sth = [], [], []
        hst = Stream(p, "hst", "gpsimd", 2)
        for kc in range(NC):
            ld_w = [(s_v, t1h[kc - 2])] if kc >= 2 else [(s_mm, t_stats)]
            lsem, t_ld = xs.load(lambda g, kc=kc: g.dma_start(
                out=xb[kc % 2], in_=xT[kc * 128:(kc + 1) * 128, :]),
                waits=ld_w)
            t1_w = [(lsem, t_ld), (s_v, t_reps)]
            if kc >= 1:
                t1_w.append((s_g, t2h[kc - 1]))
            p.op("vector", lambda v, kc=kc: v.tensor_tensor(
                out=sqb, in0=xb[kc % 2], in1=repa, op=OP.mult),
                waits=t1_w, inc=(s_v, 1))
            t1h.append(s_v.count)
            t2_w = [(s_v, t1h[-1])]
            if kc >= 2:
                t2_w.append(sth[kc - 2])
            p.op("gpsimd", lambda g, kc=kc: g.tensor_tensor(
                out=hb[kc % 2], in0=sqb, in1=repb, op=OP.add),
                waits=t2_w, inc=(s_g, 1))
            t2h.append(s_g.count)
            p.op("vector", lambda v, kc=kc: v.tensor_scalar(
                out=hb[kc % 2], in0=hb[kc % 2],
                scalar1=a1_col[:, kc:kc + 1], scalar2=mod_cols[:, kc:kc + 1],
                op0=OP.mult, op1=OP.add),
                waits=[(s_g, t2h[-1])], inc=(s_v, 1))
            t_h = s_v.count
            ssem, t_s = hst.load(lambda g, kc=kc: g.dma_start(
                out=h_scr[kc * 128:(kc + 1) * 128, :], in_=hb[kc % 2]),
                waits=[(s_v, t_h)])
            if debug == "h":
                ssem, t_s = hst.load(lambda g, kc=kc: g.dma_start(
                    out=dbg[kc * 128:(kc + 1) * 128, :], in_=hb[kc % 2]))
            sth.append((ssem, t_s))
        return hst.totals()

    def phase_attn(h_done, p2_v, p2_mm):
        hq8 = [OV_MAIN.alloc("hq0", [128, 4096]), OV_MAIN.alloc("hq1", [128, 4096])]
        qkv6 = OV_MAIN.alloc("qkv6", [128, 6 * 2048])
        wqp = OV_MAIN.alloc("wqp", [128, 6 * 2048])
        Ebuf = OV_MAIN.alloc("Ebuf", [128, 2048])

        h8 = Stream(p, "h8", "gpsimd", 2)
        wqs = Stream(p, "wqs", "sync", 6)
        vst = Stream(p, "vst", "gpsimd", 2)

        h_view = h_scr.rearrange("(c p) t -> c p t", p=128)
        val_view = val_scr.rearrange("(c p) t -> c p t", p=128)

        proj_done_hist = []
        wq_done = {}
        evac_hist = {}
        last_exp_s = 0
        last_val_v = 0
        t_store_hist = []
        prev_pair_stores = []

        for hp in range(8):
            blocks = [2 * hp, 16 + 2 * hp, 32 + 2 * hp,
                      2 * hp + 1, 16 + 2 * hp + 1, 32 + 2 * hp + 1]
            for slot, blk in enumerate(blocks):
                w = []
                if hp == 0:
                    w = list(h_done)
                    w.append((s_v, p2_v))
                else:
                    w = [(s_mm, pair_proj_mm[hp - 1])]
                wqp3 = wqp.rearrange("p (s k f) -> p s k f", s=6, k=NC)
                wq_done[slot] = wqs.load(
                    lambda e, slot=slot, blk=blk, wqp3=wqp3: e.dma_start(
                        out=wqp3[:, slot], in_=wqkv[blk].rearrange("(k p) f -> k p f", p=128)),
                    waits=w if slot < 2 else [])
            for e8 in range(8):
                ge = hp * 8 + e8
                w = []
                if ge >= 2:
                    w = [(s_mm, proj_done_hist[ge - 2])]
                elif hp == 0:
                    w = list(h_done) + [(s_v, p2_v), (s_mm, p2_mm)]
                hsem, t_h = h8.load(
                    lambda g, e8=e8, par=ge % 2: g.dma_start(
                        out=hq8[par].rearrange("p (k t) -> p k t", t=256),
                        in_=h_view[:, :, e8 * 256:(e8 + 1) * 256]),
                    waits=w)
                for slot in range(6):
                    bk = slot % 2
                    mm_w = [(hsem, t_h), wq_done[slot]]
                    if bk in evac_hist:
                        mm_w.append((s_v, evac_hist[bk]))
                    if ge == 0 and slot == 0:
                        mm_w.append((s_mm, p2_mm))
                    for kc in range(NC):
                        p.op("tensor", lambda e, slot=slot, kc=kc, bk=bk, par=ge % 2: e.matmul(
                            bank(bk, 256), wqp[:, slot * 2048 + kc * 128:slot * 2048 + (kc + 1) * 128],
                            hq8[par][:, kc * 256:(kc + 1) * 256],
                            start=(kc == 0), stop=(kc == NC - 1)),
                            waits=mm_w if kc == 0 else [],
                            inc=(s_mm, 1) if kc == NC - 1 else None)
                    t_g = s_mm.count
                    ev_w = [(s_mm, t_g)]
                    if e8 == 0 and slot == 0 and prev_pair_stores:
                        ev_w += prev_pair_stores
                    p.op("vector", lambda v, slot=slot, e8=e8, bk=bk: v.tensor_copy(
                        out=qkv6[:, slot * 2048 + e8 * 256:slot * 2048 + (e8 + 1) * 256],
                        in_=bank(bk, 256)),
                        waits=ev_w, inc=(s_v, 1))
                    evac_hist[bk] = s_v.count
                proj_done_hist.append(s_mm.count)
            pair_proj_mm.append(s_mm.count)
            t_proj_v = s_v.count

            for hh in range(2):
                qs = (3 * hh) * 2048
                ks = (3 * hh + 1) * 2048
                vs = (3 * hh + 2) * 2048
                head = 2 * hp + hh
                for j in range(16):
                    sc_w = [(s_v, t_proj_v)]
                    if last_exp_s:
                        sc_w.append((s_s, last_exp_s))
                    if j == 0 and last_val_v:
                        sc_w.append((s_v, last_val_v))
                    for pc in range(4):
                        p.op("tensor", lambda e, j=j, pc=pc, qs=qs, ks=ks: e.matmul(
                            bank(pc), qkv6[:, qs + j * 128:qs + (j + 1) * 128],
                            qkv6[:, ks + pc * 512:ks + (pc + 1) * 512],
                            start=True, stop=True),
                            waits=sc_w if pc == 0 else [],
                            inc=(s_mm, 1) if pc == 3 else None)
                    t_sc = s_mm.count
                    p.op("vector", lambda v: v.tensor_reduce(
                        out=negm_col, in_=ps[:, 0:2048], axis=AX.X, op=OP.max,
                        negate=True), waits=[(s_mm, t_sc)])
                    p.op("vector", lambda v: v.tensor_scalar(
                        out=negm_col, in0=negm_col, scalar1=INV_SQRT_DH,
                        scalar2=None, op0=OP.mult), inc=(s_v, 1), sep=True)
                    t_nm = s_v.count
                    p.op("scalar", lambda e: e.activation(
                        out=Ebuf, in_=ps[:, 0:2048], func=AF.Exp,
                        bias=negm_col, scale=INV_SQRT_DH, accum_out=zsum_col),
                        waits=[(s_v, t_nm)], inc=(s_s, 1))
                    last_exp_s = s_s.count
                    p.op("vector", lambda v: v.reciprocal(
                        out=zrec_col, in_=zsum_col),
                        waits=[(s_s, last_exp_s)], inc=(s_v, 1))
                    t_zr = s_v.count
                    for pc in range(4):
                        p.op("tensor", lambda e, j=j, pc=pc: e.matmul(
                            bank_row(4 + pc), zrec_col,
                            Ebuf[:, pc * 512:(pc + 1) * 512],
                            start=(j == 0), stop=(j == 15)),
                            waits=[(s_v, t_zr)] if pc == 0 else [],
                            inc=(s_mm, 1) if pc == 3 else None)
                t_cs = s_mm.count
                p.op("vector", lambda v: v.tensor_copy(
                    out=slotA, in_=ps[0:1, 2048:4096]),
                    waits=[(s_mm, t_cs)], inc=(s_v, 1))
                t_csv = s_v.count
                for pc in range(4):
                    w = [(s_v, t_csv), (s_s, last_exp_s)] if pc == 0 else []
                    p.op("tensor", lambda e, pc=pc: e.matmul(
                        bank(pc), ones128, slotA[:, pc * 512:(pc + 1) * 512],
                        start=True, stop=True),
                        waits=w, inc=(s_mm, 1) if pc == 3 else None)
                t_crep = s_mm.count
                for pc in range(4):
                    p.op("vector", lambda v, pc=pc, vs=vs: v.tensor_tensor(
                        out=qkv6[:, vs + pc * 512:vs + (pc + 1) * 512],
                        in0=qkv6[:, vs + pc * 512:vs + (pc + 1) * 512],
                        in1=bank(pc), op=OP.mult),
                        waits=[(s_mm, t_crep)] if pc == 0 else [],
                        inc=(s_v, 1) if pc == 3 else None)
                t_val = s_v.count
                last_val_v = t_val
                ssem, t_s = vst.load(lambda g, head=head, vs=vs: g.dma_start(
                    out=val_view[head], in_=qkv6[:, vs:vs + 2048]),
                    waits=[(s_v, t_val)])
                t_store_hist.append((ssem, t_s))
                if debug == "values":
                    vst.load(lambda g, head=head, vs=vs: g.dma_start(
                        out=dbg[head * 128:(head + 1) * 128, :],
                        in_=qkv6[:, vs:vs + 2048]))
            prev_pair_stores = vst.totals()
        return vst.totals(), s_v.count, s_mm.count

    def phase_wo(val_done, p3_v, p3_mm):
        valh = OV_MAIN.alloc("valh", [128, 16 * 1024])
        wob = [OV_MAIN.alloc("wob0", [128, 2048]), OV_MAIN.alloc("wob1", [128, 2048])]
        xpc = [OV_MAIN.alloc("xpc0", [128, 1024]), OV_MAIN.alloc("xpc1", [128, 1024])]
        x1st = [OV_MAIN.alloc("x1st0", [128, 512]), OV_MAIN.alloc("x1st1", [128, 512])]
        sqst = OV_MAIN.alloc("sqst", [128, 512])

        vls = Stream(p, "vls", "gpsimd", 1)
        dbs = Stream(p, "dbs", "gpsimd", 2)
        wos = Stream(p, "wos", "sync", 2)
        xps = Stream(p, "xps", "gpsimd", 2)
        x1s = Stream(p, "x1s", "gpsimd", 2)

        val_view = val_scr.rearrange("(c p) t -> c p t", p=128)
        valh3 = valh.rearrange("p (c t) -> p c t", t=1024)
        wob3 = [w.rearrange("p (c f) -> p c f", f=128) for w in wob]

        stt_hist = {}
        sq_mm_hist = []
        x1_store_hist = []
        x1_read_hist = []
        t_row_v = 0
        for th in range(2):
            w = list(val_done) + [(s_v, p3_v), (s_mm, p3_mm)] if th == 0 else \
                [(s_mm, stats_done_mm[-1])]
            vsem, t_vl = vls.load(lambda g, th=th: g.dma_start(
                out=valh3, in_=val_view[:, :, th * 1024:(th + 1) * 1024]),
                waits=w)
            half_sq_mm = []
            for dm in range(16):
                gi = th * 16 + dm
                w = [(s_mm, p3_mm), (s_v, p3_v)] + list(val_done) if gi < 2 else []
                if gi >= 2:
                    w = [(s_mm, wo_mm_hist[gi - 2])]
                wsem, t_wo = wos.load(lambda e, dm=dm, par=gi % 2: e.dma_start(
                    out=wob3[par], in_=wo[dm].rearrange("(c p) f -> c p f", p=128)),
                    waits=w)
                xw = []
                if gi >= 2:
                    xw = [(s_v, xpc_free_v[gi - 2])]
                elif th == 0 and dm < 2:
                    xw = [(s_v, p3_v), (s_mm, p3_mm)] + list(val_done)
                xsem, t_xp = xps.load(lambda g, th=th, dm=dm, par=gi % 2: g.dma_start(
                    out=xpc[par], in_=xT[dm * 128:(dm + 1) * 128,
                                         th * 1024:(th + 1) * 1024]),
                    waits=xw)
                for pc in range(2):
                    gpc = gi * 2 + pc
                    bk = gpc % 2
                    mm_w = [(vsem, t_vl), (wsem, t_wo)]
                    if bk in stt_hist:
                        mm_w.append((s_v, stt_hist[bk]))
                    if gi == 0 and pc == 0:
                        mm_w.append((s_s, s_s.count))
                    for vf in range(16):
                        p.op("tensor", lambda e, vf=vf, pc=pc, bk=bk, par=gi % 2: e.matmul(
                            bank(bk), wob[par][:, vf * 128:(vf + 1) * 128],
                            valh[:, vf * 1024 + pc * 512:vf * 1024 + pc * 512 + 512],
                            start=(vf == 0), stop=(vf == 15)),
                            waits=mm_w if vf == 0 else [],
                            inc=(s_mm, 1) if vf == 15 else None)
                    t_womm = s_mm.count
                    x1_w = [(s_mm, t_womm), (xsem, t_xp)]
                    if len(sq_mm_hist) >= 2:
                        x1_w += sq_mm_hist[-2][1]
                    if len(x1_read_hist) >= 2:
                        x1_w += x1_read_hist[-2]
                    p.op("vector", lambda v, pc=pc, dm=dm, bk=bk, par=gi % 2: v.scalar_tensor_tensor(
                        out=x1st[bk], in0=bank(bk), scalar=ga_col[:, dm:dm + 1],
                        in1=xpc[par][:, pc * 512:(pc + 1) * 512],
                        op0=OP.mult, op1=OP.add),
                        waits=x1_w, inc=(s_v, 1))
                    stt_hist[bk] = s_v.count
                    if pc == 1:
                        xpc_free_v.append(s_v.count)
                    ssem, t_s = x1s.load(lambda g, th=th, dm=dm, pc=pc, bk=bk: g.dma_start(
                        out=x1_scr[dm * 128:(dm + 1) * 128,
                                   th * 1024 + pc * 512:th * 1024 + pc * 512 + 512],
                        in_=x1st[bk]),
                        waits=[(s_v, stt_hist[bk])])
                    x1_store_hist.append((ssem, t_s))
                    reads = [(ssem, t_s)]
                    if debug == "x1":
                        dsem, t_d = dbs.load(lambda g, th=th, dm=dm, pc=pc, bk=bk: g.dma_start(
                            out=dbg[dm * 128:(dm + 1) * 128,
                                    th * 1024 + pc * 512:th * 1024 + pc * 512 + 512],
                            in_=x1st[bk]), waits=[(s_v, stt_hist[bk])])
                        reads.append((dsem, t_d))
                    x1_read_hist.append(reads)
                    sq_w = [(s_v, stt_hist[bk])]
                    if half_sq_mm:
                        sq_w.append((s_mm, half_sq_mm[-1][0]))
                    p.op("scalar", lambda e, bk=bk: e.activation(
                        out=sqst, in_=x1st[bk], func=AF.Square, bias=0.0, scale=1.0),
                        waits=sq_w, inc=(s_s, 1))
                    t_sq = s_s.count
                    st_w = [(s_v, stt_hist[bk])]
                    if th == 1 and gi == 16 and pc == 0:
                        st_w.append((s_v, t_row_v))
                    p.op("tensor", lambda e, pc=pc, bk=bk, dm=dm: e.matmul(
                        bank_row(4 + pc), ones_col, x1st[bk],
                        start=(dm == 0), stop=(dm == 15)),
                        waits=st_w)
                    p.op("tensor", lambda e, pc=pc, dm=dm: e.matmul(
                        bank_row(6 + pc), ones_col, sqst,
                        start=(dm == 0), stop=(dm == 15)),
                        waits=[(s_s, t_sq)],
                        inc=(s_mm, 1))
                    t_smm = s_mm.count
                    half_sq_mm.append((t_smm, [(s_mm, t_smm)]))
                    sq_mm_hist.append((t_smm, [(s_mm, t_smm)]))
                wo_mm_hist.append(s_mm.count)
            stats_done_mm.append(s_mm.count)
            t_all = s_mm.count
            sA = slotA[:, 0:1024]
            sB = slotB[:, 0:1024]
            p.op("vector", lambda v: v.tensor_scalar(
                out=sB, in0=ps[0:1, 2048:3072], scalar1=float(1.0 / D),
                scalar2=None, op0=OP.mult), waits=[(s_mm, t_all)])
            p.op("vector", lambda v: v.tensor_scalar(
                out=ps[0:1, 3072:4096], in0=ps[0:1, 3072:4096],
                scalar1=float(1.0 / D), scalar2=None, op0=OP.mult))
            p.op("vector", lambda v: v.tensor_tensor(
                out=sA, in0=sB, in1=sB, op=OP.mult), sep=True)
            p.op("vector", lambda v: v.tensor_tensor(
                out=ps[0:1, 3072:4096], in0=ps[0:1, 3072:4096], in1=sA,
                op=OP.subtract), inc=(s_v, 1), sep=True)
            t_var = s_v.count
            p.op("scalar", lambda e: e.activation(
                out=sA, in_=ps[0:1, 3072:4096], func=AF.Sqrt,
                bias=eps11, scale=1.0), waits=[(s_v, t_var)], inc=(s_s, 1))
            t_sq2 = s_s.count
            p.op("vector", lambda v: v.reciprocal(out=sA, in_=sA),
                 waits=[(s_s, t_sq2)], inc=(s_v, 1))
            t_rows2 = s_v.count
            for which, (rowsrc, cdst) in enumerate([(sB, m2_col), (sA, r2_col)]):
                for cc in range(8):
                    k = which * 8 + cc
                    p.op("tensor", lambda e, rowsrc=rowsrc, cc=cc, k=k: e.matmul(
                        bank(2, 16)[:, k:k + 1],
                        rowsrc[:, cc * 128:(cc + 1) * 128], id11,
                        start=(k == 0), stop=(k == 15)),
                        waits=[(s_v, t_rows2)] if k == 0 else [],
                        inc=(s_mm, 1) if k == 15 else None)
            t_cmm = s_mm.count
            p.op("vector", lambda v, th=th: v.tensor_copy(
                out=m2_col[:, th * 8:th * 8 + 8], in_=bank(2, 16)[:, 0:8]),
                waits=[(s_mm, t_cmm)])
            p.op("vector", lambda v, th=th: v.tensor_copy(
                out=r2_col[:, th * 8:th * 8 + 8], in_=bank(2, 16)[:, 8:16]),
                inc=(s_v, 1))
            t_row_v = s_v.count
        return x1_store_hist, x1s.totals(), s_v.count, s_mm.count

    qsc = {}

    def alloc_quant_scratch():
        qsc["in"] = [OV_MAIN.alloc("qin0", [128, 2048]), OV_MAIN.alloc("qin1", [128, 2048])]
        qsc["tstg"] = [OV_MAIN.alloc("qtstg0", [128, 2048], BF16),
                       OV_MAIN.alloc("qtstg1", [128, 2048], BF16)]
        for si in range(2):
            st = {"prev_v": 0, "prev_s": 0, "prev_g": 0}
            for nm in ("y", "c", "d"):
                st[nm] = OV_Q.alloc(f"q{nm}{si}", [128, QW])
            for nm in ("amx", "lg", "s0", "cs"):
                st[nm] = OV_Q.alloc(f"q{nm}{si}", [128, 64])
            st["tt"] = st["d"][:, 0:64]
            st["gg"] = st["lg"]
            st["invs"] = st["cs"]
            st["scl"] = st["amx"]
            qsc[si] = st
        qsc["silu"] = qsc[1]["y"][:, 0:256]

    def q_pair(jobs):
        assert len(jobs) == 2
        for job in jobs:
            s = qsc[job["set"]]
            job["S"] = s
            job["src3"] = job["src"].rearrange("p (b k) -> p b k", k=32)
            job["velt"] = c_velt4 if job["mfrac"] == 3 else c_velt2
            job["fix"] = c_fix3 if job["mfrac"] == 3 else c_fix1
            job["mn"] = 7.5 if job["mfrac"] == 3 else 6.0
            job["vp"], job["sp"], job["gp"] = s["prev_v"], s["prev_s"], s["prev_g"]

        def opV(job, fn, extra=(), mark=None):
            w = ([(s_v, job["vp"])] if job["vp"] else []) + list(extra)
            p.op("vector", fn, waits=w, inc=(s_v, 1))
            job["vp"] = s_v.count
            if mark:
                job[mark] = s_v.count

        def opP(job, fn, extra=(), mark=None):
            w = ([(s_g, job["gp"])] if job["gp"] else []) + list(extra)
            p.op("gpsimd", fn, waits=w, inc=(s_g, 1))
            job["gp"] = s_g.count
            if mark:
                job[mark] = s_g.count

        def opS(job, fn, extra=(), mark=None):
            w = ([(s_s, job["sp"])] if job["sp"] else []) + list(extra)
            p.op("scalar", fn, waits=w, inc=(s_s, 1))
            job["sp"] = s_s.count
            if mark:
                job[mark] = s_s.count

        for job in jobs:
            extra = list(job.get("w_v", ()))
            if job["sp"]:
                extra.append((s_s, job["sp"]))
            if job["gp"]:
                extra.append((s_g, job["gp"]))
            opV(job, lambda v, s=job["S"], src3=job["src3"]: v.tensor_reduce(
                out=s["amx"], in_=src3, axis=AX.X, op=OP.max,
                apply_absolute_value=True), extra=extra)
        for job in jobs:
            opV(job, lambda v, s=job["S"]: v.tensor_scalar(
                out=s["amx"], in0=s["amx"], scalar1=float(2.0 ** -96),
                scalar2=None, op0=OP.max), mark="m_amx")
        for job in jobs:
            opS(job, lambda e, s=job["S"]: e.activation(
                out=s["lg"], in_=s["amx"], func=AF.Ln, bias=0.0, scale=1.0),
                extra=list(job.get("w_s", ())) + [(s_v, job["m_amx"])],
                mark="m_lg")
        for job in jobs:
            opV(job, lambda v, s=job["S"]: v.tensor_scalar(
                out=s["tt"], in0=s["lg"], scalar1=INV_LN2, scalar2=c_round,
                op0=OP.mult, op1=OP.add), extra=[(s_s, job["m_lg"])])
        for job in jobs:
            opV(job, lambda v, s=job["S"]: v.tensor_scalar(
                out=s["tt"], in0=s["tt"], scalar1=c_round, scalar2=None,
                op0=OP.subtract), mark="m_r")
        for job in jobs:
            opS(job, lambda e, s=job["S"]: e.activation(
                out=s["s0"], in_=s["tt"], func=AF.Exp, bias=0.0, scale=LN2C),
                extra=[(s_v, job["m_r"])], mark="m_s0")
        first = True
        for fn in (
            lambda v, s: v.tensor_scalar(out=s["cs"], in0=s["s0"],
                                         scalar1=c_velt_s, scalar2=None,
                                         op0=OP.mult),
            lambda v, s: v.tensor_tensor(out=s["tt"], in0=s["cs"], in1=s["s0"],
                                         op=OP.subtract),
            lambda v, s: v.tensor_tensor(out=s["s0"], in0=s["cs"], in1=s["tt"],
                                         op=OP.subtract),
            lambda v, s: v.tensor_tensor(out=s["gg"], in0=s["s0"], in1=s["amx"],
                                         op=OP.is_gt),
            lambda v, s: v.tensor_scalar(out=s["gg"], in0=s["gg"], scalar1=-0.5,
                                         scalar2=1.0, op0=OP.mult, op1=OP.add),
            lambda v, s: v.tensor_tensor(out=s["s0"], in0=s["s0"], in1=s["gg"],
                                         op=OP.mult),
            lambda v, s: v.tensor_scalar(out=s["tt"], in0=s["s0"], scalar1=2.0,
                                         scalar2=None, op0=OP.mult),
            lambda v, s: v.tensor_tensor(out=s["gg"], in0=s["tt"], in1=s["amx"],
                                         op=OP.is_le),
            lambda v, s: v.tensor_scalar(out=s["gg"], in0=s["gg"], scalar1=1.0,
                                         scalar2=None, op0=OP.add),
            lambda v, s: v.tensor_tensor(out=s["s0"], in0=s["s0"], in1=s["gg"],
                                         op=OP.mult),
            lambda v, s: v.reciprocal(out=s["invs"], in_=s["s0"]),
            lambda v, s: v.tensor_scalar(out=s["invs"], in0=s["invs"],
                                         scalar1=4.0, scalar2=None, op0=OP.mult),
            lambda v, s: v.tensor_scalar(out=s["scl"], in0=s["s0"], scalar1=0.25,
                                         scalar2=None, op0=OP.mult),
        ):
            for job in jobs:
                extra = [(s_s, job["m_s0"])] if first else ()
                opV(job, lambda v, fn=fn, s=job["S"]: fn(v, s), extra=extra)
            first = False
        for st_ in range(4):
            odd = (st_ % 2 == 1)
            for job in jobs:
                s = job["S"]
                bsl = slice(st_ * 16, (st_ + 1) * 16)
                job["y3"] = s["y"].rearrange("p (b k) -> p b k", k=32)
                job["src3s"] = job["src3"][:, bsl, :]
                job["qeF"] = job["src"][:, st_ * QW:(st_ + 1) * QW]
                job["invs3"] = s["invs"][:, bsl].unsqueeze(2).broadcast_to([128, 16, 32])
                job["scl3"] = s["scl"][:, bsl].unsqueeze(2).broadcast_to([128, 16, 32])
            for job in jobs:
                extra = [(s_g, job["gp"])] if job["gp"] else []
                opV(job, lambda v, y3=job["y3"], src3s=job["src3s"],
                         invs3=job["invs3"]: v.tensor_tensor(
                    out=y3, in0=src3s, in1=invs3,
                    op=OP.mult), extra=extra, mark="m1")
            for job in jobs:
                opS(job, lambda e, s=job["S"], velt=job["velt"]: e.activation(
                    out=s["c"], in_=s["y"], func=AF.Copy, bias=0.0,
                    scale=velt), extra=[(s_v, job["m1"])], mark="m2")
            for job in jobs:
                opS(job, lambda e, s=job["S"]: e.activation(
                    out=s["d"], in_=s["y"], func=AF.Square, bias=0.0, scale=1.0),
                    mark="m3")
            for job in jobs:
                opV(job, lambda x, s=job["S"], qeF=job["qeF"]: x.tensor_tensor(
                    out=qeF, in0=s["c"], in1=s["y"], op=OP.subtract),
                    extra=[(s_s, job["m2"])])
            for job in jobs:
                opV(job, lambda x, s=job["S"], qeF=job["qeF"]: x.tensor_tensor(
                    out=s["c"], in0=s["c"], in1=qeF, op=OP.subtract))
            for job in jobs:
                opV(job, lambda v, s=job["S"]: v.tensor_scalar(
                    out=s["d"], in0=s["d"], scalar1=1.0, scalar2=None,
                    op0=OP.is_ge), extra=[(s_s, job["m3"])])
            for job in jobs:
                opV(job, lambda v, s=job["S"], qeF=job["qeF"],
                         fx=job["fix"]: v.tensor_scalar(
                    out=qeF, in0=s["y"], scalar1=fx,
                    scalar2=fx, op0=OP.add, op1=OP.subtract))
            for job in jobs:
                opV(job, lambda x, s=job["S"], qeF=job["qeF"]: x.tensor_tensor(
                    out=s["y"], in0=s["c"], in1=qeF, op=OP.subtract))
            for job in jobs:
                opV(job, lambda x, s=job["S"]: x.tensor_tensor(
                    out=s["d"], in0=s["d"], in1=s["y"], op=OP.mult))
            for job in jobs:
                opV(job, lambda v, s=job["S"], qeF=job["qeF"]: v.tensor_tensor(
                    out=qeF, in0=qeF, in1=s["d"], op=OP.add))
            for job in jobs:
                opV(job, lambda v, qeF=job["qeF"], mn=float(job["mn"]):
                     v.tensor_scalar(
                    out=qeF, in0=qeF, scalar1=mn,
                    scalar2=-mn, op0=OP.min, op1=OP.max))
            for job in jobs:
                opV(job, lambda v, src3s=job["src3s"], scl3=job["scl3"]:
                     v.tensor_tensor(
                    out=src3s, in0=src3s, in1=scl3,
                    op=OP.mult))
        for job in jobs:
            s = job["S"]
            s["prev_v"], s["prev_s"], s["prev_g"] = job["vp"], job["sp"], job["gp"]
        return [job["vp"] for job in jobs]

    def emit_qtile_tr_only(srcbf, waits_t, bank_free):
        marks = []
        for b4 in range(4):
            bkk = 4 + b4
            for tt_ in range(4):
                idx = b4 * 4 + tt_
                w = []
                if tt_ == 0:
                    w = list(waits_t) if b4 == 0 else []
                    if bank_free:
                        w.append((s_v, bank_free[b4]))
                p.op("tensor", lambda e, idx=idx, bkk=bkk, tt_=tt_, srcbf=srcbf: e.matmul(
                    ps[:, bkk * 512 + tt_ * 128:bkk * 512 + (tt_ + 1) * 128],
                    srcbf[:, idx * 128:(idx + 1) * 128], ident,
                    start=(tt_ == 0), stop=(tt_ == 3), is_transpose=True),
                    waits=w, inc=(s_mm, 1) if tt_ == 3 else None)
            marks.append(s_mm.count)
        return marks

    def emit_qtile_evacs(par, tr4, dst_dma_fn, store_stream, store_waits):
        tstg = qsc["tstg"][par]
        ev = []
        for b4 in range(4):
            bkk = 4 + b4
            evw = [(s_mm, tr4[b4])] + (list(store_waits) if b4 == 0 else [])
            p.op("vector", lambda v, b4=b4, bkk=bkk, tstg=tstg: v.tensor_copy(
                out=tstg[:, b4 * 512:(b4 + 1) * 512],
                in_=ps[:, bkk * 512:bkk * 512 + 512]),
                waits=evw, inc=(s_v, 1))
            ev.append(s_v.count)
        ssem, t_s = store_stream.load(dst_dma_fn(tstg),
                                      waits=[(s_v, ev[-1])] + list(store_waits))
        return ev, (ssem, t_s)

    def phase_wquant(entry_waits_v, entry_waits_mm):
        alloc_quant_scratch()
        qls = Stream(p, "qls", "sync", 2)
        qst = Stream(p, "qst", "gpsimd", 2)

        wgT3 = wgqT_scr.rearrange("(hc q) o -> hc q o", q=128)
        wuT3 = wuqT_scr.rearrange("(hc q) o -> hc q o", q=128)

        jobs = []
        for oc in range(IFF // 128):
            jobs.append((lambda oc=oc: wg_h[oc * 128:(oc + 1) * 128, :],
                         lambda tstg, oc=oc: (lambda g: g.dma_start(
                             out=wgT3[:, :, oc * 128:(oc + 1) * 128]
                                 .transpose([1, 0, 2]),
                             in_=tstg.rearrange("q (hc j) -> q hc j", j=128)))))
        for oc in range(IFF // 128):
            jobs.append((lambda oc=oc: wu_h[oc * 128:(oc + 1) * 128, :],
                         lambda tstg, oc=oc: (lambda g: g.dma_start(
                             out=wuT3[:, :, oc * 128:(oc + 1) * 128]
                                 .transpose([1, 0, 2]),
                             in_=tstg.rearrange("q (hc j) -> q hc j", j=128)))))
        for dc in range(16):
            wdT3 = wdqT_scr[dc].rearrange("(ic q) j -> ic q j", q=128)
            for islab in range(4):
                jobs.append((lambda dc=dc, islab=islab:
                             wd_h[dc * 128:(dc + 1) * 128,
                                  islab * 2048:(islab + 1) * 2048],
                             lambda tstg, dc=dc, islab=islab, wdT3=wdT3:
                             (lambda g: g.dma_start(
                                 out=wdT3[islab * 16:(islab + 1) * 16]
                                     .transpose([1, 0, 2]),
                                 in_=tstg.rearrange("q (ic j) -> q ic j", j=128)))))

        tr_marks = {}
        st_marks = {}
        pend = None
        ev_last = None
        for tp in range(0, len(jobs), 2):
            lsems = []
            for j in (0, 1):
                t = tp + j
                if t < 2:
                    lw = [(s_v, entry_waits_v), (s_mm, entry_waits_mm)]
                else:
                    lw = [(s_mm, tr_marks[t - 2])]
                lsems.append(qls.load(lambda e, srcfn=jobs[t][0], j=j: e.dma_start(
                    out=qsc["in"][j], in_=srcfn()), waits=lw))
            pj = []
            for j in (0, 1):
                wv = [lsems[j]]
                if tp + j < 2:
                    wv.append((s_v, entry_waits_v))
                pj.append(dict(src=qsc["in"][j], set=j, mfrac=1, w_v=wv, w_s=[]))
            mfin = q_pair(pj)
            if pend is not None:
                pt, ppar, ptr4, pdst = pend
                sw = [st_marks[pt - 2]] if pt - 2 in st_marks else []
                ev_last, st_marks[pt] = emit_qtile_evacs(ppar, ptr4, pdst, qst, sw)
            for j in (0, 1):
                t = tp + j
                tw = [(s_v, mfin[j])]
                if t == 0:
                    tw.append((s_mm, entry_waits_mm))
                tr4 = emit_qtile_tr_only(qsc["in"][j], tw, ev_last)
                tr_marks[t] = tr4[-1]
                if j == 0:
                    sw = [st_marks[t - 2]] if t - 2 in st_marks else []
                    ev_last, st_marks[t] = emit_qtile_evacs(j, tr4, jobs[t][1], qst, sw)
                else:
                    pend = (t, j, tr4, jobs[t][1])
        pt, ppar, ptr4, pdst = pend
        sw = [st_marks[pt - 2]] if pt - 2 in st_marks else []
        ev_last, st_marks[pt] = emit_qtile_evacs(ppar, ptr4, pdst, qst, sw)
        return qst.totals(), s_v.count, s_mm.count

    def phase_ffn(x1_done, wq_done, p5_v, p5_mm):
        x1r = [OV_MAIN.alloc("x1r0", [128, 512]), OV_MAIN.alloc("x1r1", [128, 512])]
        outst = [OV_MAIN.alloc("outst0", [128, 512]), OV_MAIN.alloc("outst1", [128, 512])]
        xT4 = OV_MAIN.alloc("xT4", [128, 8192])
        repB1 = OV_MAIN.alloc("repB1", [128, 2048])
        repsb = OV_MAIN.alloc("repsb", [128, 2048])
        h2q = OV_MAIN.alloc("h2q", [128, 16 * 512], BF16)
        interq = OV_MAIN.alloc("interq", [128, 64 * 512], BF16)
        wst = OV_MAIN.alloc("wst", [128, 16384], BF16)

        xls = Stream(p, "xls", "gpsimd", 2)
        xle = Stream(p, "xle", "gpsimd", 2)
        wgl = Stream(p, "wgl", "sync", 4)
        wdl = Stream(p, "wdl", "sync", 2)
        ous = Stream(p, "ous", "gpsimd", 2)

        wgT3 = wgqT_scr.rearrange("(hc q) o -> hc q o", q=128)
        wuT3 = wuqT_scr.rearrange("(hc q) o -> hc q o", q=128)
        wst3 = wst.rearrange("q (pr w hc c) -> q pr w hc c", pr=2, w=2, hc=16)
        wstd3 = wst.rearrange("q (pr ic c) -> q pr ic c", pr=2, ic=64)

        ent = [(s_v, p5_v), (s_mm, p5_mm)]
        t_prev_ev = 0
        for which, seg in ((0, 3), (1, 4)):
            for cc in range(NC):
                w = list(ent) if (which == 0 and cc == 0) else []
                if which == 1 and cc == 0:
                    w = [(s_v, t_prev_ev)]
                p.op("tensor", lambda e, which=which, seg=seg, cc=cc: e.matmul(
                    ps[0:1, cc * 128:(cc + 1) * 128],
                    mod_cols[:, seg * NC + cc:seg * NC + cc + 1], ident,
                    start=(cc % 4 == 0), stop=(cc % 4 == 3)),
                    waits=w, inc=(s_mm, 1) if cc == NC - 1 else None)
            t_row = s_mm.count
            dstrow = slotA if which == 0 else slotB
            p.op("vector", lambda v, dstrow=dstrow: v.tensor_copy(
                out=dstrow, in_=ps[0:1, 0:2048]),
                waits=[(s_mm, t_row)], inc=(s_v, 1))
            t_prev_ev = s_v.count
        p.op("vector", lambda v: v.tensor_scalar(
            out=slotB, in0=slotB, scalar1=1.0, scalar2=None, op0=OP.add),
            inc=(s_v, 1), sep=True)
        t_rows = s_v.count
        evh = []
        for krep in range(8):
            which, piece = divmod(krep, 4)
            src = slotB if which == 0 else slotA
            rdst = repB1 if which == 0 else repsb
            w = [(s_v, t_rows)]
            if krep >= 2:
                w.append((s_v, evh[krep - 2]))
            p.op("tensor", lambda e, src=src, piece=piece, krep=krep: e.matmul(
                bank(krep % 2), ones128, src[:, piece * 512:(piece + 1) * 512],
                start=True, stop=True), waits=w, inc=(s_mm, 1))
            t_k = s_mm.count
            p.op("vector", lambda v, rdst=rdst, piece=piece, krep=krep: v.tensor_copy(
                out=rdst[:, piece * 512:(piece + 1) * 512], in_=bank(krep % 2)),
                waits=[(s_mm, t_k)], inc=(s_v, 1))
            evh.append(s_v.count)
        t_reps = s_v.count

        out_stores = []
        wgl_hist = []
        dbg_mark = [0]
        dbg_rows = [256]
        t_prev_group_done = 0
        for g in range(4):
            gtc = [g * 4 + tc for tc in range(4)]
            ln_ev = []
            x1l_hist = []
            for fc in range(NC):
                lw = []
                if fc >= 2:
                    lw = [(s_mm, x1l_hist[fc - 2])]
                elif g == 0:
                    lw = list(x1_done) + ent
                elif g > 0:
                    lw = [(s_mm, t_prev_group_done)]
                lsem, t_l = xls.load(lambda gg_, fc=fc, g=g: gg_.dma_start(
                    out=x1r[fc % 2], in_=x1_scr[fc * 128:(fc + 1) * 128,
                                                g * 512:(g + 1) * 512]),
                    waits=lw)
                bkk = 4 + (fc % 2)
                tw = [(lsem, t_l)]
                if len(ln_ev) >= 2:
                    tw.append((s_v, ln_ev[-2]))
                if fc < 2 and g == 0:
                    tw.append((s_s, s_s.count))
                for tc in range(4):
                    p.op("tensor", lambda e, tc=tc, bkk=bkk, fc=fc: e.matmul(
                        ps[:, bkk * 512 + tc * 128:bkk * 512 + (tc + 1) * 128],
                        x1r[fc % 2][:, tc * 128:(tc + 1) * 128], ident,
                        start=(tc == 0), stop=(tc == 3), is_transpose=True),
                        waits=tw if tc == 0 else [],
                        inc=(s_mm, 1) if tc == 3 else None)
                t_tr = s_mm.count
                x1l_hist.append(t_tr)
                for tc in range(4):
                    p.op("vector", lambda v, tc=tc, fc=fc, bkk=bkk, g=g: v.tensor_scalar(
                        out=xT4[:, tc * 2048 + fc * 128:tc * 2048 + (fc + 1) * 128],
                        in0=ps[:, bkk * 512 + tc * 128:bkk * 512 + (tc + 1) * 128],
                        scalar1=m2_col[:, g * 4 + tc:g * 4 + tc + 1],
                        scalar2=r2_col[:, g * 4 + tc:g * 4 + tc + 1],
                        op0=OP.subtract, op1=OP.mult),
                        waits=[(s_mm, t_tr)] if tc == 0 else [],
                        inc=(s_v, 1) if tc == 3 else None)
                ln_ev.append(s_v.count)
            t_u = s_v.count
            for tc in range(4):
                p.op("gpsimd", lambda gg_, tc=tc: gg_.tensor_tensor(
                    out=xT4[:, tc * 2048:(tc + 1) * 2048],
                    in0=xT4[:, tc * 2048:(tc + 1) * 2048],
                    in1=repB1, op=OP.mult),
                    waits=([(s_v, t_u), (s_v, t_reps)] if tc == 0 else []),
                    inc=(s_g, 1))
                t_m = s_g.count
                p.op("vector", lambda v, tc=tc: v.tensor_tensor(
                    out=xT4[:, tc * 2048:(tc + 1) * 2048],
                    in0=xT4[:, tc * 2048:(tc + 1) * 2048],
                    in1=repsb, op=OP.add),
                    waits=[(s_g, t_m)], inc=(s_v, 1))
            t_h2add = s_v.count
            hq_ev = 0
            h2q3 = h2q.rearrange("q (fc t) -> q fc t", t=512)
            for tcp in (0, 2):
                mfin = q_pair([dict(src=xT4[:, tc * 2048:(tc + 1) * 2048],
                                    set=tc - tcp, mfrac=3,
                                    w_v=[(s_v, t_h2add)], w_s=[])
                               for tc in (tcp, tcp + 1)])
                for tc in (tcp, tcp + 1):
                    ev_by_b4 = []
                    for b4 in range(4):
                        bkk = 4 + (b4 % 2)
                        tw2 = []
                        if b4 == 0:
                            tw2 = [(s_v, mfin[tc - tcp])]
                            if hq_ev:
                                tw2.append((s_v, hq_ev))
                            if ln_ev:
                                tw2.append((s_v, ln_ev[-1]))
                        elif b4 >= 2:
                            tw2 = [(s_v, ev_by_b4[b4 - 2])]
                        for tt_ in range(4):
                            idx = b4 * 4 + tt_
                            p.op("tensor", lambda e, idx=idx, bkk=bkk, tt_=tt_, tc=tc: e.matmul(
                                ps[:, bkk * 512 + tt_ * 128:bkk * 512 + (tt_ + 1) * 128],
                                xT4[:, tc * 2048 + idx * 128:tc * 2048 + (idx + 1) * 128],
                                ident,
                                start=(tt_ == 0), stop=(tt_ == 3), is_transpose=True),
                                waits=tw2 if tt_ == 0 else [],
                                inc=(s_mm, 1) if tt_ == 3 else None)
                        t_tr2 = s_mm.count
                        h2ev_w = [(s_mm, t_tr2)]
                        if debug == "ffn" and tc == 0 and b4 == 0 and dbg_mark[0]:
                            h2ev_w.append((p.sem("dbgst"), dbg_mark[0]))
                        p.op("vector", lambda v, b4=b4, tc=tc, bkk=bkk, h2q3=h2q3: v.tensor_copy(
                            out=h2q3[:, b4 * 4:(b4 + 1) * 4, tc * 128:(tc + 1) * 128],
                            in_=ps[:, bkk * 512:bkk * 512 + 512]
                                .rearrange("q (i c) -> q i c", c=128)),
                            waits=h2ev_w, inc=(s_v, 1))
                        ev_by_b4.append(s_v.count)
                        hq_ev = s_v.count
            t_h2q_v = s_v.count
            t_h2q_mm = s_mm.count
            if debug == "ffn":
                p.op("gpsimd", lambda gg_, g=g: gg_.dma_start(
                    out=h2q_dbg[g], in_=h2q), waits=[(s_v, t_h2q_v)],
                    inc=(p.sem("dbgst"), 16))
            dbg_mark[0] = p.sem("dbgst").count

            iq_tr_mm = 0
            iq_ev_v = 0
            ring = [x1r[0], x1r[1], outst[0], outst[1]]
            cp_marks = []
            pm_marks = []
            trtc = {tc: t_h2q_mm for tc in range(4)}
            for oq in range(4):
                for op8 in range(8):
                    opi = oq * 8 + op8
                    gop = g * 32 + opi
                    if opi < 2:
                        lw = (list(wq_done) + [(s_mm, t_h2q_mm)]) if g == 0 else \
                            [(s_mm, t_prev_group_done)]
                    else:
                        lw = [(s_mm, wgl_hist[gop - 2])]
                    par = gop % 2
                    lsg, t_lg = wgl.load(lambda e, opi=opi, par=par: e.dma_start(
                        out=wst3[:, par, 0],
                        in_=wgT3[:, :, opi * 256:(opi + 1) * 256].transpose([1, 0, 2])),
                        waits=lw)
                    lsu, t_lu = wgl.load(lambda e, opi=opi, par=par: e.dma_start(
                        out=wst3[:, par, 1],
                        in_=wuT3[:, :, opi * 256:(opi + 1) * 256].transpose([1, 0, 2])))
                    for tc in range(4):
                        nn = len(cp_marks)
                        bko = 2 * (nn % 2)
                        slot = ring[nn % 4]
                        mm_w = [(lsg, t_lg), (lsu, t_lu), (s_v, t_h2q_v)]
                        if nn >= 2:
                            mm_w.append((s_s, cp_marks[nn - 2]))
                        for kc in range(NC):
                            p.op("tensor", lambda e, kc=kc, tc=tc, par=par, bko=bko: e.matmul(
                                bank(bko, 256), h2q[:, kc * 512 + tc * 128:kc * 512 + (tc + 1) * 128],
                                wst3[:, par, 0, kc], start=(kc == 0), stop=(kc == NC - 1)),
                                waits=mm_w if kc == 0 else [])
                        for kc in range(NC):
                            p.op("tensor", lambda e, kc=kc, tc=tc, par=par, bko=bko: e.matmul(
                                bank(bko + 1, 256), h2q[:, kc * 512 + tc * 128:kc * 512 + (tc + 1) * 128],
                                wst3[:, par, 1, kc], start=(kc == 0), stop=(kc == NC - 1)),
                                waits=[], inc=(s_mm, 1) if kc == NC - 1 else None)
                        t_gu = s_mm.count
                        si_w = [(s_mm, t_gu)]
                        if nn >= 4:
                            si_w.append((s_g, pm_marks[nn - 4]))
                        elif nn >= 2 and len(out_stores) >= 2:
                            si_w += out_stores[-2:]
                        p.op("scalar", lambda e, bko=bko, slot=slot: e.activation(
                            out=slot[:, 0:256], in_=bank(bko, 256), func=AF.Silu,
                            bias=0.0, scale=1.0),
                            waits=si_w, inc=(s_s, 1))
                        p.op("scalar", lambda e, bko=bko, slot=slot: e.activation(
                            out=slot[:, 256:512], in_=bank(bko + 1, 256),
                            func=AF.Copy, bias=0.0, scale=1.0),
                            inc=(s_s, 1))
                        cp_marks.append(s_s.count)
                        p.op("gpsimd", lambda x, tc=tc, op8=op8, slot=slot: x.tensor_tensor(
                            out=xT4[:, tc * 2048 + op8 * 256:tc * 2048 + (op8 + 1) * 256],
                            in0=slot[:, 0:256], in1=slot[:, 256:512], op=OP.mult),
                            waits=[(s_s, cp_marks[-1]), (s_mm, trtc[tc])],
                            inc=(s_g, 1))
                        pm_marks.append(s_g.count)
                    wgl_hist.append(s_mm.count)
                iq3 = interq.rearrange("q (ic t) -> q ic t", t=512)
                mfin4 = [0] * 4
                for tcp in (0, 2):
                    mm_ = q_pair([dict(src=xT4[:, tc * 2048:(tc + 1) * 2048],
                                       set=tc - tcp, mfrac=3,
                                       w_v=[(s_g, pm_marks[-1])], w_s=[])
                                  for tc in (tcp, tcp + 1)])
                    mfin4[tcp], mfin4[tcp + 1] = mm_
                for tc in range(4):
                    last_v = mfin4[tc]
                    ev_by_b4 = []
                    for b4 in range(4):
                        bkk = 4 + (b4 % 2)
                        tw2 = []
                        if b4 == 0:
                            tw2 = [(s_v, last_v)]
                            if iq_ev_v:
                                tw2.append((s_v, iq_ev_v))
                        elif b4 >= 2:
                            tw2 = [(s_v, ev_by_b4[b4 - 2])]
                        for tt_ in range(4):
                            idx = b4 * 4 + tt_
                            p.op("tensor", lambda e, idx=idx, bkk=bkk, tt_=tt_, tc=tc: e.matmul(
                                ps[:, bkk * 512 + tt_ * 128:bkk * 512 + (tt_ + 1) * 128],
                                xT4[:, tc * 2048 + idx * 128:tc * 2048 + (idx + 1) * 128],
                                ident,
                                start=(tt_ == 0), stop=(tt_ == 3), is_transpose=True),
                                waits=tw2 if tt_ == 0 else [],
                                inc=(s_mm, 1) if tt_ == 3 else None)
                        t_tr2 = s_mm.count
                        iqev_w = [(s_mm, t_tr2)]
                        if debug == "ffn" and oq == 0 and tc == 0 and b4 == 0 and dbg_mark[0]:
                            iqev_w.append((p.sem("dbgst"), dbg_mark[0]))
                        p.op("vector", lambda v, b4=b4, tc=tc, oq=oq, bkk=bkk, iq3=iq3: v.tensor_copy(
                            out=iq3[:, oq * 16 + b4 * 4:oq * 16 + (b4 + 1) * 4,
                                    tc * 128:(tc + 1) * 128],
                            in_=ps[:, bkk * 512:bkk * 512 + 512]
                                .rearrange("q (i c) -> q i c", c=128)),
                            waits=iqev_w, inc=(s_v, 1))
                        ev_by_b4.append(s_v.count)
                        iq_ev_v = s_v.count
                    iq_tr_mm = s_mm.count
                    trtc[tc] = s_mm.count
            t_stage_g = s_g.count
            t_iq_v = s_v.count
            t_iq_mm = s_mm.count
            if debug == "ffn":
                p.op("gpsimd", lambda gg_, g=g: gg_.dma_start(
                    out=iq_dbg[g], in_=interq), waits=[(s_v, t_iq_v)],
                    inc=(p.sem("dbgst"), 16))
            dbg_mark[0] = p.sem("dbgst").count

            dn_stt = {}
            for dm in range(16):
                gdm = g * 16 + dm
                par = gdm % 2
                if dm < 2:
                    lw = (list(wq_done) if g == 0 else []) + [(s_mm, t_iq_mm)]
                else:
                    lw = [(s_mm, dn_mm_hist[gdm - 2])]
                wdT3 = wdqT_scr[dm].rearrange("(ic q) j -> ic q j", q=128)
                lsd, t_ld = wdl.load(lambda e, dm=dm, par=par, wdT3=wdT3: e.dma_start(
                    out=wstd3[:, par], in_=wdT3.transpose([1, 0, 2])), waits=lw)
                xw = [(s_v, dn_stt[par])] if par in dn_stt else []
                if len(out_stores) >= 2:
                    xw.append(out_stores[-2])
                if dm < 2:
                    xw.append((s_g, t_stage_g))
                if gdm < 2 and g == 0:
                    xw = list(x1_done) + [(s_g, t_stage_g)]
                lsx, t_lx = xle.load(lambda gg_, dm=dm, g=g, par=par: gg_.dma_start(
                    out=x1r[par] if False else outst[par],
                    in_=x1_scr[dm * 128:(dm + 1) * 128, g * 512:(g + 1) * 512]),
                    waits=xw)
                bkd = par
                mm_w = [(lsd, t_ld), (s_v, t_iq_v)]
                if bkd in dn_stt:
                    mm_w.append((s_v, dn_stt[bkd]))
                for ic in range(64):
                    p.op("tensor", lambda e, ic=ic, par=par, bkd=bkd: e.matmul(
                        bank(bkd), wstd3[:, par, ic],
                        interq[:, ic * 512:(ic + 1) * 512],
                        start=(ic == 0), stop=(ic == 63)),
                        waits=mm_w if ic == 0 else [],
                        inc=(s_mm, 1) if ic == 63 else None)
                t_dn = s_mm.count
                dn_mm_hist.append(t_dn)
                st_w = [(s_mm, t_dn), (lsx, t_lx)]
                if len(out_stores) >= 2:
                    st_w.append(out_stores[-2])
                p.op("vector", lambda v, dm=dm, par=par, bkd=bkd: v.scalar_tensor_tensor(
                    out=outst[par], in0=bank(bkd), scalar=gb_col[:, dm:dm + 1],
                    in1=outst[par], op0=OP.mult, op1=OP.add),
                    waits=st_w, inc=(s_v, 1))
                dn_stt[par] = s_v.count
                osem, t_os = ous.load(lambda gg_, dm=dm, g=g, par=par: gg_.dma_start(
                    out=outT[dm * 128:(dm + 1) * 128, g * 512:(g + 1) * 512],
                    in_=outst[par]), waits=[(s_v, dn_stt[par])])
                out_stores.append((osem, t_os))
            t_prev_group_done = s_mm.count
        return ous.totals()

    seg_slab_mm = []
    pair_proj_mm = []
    wo_mm_hist = []
    xpc_free_v = []
    stats_done_mm = []
    dn_mm_hist = []
    init_consts()
    mod_v, mod_mm, slab_done = phase_mod()
    t_h_done = []
    val_done = []
    if phases >= 2:
        OV_MAIN.reset()
        t_h_done = phase_ln1_h(mod_v, mod_mm, slab_done)
    x1_done = []
    if phases >= 3:
        OV_MAIN.reset()
        val_done, p3_v, p3_mm = phase_attn(t_h_done, p.sem("s_v").count,
                                           p.sem("s_mm").count)
    if phases >= 4:
        OV_MAIN.reset()
        _, x1_done, p4_v, p4_mm = phase_wo(val_done, p3_v, p3_mm)
    out_done = []
    if phases >= 5:
        wq_done, wq_v, wq_mm = phase_wquant(p4_v, p4_mm)
        OV_MAIN.reset()
        out_done = phase_ffn(x1_done, wq_done, wq_v, wq_mm)

    if debug == "mod":
        p.op("gpsimd", lambda g: g.dma_start(out=dbg[0:128, 0:6 * NC], in_=mod_cols),
             waits=[(s_v, s_v.count)], inc=(s_st, 16))
        p.op("gpsimd", lambda g: g.dma_start(out=dbg[0:128, 96:96 + NC], in_=sc_col),
             inc=(s_st, 16))
        p.op("gpsimd", lambda g: g.dma_start(out=dbg[200:201, 0:2048], in_=prow[:, 0:2048]),
             inc=(s_st, 16))

    if out_done:
        p.op("tensor", lambda e: e.nop(), waits=list(out_done))

    if phases < 5:
        for kc in range(NC):
            p.op("sync", lambda e, kc=kc: e.dma_start(
                out=outT[kc * 128:(kc + 1) * 128, 0:96], in_=mod_cols),
                waits=[(s_v, s_v.count)], inc=(s_out, 16))
        p.op("tensor", lambda e: e.nop(), waits=[(s_out, s_out.count)])
    if t_h_done:
        p.op("tensor", lambda e: e.nop(), waits=list(t_h_done))
    if val_done:
        p.op("tensor", lambda e: e.nop(), waits=list(val_done))
    if x1_done:
        p.op("tensor", lambda e: e.nop(), waits=list(x1_done))

    emit_program(nc, p)
    return nc


def _prep_inputs(L0_attn_x, L0_attn_w_qkv, L0_attn_w_o, L0_ffn_w_gate,
                 L0_ffn_w_up, L0_ffn_w_down, L0_c, L0_wmod):
    x = np.asarray(L0_attn_x, np.float32)
    wqkv_b = np.ascontiguousarray(
        np.asarray(L0_attn_w_qkv, np.float32).reshape(D, 48, 128).transpose(1, 0, 2))
    wo_b = np.ascontiguousarray(
        np.asarray(L0_attn_w_o, np.float32).reshape(D, 16, 128).transpose(1, 0, 2))
    wmod6_ = np.ascontiguousarray(
        np.asarray(L0_wmod, np.float32).reshape(D, 6, D).transpose(1, 0, 2))
    wg = np.ascontiguousarray(np.asarray(L0_ffn_w_gate, np.float32))
    wu = np.ascontiguousarray(np.asarray(L0_ffn_w_up, np.float32))
    wd = np.ascontiguousarray(np.asarray(L0_ffn_w_down, np.float32))
    c = np.asarray(L0_c, np.float32)
    maps = []
    for b in range(8):
        maps.append({
            "xT": np.ascontiguousarray(x[b].T),
            "wqkv": wqkv_b, "wo": wo_b, "wg": wg, "wu": wu, "wd": wd,
            "ccol": np.ascontiguousarray(c[b].reshape(NC, 128).T),
            "wmod6": wmod6_,
        })
    return maps


def kernel(**inputs):
    maps = _prep_inputs(**inputs)
    nc = build_nc(debug=DEBUG, phases=PHASES)
    res = run_bass_kernel_spmd(nc, maps, core_ids=list(range(8)))
    out = np.stack([np.ascontiguousarray(r["outT"].T) for r in res.results])
    return out


# version 09
# speedup vs baseline: 1.0528x; 1.0528x over previous
import contextlib
import numpy as np
import ml_dtypes
import concourse.bass as bass
import concourse.mybir as mybir
from concourse.bass_utils import run_bass_kernel_spmd

F32 = mybir.dt.float32
BF16 = mybir.dt.bfloat16
U8 = mybir.dt.uint8
OP = mybir.AluOpType
AF = mybir.ActivationFunctionType
AX = mybir.AxisListType

D = 2048
SEQ = 2048
NH = 16
IFF = 8192
NC = D // 128
INV_SQRT_DH = float(1.0 / np.sqrt(np.float64(128.0)))
LN2C = float(np.log(2.0))
INV_LN2 = float(1.0 / np.log(2.0))

PHASES = 5
DEBUG = None
QW = 512


class Sem:
    __slots__ = ("name", "count", "handle")

    def __init__(self, name):
        self.name, self.count, self.handle = name, 0, None

    def inc(self, n=1):
        self.count += n
        return self.count


class Prog:
    ENGINES = ("sync", "tensor", "vector", "scalar", "gpsimd")

    def __init__(self):
        self.ops = []
        self.sems = {}

    def sem(self, name):
        if name not in self.sems:
            self.sems[name] = Sem(name)
        return self.sems[name]

    def op(self, engine, fn, waits=(), inc=None, sep=False):
        if inc is not None:
            inc[0].inc(inc[1])
        self.ops.append((engine, tuple(waits), inc, fn, sep))


class Stream:

    def __init__(self, prog, name, engine, nbuf):
        self.prog, self.engine, self.nbuf = prog, engine, nbuf
        self.sems = [prog.sem(f"{name}_{i}") for i in range(nbuf)]
        self.idx = 0

    def load(self, fn, waits=()):
        sem = self.sems[self.idx % self.nbuf]
        self.idx += 1
        self.prog.op(self.engine, fn, waits=waits, inc=(sem, 16))
        return sem, sem.count

    def totals(self):
        return [(sem, sem.count) for sem in self.sems]


def emit_program(nc, prog):
    with contextlib.ExitStack() as stack:
        for s in prog.sems.values():
            s.handle = stack.enter_context(nc.semaphore(s.name))
        block = stack.enter_context(nc.Block())
        per_engine = {e: [] for e in Prog.ENGINES}
        for rec in prog.ops:
            per_engine[rec[0]].append(rec)

        def runner(engname):
            def run(eng):
                marks = {}
                for _, waits, inc, fn, sep in per_engine[engname]:
                    for sem, val in waits:
                        if val > 0 and marks.get(sem.name, -1) < val:
                            eng.wait_ge(sem.handle, val)
                            marks[sem.name] = val
                    if sep:
                        eng.drain()
                    ins = fn(eng)
                    if inc is not None:
                        sem, amount = inc
                        ins.then_inc(sem.handle, amount)
            return run

        for e in Prog.ENGINES:
            getattr(block, e)(runner(e))


class Overlay:
    def __init__(self, nc, base_bytes, size_bytes, tag):
        self.nc, self.base, self.size, self.tag = nc, base_bytes, size_bytes, tag
        self.cur = base_bytes
        self.n = 0

    def reset(self):
        self.cur = self.base

    def alloc(self, name, shape, dt=F32):
        esz = {F32: 4, BF16: 2, U8: 1}[dt]
        nbytes = int(np.prod(shape[1:])) * esz
        off = (self.cur + 31) & ~31
        self.cur = off + nbytes
        assert self.cur <= self.base + self.size, \
            f"overlay {self.tag} overflow at {name}: {self.cur - self.base} > {self.size}"
        self.n += 1
        return self.nc.alloc_sbuf_tensor_at(f"{self.tag}{self.n}_{name}", shape, dt,
                                            offset=off).ap()


def build_nc(debug=None, phases=PHASES):
    nc = bass.Bass("TRN2", target_bir_lowering=False, debug=False)
    p = Prog()

    xT = nc.dram_tensor("xT", [D, SEQ], F32, kind="ExternalInput").ap()
    wqkv = nc.dram_tensor("wqkv", [48, D, 128], F32, kind="ExternalInput").ap()
    wo = nc.dram_tensor("wo", [16, D, 128], F32, kind="ExternalInput").ap()
    wg_h = nc.dram_tensor("wg", [IFF, D], F32, kind="ExternalInput").ap()
    wu_h = nc.dram_tensor("wu", [IFF, D], F32, kind="ExternalInput").ap()
    wd_h = nc.dram_tensor("wd", [D, IFF], F32, kind="ExternalInput").ap()
    ccol_h = nc.dram_tensor("ccol", [128, NC], F32, kind="ExternalInput").ap()
    wmod6 = nc.dram_tensor("wmod6", [6, D, D], F32, kind="ExternalInput").ap()

    outT = nc.dram_tensor("outT", [D, SEQ], F32, kind="ExternalOutput").ap()
    dbg = nc.dram_tensor("dbg", [D, SEQ], F32, kind="ExternalOutput").ap() if debug else None

    h_scr = nc.dram_tensor("h_scr", [D, SEQ], F32).ap()
    val_scr = nc.dram_tensor("val_scr", [D, SEQ], F32).ap()
    x1_scr = nc.dram_tensor("x1_scr", [D, SEQ], F32).ap()
    wgqT_scr = nc.dram_tensor("wgqT_scr", [D, IFF], BF16).ap()
    h2q_dbg = nc.dram_tensor("h2q_dbg", [4, 128, 16 * 512], BF16).ap()
    iq_dbg = nc.dram_tensor("iq_dbg", [4, 128, 64 * 512], BF16).ap()
    wuqT_scr = nc.dram_tensor("wuqT_scr", [D, IFF], BF16).ap()
    wdqT_scr = nc.dram_tensor("wdqT_scr", [16, IFF, 128], BF16).ap()

    sb = lambda name, shape, dt=F32: nc.alloc_sbuf_tensor(name, shape, dt).ap()
    ones_col = sb("ones_col", [128, 1])
    flagc = sb("flagc", [128, 1])
    c_round = sb("c_round", [128, 1])
    c_velt_s = sb("c_velt_s", [128, 1])
    c_fix3 = sb("c_fix3", [128, 1])
    c_velt4 = sb("c_velt4", [128, 1])
    c_fix1 = sb("c_fix1", [128, 1])
    c_velt2 = sb("c_velt2", [128, 1])
    ident = sb("ident", [128, 128])
    identb = sb("identb", [128, 128], BF16)
    sc_col = sb("sc_col", [128, NC])
    mod_cols = sb("mod_cols", [128, 6 * NC])
    a1_col = sb("a1_col", [128, NC])
    b1_col = sb("b1_col", [128, NC])
    ga_col = sb("ga_col", [128, NC])
    gb_col = sb("gb_col", [128, NC])
    m2_col = sb("m2_col", [128, NC])
    r2_col = sb("r2_col", [128, NC])
    gcol_scr = sb("gcol_scr", [128, NC])
    negm_col = sb("negm_col", [128, 1])
    zrec_col = sb("zrec_col", [128, 1])
    zsum_col = sb("zsum_col", [128, 1])
    prow = sb("prow", [1, 4480])
    slotA = prow[:, 0:2048]
    slotB = prow[:, 2048:4096]
    ones128 = prow[:, 4096:4224]
    id11 = prow[:, 4224:4225]
    eps11 = prow[:, 4225:4226]

    TOP = 229344
    OV_MAIN = Overlay(nc, TOP - 168 * 1024, 168 * 1024, "M")
    OV_Q = Overlay(nc, TOP - 182 * 1024, 14 * 1024, "Q")

    ps = nc.alloc_psum_tensor("ps", [128, 4096], F32).ap()

    def bank(i, cols=512, off=0):
        return ps[:, i * 512 + off:i * 512 + off + cols]

    def bank_row(i, cols=512):
        return ps[0:1, i * 512:i * 512 + cols]

    s_wld = p.sem("s_wld")
    s_xld = p.sem("s_xld")
    s_st = p.sem("s_st")
    s_sld = p.sem("s_sld")
    s_mm = p.sem("s_mm")
    s_v = p.sem("s_v")
    s_s = p.sem("s_s")
    s_g = p.sem("s_g")
    s_out = p.sem("s_out")

    def init_consts():
        p.op("vector", lambda v: v.memset(ones_col, 1.0))
        p.op("vector", lambda v: v.memset(prow, 0.0))
        p.op("vector", lambda v: v.memset(ones128, 1.0), sep=True)
        p.op("vector", lambda v: v.memset(id11, 1.0))
        p.op("vector", lambda v: v.memset(eps11, 1e-5))
        p.op("vector", lambda v: v.tensor_scalar(
            out=flagc, in0=ones_col, scalar1=float(2.0 ** -30), scalar2=None,
            op0=OP.add))
        p.op("vector", lambda v: v.drain())
        p.op("vector", lambda v: v.tensor_tensor(
            out=flagc, in0=flagc, in1=ones_col, op=OP.is_equal))
        p.op("vector", lambda v: v.drain())

        def mkconst(dst, c32, c64):
            p.op("vector", lambda v, dst=dst: v.tensor_scalar(
                out=dst, in0=flagc, scalar1=-1.0, scalar2=1.0,
                op0=OP.mult, op1=OP.add))
            p.op("vector", lambda v, dst=dst, c64=c64: v.tensor_scalar(
                out=dst, in0=dst, scalar1=float(c64), scalar2=None, op0=OP.mult),
                sep=True)
            p.op("vector", lambda v, dst=dst, c32=c32: v.scalar_tensor_tensor(
                out=dst, in0=flagc, scalar=float(c32), in1=dst,
                op0=OP.mult, op1=OP.add), sep=True)

        mkconst(c_round, 1.5 * 2.0 ** 23, 1.5 * 2.0 ** 52)
        mkconst(c_velt_s, 2.0 ** 22 + 1, 2.0 ** 51 + 1)
        mkconst(c_fix3, 1.5 * 2.0 ** 20, 1.5 * 2.0 ** 49)
        mkconst(c_velt4, 2.0 ** 20 + 1, 2.0 ** 49 + 1)
        mkconst(c_fix1, 1.5 * 2.0 ** 22, 1.5 * 2.0 ** 51)
        mkconst(c_velt2, 2.0 ** 22 + 1, 2.0 ** 51 + 1)
        p.op("vector", lambda v: v.drain(), inc=(s_v, 1))
        p.op("gpsimd", lambda g: g.memset(ident, 0.0))
        p.op("gpsimd", lambda g: g.affine_select(
            out=ident, in_=ident, compare_op=OP.not_equal, fill=1.0, base=0,
            pattern=[[-1, 128]], channel_multiplier=1), sep=True)
        p.op("gpsimd", lambda g: g.memset(identb, 0.0))
        p.op("gpsimd", lambda g: g.affine_select(
            out=identb, in_=identb, compare_op=OP.not_equal, fill=1.0, base=0,
            pattern=[[-1, 128]], channel_multiplier=1), inc=(s_g, 1), sep=True)

    def phase_mod():
        slabs = [OV_MAIN.alloc("slab0", [128, 2048]),
                 OV_MAIN.alloc("slab1", [128, 2048])]
        p.op("sync", lambda e: e.dma_start(out=sc_col, in_=ccol_h), inc=(s_sld, 16))
        t_cld = s_sld.count
        p.op("scalar", lambda e: e.activation(
            out=sc_col, in_=sc_col, func=AF.Silu, bias=0.0, scale=1.0),
            waits=[(s_sld, t_cld), (s_v, 1)], inc=(s_s, 1))
        t_silu = s_s.count

        t_evac_v = 0
        t_tr_mm = 0
        slab_stream = Stream(p, "sw", "sync", 2)
        for seg in range(6):
            for kc in range(NC):
                i = seg * NC + kc
                waits = [(s_mm, seg_slab_mm[i - 2])] if i >= 2 else []
                lsem, t_ld = slab_stream.load(
                    lambda e, seg=seg, kc=kc, i=i: e.dma_start(
                        out=slabs[i % 2], in_=wmod6[seg, kc * 128:(kc + 1) * 128, :]),
                    waits=waits)
                mm_w = [(lsem, t_ld), (s_s, t_silu)]
                if kc == 0 and seg > 0:
                    mm_w.append((s_v, t_evac_v))
                for piece in range(4):
                    p.op("tensor", lambda e, kc=kc, piece=piece, i=i: e.matmul(
                        bank_row(piece), sc_col[:, kc:kc + 1],
                        slabs[i % 2][:, piece * 512:(piece + 1) * 512],
                        start=(kc == 0), stop=(kc == NC - 1)),
                        waits=mm_w if piece == 0 else [],
                        inc=(s_mm, 1) if piece == 3 else None)
                seg_slab_mm.append(s_mm.count)
            t_seg_mm = s_mm.count
            p.op("vector", lambda v: v.tensor_copy(out=slotA, in_=ps[0:1, 0:2048]),
                 waits=[(s_mm, t_seg_mm), (s_mm, t_tr_mm)], inc=(s_v, 1))
            t_evac_v = s_v.count
            for j in range(16):
                jj = seg * 16 + j
                p.op("tensor", lambda e, j=j, jj=jj: e.matmul(
                    bank(4, 96)[:, jj:jj + 1], slotA[:, j * 128:(j + 1) * 128],
                    id11, start=(jj == 0), stop=(jj == 95)),
                    waits=[(s_v, t_evac_v), (s_g, 1)] if j == 0 else [],
                    inc=(s_mm, 1) if j == 15 else None)
            t_tr_mm = s_mm.count

        p.op("vector", lambda v: v.tensor_copy(out=mod_cols, in_=bank(4, 96)),
             waits=[(s_mm, t_tr_mm)], inc=(s_v, 1))
        t_cols_v = s_v.count

        p.op("vector", lambda v: v.tensor_scalar(
            out=a1_col, in0=mod_cols[:, NC:2 * NC], scalar1=1.0, scalar2=None,
            op0=OP.add), waits=[(s_v, t_cols_v)])
        p.op("vector", lambda v: v.tensor_scalar(
            out=b1_col, in0=mod_cols[:, 4 * NC:5 * NC], scalar1=1.0, scalar2=None,
            op0=OP.add))

        def gtransform(dst, src):
            p.op("vector", lambda v, dst=dst, src=src: v.tensor_scalar(
                out=dst, in0=src, scalar1=0.0, scalar2=2.0,
                op0=OP.is_ge, op1=OP.mult))
            p.op("vector", lambda v, dst=dst: v.tensor_scalar(
                out=dst, in0=dst, scalar1=-1.0, scalar2=None, op0=OP.add), sep=True)
            p.op("vector", lambda v, dst=dst, src=src: v.tensor_tensor(
                out=gcol_scr, in0=dst, in1=src, op=OP.mult), sep=True)
            p.op("vector", lambda v: v.tensor_scalar(
                out=gcol_scr, in0=gcol_scr, scalar1=0.25, scalar2=None, op0=OP.max),
                sep=True)
            p.op("vector", lambda v, dst=dst: v.tensor_tensor(
                out=dst, in0=dst, in1=gcol_scr, op=OP.mult), sep=True)

        gtransform(ga_col, mod_cols[:, 2 * NC:3 * NC])
        gtransform(gb_col, mod_cols[:, 5 * NC:6 * NC])
        p.op("vector", lambda v: v.drain(), inc=(s_v, 1))
        return s_v.count, t_tr_mm, slab_stream.totals()

    def phase_ln1_h(mod_v, mod_mm, slab_done):
        xb = [OV_MAIN.alloc("xb0", [128, 2048]), OV_MAIN.alloc("xb1", [128, 2048])]
        sqb = OV_MAIN.alloc("sqb", [128, 2048])
        hb = [OV_MAIN.alloc("hb0", [128, 2048]), OV_MAIN.alloc("hb1", [128, 2048])]
        repa = OV_MAIN.alloc("repa", [128, 2048])
        repb = OV_MAIN.alloc("repb", [128, 2048])

        chunk_mm = []
        xs = Stream(p, "xs", "gpsimd", 2)
        for kc in range(NC):
            waits = [(s_mm, chunk_mm[kc - 2])] if kc >= 2 else \
                (slab_done + [(s_mm, mod_mm)])
            lsem, t_ld = xs.load(lambda g, kc=kc: g.dma_start(
                out=xb[kc % 2], in_=xT[kc * 128:(kc + 1) * 128, :]),
                waits=waits)
            sq_w = [(lsem, t_ld)]
            if kc >= 1:
                sq_w.append((s_mm, chunk_mm[kc - 1]))
            p.op("scalar", lambda e, kc=kc: e.activation(
                out=sqb, in_=xb[kc % 2], func=AF.Square, bias=0.0, scale=1.0),
                waits=sq_w, inc=(s_s, 1))
            t_sq = s_s.count
            mm_w = [(lsem, t_ld), (s_v, 1)]
            if kc == 0:
                mm_w += [(s_mm, mod_mm), (s_v, mod_v)]
            for piece in range(4):
                p.op("tensor", lambda e, kc=kc, piece=piece: e.matmul(
                    bank_row(piece), ones_col,
                    xb[kc % 2][:, piece * 512:(piece + 1) * 512],
                    start=(kc == 0), stop=(kc == NC - 1)),
                    waits=mm_w if piece == 0 else [])
            for piece in range(4):
                p.op("tensor", lambda e, kc=kc, piece=piece: e.matmul(
                    bank_row(4 + piece), ones_col,
                    sqb[:, piece * 512:(piece + 1) * 512],
                    start=(kc == 0), stop=(kc == NC - 1)),
                    waits=[(s_s, t_sq)] if piece == 0 else [],
                    inc=(s_mm, 1) if piece == 3 else None)
            chunk_mm.append(s_mm.count)
        t_stats = s_mm.count

        p.op("vector", lambda v: v.tensor_scalar(
            out=slotB, in0=ps[0:1, 0:2048], scalar1=float(1.0 / D),
            scalar2=None, op0=OP.mult), waits=[(s_mm, t_stats)])
        p.op("vector", lambda v: v.tensor_scalar(
            out=ps[0:1, 2048:4096], in0=ps[0:1, 2048:4096], scalar1=float(1.0 / D),
            scalar2=None, op0=OP.mult))
        p.op("vector", lambda v: v.tensor_tensor(
            out=slotA, in0=slotB, in1=slotB, op=OP.mult), sep=True)
        p.op("vector", lambda v: v.tensor_tensor(
            out=ps[0:1, 2048:4096], in0=ps[0:1, 2048:4096], in1=slotA,
            op=OP.subtract), inc=(s_v, 1), sep=True)
        t_var = s_v.count
        p.op("scalar", lambda e: e.activation(
            out=slotA, in_=ps[0:1, 2048:4096], func=AF.Sqrt,
            bias=eps11, scale=1.0),
            waits=[(s_v, t_var)], inc=(s_s, 1))
        t_rstd = s_s.count
        p.op("vector", lambda v: v.reciprocal(out=slotA, in_=slotA),
             waits=[(s_s, t_rstd)])
        p.op("vector", lambda v: v.scalar_tensor_tensor(
            out=slotB, in0=slotB, scalar=-1.0, in1=slotA,
            op0=OP.mult, op1=OP.mult), inc=(s_v, 1), sep=True)
        t_rows = s_v.count

        evac_hist = []
        for k in range(8):
            which, piece = divmod(k, 4)
            src = slotA if which == 0 else slotB
            w = [(s_v, t_rows)]
            if k >= 2:
                w.append((s_v, evac_hist[k - 2]))
            p.op("tensor", lambda e, src=src, piece=piece, k=k: e.matmul(
                bank(k % 2), ones128, src[:, piece * 512:(piece + 1) * 512],
                start=True, stop=True),
                waits=w, inc=(s_mm, 1))
            t_k = s_mm.count
            rdst = repa if which == 0 else repb
            p.op("vector", lambda v, rdst=rdst, piece=piece, k=k: v.tensor_copy(
                out=rdst[:, piece * 512:(piece + 1) * 512], in_=bank(k % 2)),
                waits=[(s_mm, t_k)], inc=(s_v, 1))
            evac_hist.append(s_v.count)
        t_reps = s_v.count

        t1h, t2h, sth = [], [], []
        hst = Stream(p, "hst", "gpsimd", 2)
        for kc in range(NC):
            ld_w = [(s_v, t1h[kc - 2])] if kc >= 2 else [(s_mm, t_stats)]
            lsem, t_ld = xs.load(lambda g, kc=kc: g.dma_start(
                out=xb[kc % 2], in_=xT[kc * 128:(kc + 1) * 128, :]),
                waits=ld_w)
            t1_w = [(lsem, t_ld), (s_v, t_reps)]
            if kc >= 1:
                t1_w.append((s_g, t2h[kc - 1]))
            p.op("vector", lambda v, kc=kc: v.tensor_tensor(
                out=sqb, in0=xb[kc % 2], in1=repa, op=OP.mult),
                waits=t1_w, inc=(s_v, 1))
            t1h.append(s_v.count)
            t2_w = [(s_v, t1h[-1])]
            if kc >= 2:
                t2_w.append(sth[kc - 2])
            p.op("gpsimd", lambda g, kc=kc: g.tensor_tensor(
                out=hb[kc % 2], in0=sqb, in1=repb, op=OP.add),
                waits=t2_w, inc=(s_g, 1))
            t2h.append(s_g.count)
            p.op("vector", lambda v, kc=kc: v.tensor_scalar(
                out=hb[kc % 2], in0=hb[kc % 2],
                scalar1=a1_col[:, kc:kc + 1], scalar2=mod_cols[:, kc:kc + 1],
                op0=OP.mult, op1=OP.add),
                waits=[(s_g, t2h[-1])], inc=(s_v, 1))
            t_h = s_v.count
            ssem, t_s = hst.load(lambda g, kc=kc: g.dma_start(
                out=h_scr[kc * 128:(kc + 1) * 128, :], in_=hb[kc % 2]),
                waits=[(s_v, t_h)])
            if debug == "h":
                ssem, t_s = hst.load(lambda g, kc=kc: g.dma_start(
                    out=dbg[kc * 128:(kc + 1) * 128, :], in_=hb[kc % 2]))
            sth.append((ssem, t_s))
        return hst.totals()

    def phase_attn(h_done, p2_v, p2_mm):
        hq8 = [OV_MAIN.alloc("hq0", [128, 4096]), OV_MAIN.alloc("hq1", [128, 4096])]
        qkv6 = OV_MAIN.alloc("qkv6", [128, 6 * 2048])
        wqp = OV_MAIN.alloc("wqp", [128, 6 * 2048])
        Ebuf = OV_MAIN.alloc("Ebuf", [128, 2048])

        h8 = Stream(p, "h8", "gpsimd", 2)
        wqs = Stream(p, "wqs", "sync", 6)
        vst = Stream(p, "vst", "gpsimd", 2)

        h_view = h_scr.rearrange("(c p) t -> c p t", p=128)
        val_view = val_scr.rearrange("(c p) t -> c p t", p=128)

        proj_done_hist = []
        wq_done = {}
        evac_hist = {}
        last_exp_s = 0
        last_val_v = 0
        t_store_hist = []
        prev_pair_stores = []

        for hp in range(8):
            blocks = [2 * hp, 16 + 2 * hp, 32 + 2 * hp,
                      2 * hp + 1, 16 + 2 * hp + 1, 32 + 2 * hp + 1]
            for slot, blk in enumerate(blocks):
                w = []
                if hp == 0:
                    w = list(h_done)
                    w.append((s_v, p2_v))
                else:
                    w = [(s_mm, pair_proj_mm[hp - 1])]
                wqp3 = wqp.rearrange("p (s k f) -> p s k f", s=6, k=NC)
                wq_done[slot] = wqs.load(
                    lambda e, slot=slot, blk=blk, wqp3=wqp3: e.dma_start(
                        out=wqp3[:, slot], in_=wqkv[blk].rearrange("(k p) f -> k p f", p=128)),
                    waits=w if slot < 2 else [])
            for e8 in range(8):
                ge = hp * 8 + e8
                w = []
                if ge >= 2:
                    w = [(s_mm, proj_done_hist[ge - 2])]
                elif hp == 0:
                    w = list(h_done) + [(s_v, p2_v), (s_mm, p2_mm)]
                hsem, t_h = h8.load(
                    lambda g, e8=e8, par=ge % 2: g.dma_start(
                        out=hq8[par].rearrange("p (k t) -> p k t", t=256),
                        in_=h_view[:, :, e8 * 256:(e8 + 1) * 256]),
                    waits=w)
                for slot in range(6):
                    bk = slot % 2
                    mm_w = [(hsem, t_h), wq_done[slot]]
                    if bk in evac_hist:
                        mm_w.append((s_v, evac_hist[bk]))
                    if ge == 0 and slot == 0:
                        mm_w.append((s_mm, p2_mm))
                    for kc in range(NC):
                        p.op("tensor", lambda e, slot=slot, kc=kc, bk=bk, par=ge % 2: e.matmul(
                            bank(bk, 256), wqp[:, slot * 2048 + kc * 128:slot * 2048 + (kc + 1) * 128],
                            hq8[par][:, kc * 256:(kc + 1) * 256],
                            start=(kc == 0), stop=(kc == NC - 1)),
                            waits=mm_w if kc == 0 else [],
                            inc=(s_mm, 1) if kc == NC - 1 else None)
                    t_g = s_mm.count
                    ev_w = [(s_mm, t_g)]
                    if e8 == 0 and slot == 0 and prev_pair_stores:
                        ev_w += prev_pair_stores
                    p.op("vector", lambda v, slot=slot, e8=e8, bk=bk: v.tensor_copy(
                        out=qkv6[:, slot * 2048 + e8 * 256:slot * 2048 + (e8 + 1) * 256],
                        in_=bank(bk, 256)),
                        waits=ev_w, inc=(s_v, 1))
                    evac_hist[bk] = s_v.count
                proj_done_hist.append(s_mm.count)
            pair_proj_mm.append(s_mm.count)
            t_proj_v = s_v.count

            for hh in range(2):
                qs = (3 * hh) * 2048
                ks = (3 * hh + 1) * 2048
                vs = (3 * hh + 2) * 2048
                head = 2 * hp + hh
                for j in range(16):
                    sc_w = [(s_v, t_proj_v)]
                    if last_exp_s:
                        sc_w.append((s_s, last_exp_s))
                    if j == 0 and last_val_v:
                        sc_w.append((s_v, last_val_v))
                    for pc in range(4):
                        p.op("tensor", lambda e, j=j, pc=pc, qs=qs, ks=ks: e.matmul(
                            bank(pc), qkv6[:, qs + j * 128:qs + (j + 1) * 128],
                            qkv6[:, ks + pc * 512:ks + (pc + 1) * 512],
                            start=True, stop=True),
                            waits=sc_w if pc == 0 else [],
                            inc=(s_mm, 1) if pc == 3 else None)
                    t_sc = s_mm.count
                    p.op("vector", lambda v: v.tensor_reduce(
                        out=negm_col, in_=ps[:, 0:2048], axis=AX.X, op=OP.max,
                        negate=True), waits=[(s_mm, t_sc)])
                    p.op("vector", lambda v: v.tensor_scalar(
                        out=negm_col, in0=negm_col, scalar1=INV_SQRT_DH,
                        scalar2=None, op0=OP.mult), inc=(s_v, 1), sep=True)
                    t_nm = s_v.count
                    p.op("scalar", lambda e: e.activation(
                        out=Ebuf, in_=ps[:, 0:2048], func=AF.Exp,
                        bias=negm_col, scale=INV_SQRT_DH, accum_out=zsum_col),
                        waits=[(s_v, t_nm)], inc=(s_s, 1))
                    last_exp_s = s_s.count
                    p.op("vector", lambda v: v.reciprocal(
                        out=zrec_col, in_=zsum_col),
                        waits=[(s_s, last_exp_s)], inc=(s_v, 1))
                    t_zr = s_v.count
                    for pc in range(4):
                        p.op("tensor", lambda e, j=j, pc=pc: e.matmul(
                            bank_row(4 + pc), zrec_col,
                            Ebuf[:, pc * 512:(pc + 1) * 512],
                            start=(j == 0), stop=(j == 15)),
                            waits=[(s_v, t_zr)] if pc == 0 else [],
                            inc=(s_mm, 1) if pc == 3 else None)
                t_cs = s_mm.count
                p.op("vector", lambda v: v.tensor_copy(
                    out=slotA, in_=ps[0:1, 2048:4096]),
                    waits=[(s_mm, t_cs)], inc=(s_v, 1))
                t_csv = s_v.count
                for pc in range(4):
                    w = [(s_v, t_csv), (s_s, last_exp_s)] if pc == 0 else []
                    p.op("tensor", lambda e, pc=pc: e.matmul(
                        bank(pc), ones128, slotA[:, pc * 512:(pc + 1) * 512],
                        start=True, stop=True),
                        waits=w, inc=(s_mm, 1) if pc == 3 else None)
                t_crep = s_mm.count
                for pc in range(4):
                    p.op("vector", lambda v, pc=pc, vs=vs: v.tensor_tensor(
                        out=qkv6[:, vs + pc * 512:vs + (pc + 1) * 512],
                        in0=qkv6[:, vs + pc * 512:vs + (pc + 1) * 512],
                        in1=bank(pc), op=OP.mult),
                        waits=[(s_mm, t_crep)] if pc == 0 else [],
                        inc=(s_v, 1) if pc == 3 else None)
                t_val = s_v.count
                last_val_v = t_val
                ssem, t_s = vst.load(lambda g, head=head, vs=vs: g.dma_start(
                    out=val_view[head], in_=qkv6[:, vs:vs + 2048]),
                    waits=[(s_v, t_val)])
                t_store_hist.append((ssem, t_s))
                if debug == "values":
                    vst.load(lambda g, head=head, vs=vs: g.dma_start(
                        out=dbg[head * 128:(head + 1) * 128, :],
                        in_=qkv6[:, vs:vs + 2048]))
            prev_pair_stores = vst.totals()
        return vst.totals(), s_v.count, s_mm.count

    def phase_wo(val_done, p3_v, p3_mm):
        valh = OV_MAIN.alloc("valh", [128, 16 * 1024])
        wob = [OV_MAIN.alloc("wob0", [128, 2048]), OV_MAIN.alloc("wob1", [128, 2048])]
        xpc = [OV_MAIN.alloc("xpc0", [128, 1024]), OV_MAIN.alloc("xpc1", [128, 1024])]
        x1st = [OV_MAIN.alloc("x1st0", [128, 512]), OV_MAIN.alloc("x1st1", [128, 512])]
        sqst = OV_MAIN.alloc("sqst", [128, 512])

        vls = Stream(p, "vls", "gpsimd", 1)
        dbs = Stream(p, "dbs", "gpsimd", 2)
        wos = Stream(p, "wos", "sync", 2)
        xps = Stream(p, "xps", "gpsimd", 2)
        x1s = Stream(p, "x1s", "gpsimd", 2)

        val_view = val_scr.rearrange("(c p) t -> c p t", p=128)
        valh3 = valh.rearrange("p (c t) -> p c t", t=1024)
        wob3 = [w.rearrange("p (c f) -> p c f", f=128) for w in wob]

        stt_hist = {}
        sq_mm_hist = []
        x1_store_hist = []
        x1_read_hist = []
        t_row_v = 0
        for th in range(2):
            w = list(val_done) + [(s_v, p3_v), (s_mm, p3_mm)] if th == 0 else \
                [(s_mm, stats_done_mm[-1])]
            vsem, t_vl = vls.load(lambda g, th=th: g.dma_start(
                out=valh3, in_=val_view[:, :, th * 1024:(th + 1) * 1024]),
                waits=w)
            half_sq_mm = []
            for dm in range(16):
                gi = th * 16 + dm
                w = [(s_mm, p3_mm), (s_v, p3_v)] + list(val_done) if gi < 2 else []
                if gi >= 2:
                    w = [(s_mm, wo_mm_hist[gi - 2])]
                wsem, t_wo = wos.load(lambda e, dm=dm, par=gi % 2: e.dma_start(
                    out=wob3[par], in_=wo[dm].rearrange("(c p) f -> c p f", p=128)),
                    waits=w)
                xw = []
                if gi >= 2:
                    xw = [(s_v, xpc_free_v[gi - 2])]
                elif th == 0 and dm < 2:
                    xw = [(s_v, p3_v), (s_mm, p3_mm)] + list(val_done)
                xsem, t_xp = xps.load(lambda g, th=th, dm=dm, par=gi % 2: g.dma_start(
                    out=xpc[par], in_=xT[dm * 128:(dm + 1) * 128,
                                         th * 1024:(th + 1) * 1024]),
                    waits=xw)
                for pc in range(2):
                    gpc = gi * 2 + pc
                    bk = gpc % 2
                    mm_w = [(vsem, t_vl), (wsem, t_wo)]
                    if bk in stt_hist:
                        mm_w.append((s_v, stt_hist[bk]))
                    if gi == 0 and pc == 0:
                        mm_w.append((s_s, s_s.count))
                    for vf in range(16):
                        p.op("tensor", lambda e, vf=vf, pc=pc, bk=bk, par=gi % 2: e.matmul(
                            bank(bk), wob[par][:, vf * 128:(vf + 1) * 128],
                            valh[:, vf * 1024 + pc * 512:vf * 1024 + pc * 512 + 512],
                            start=(vf == 0), stop=(vf == 15)),
                            waits=mm_w if vf == 0 else [],
                            inc=(s_mm, 1) if vf == 15 else None)
                    t_womm = s_mm.count
                    x1_w = [(s_mm, t_womm), (xsem, t_xp)]
                    if len(sq_mm_hist) >= 2:
                        x1_w += sq_mm_hist[-2][1]
                    if len(x1_read_hist) >= 2:
                        x1_w += x1_read_hist[-2]
                    p.op("vector", lambda v, pc=pc, dm=dm, bk=bk, par=gi % 2: v.scalar_tensor_tensor(
                        out=x1st[bk], in0=bank(bk), scalar=ga_col[:, dm:dm + 1],
                        in1=xpc[par][:, pc * 512:(pc + 1) * 512],
                        op0=OP.mult, op1=OP.add),
                        waits=x1_w, inc=(s_v, 1))
                    stt_hist[bk] = s_v.count
                    if pc == 1:
                        xpc_free_v.append(s_v.count)
                    ssem, t_s = x1s.load(lambda g, th=th, dm=dm, pc=pc, bk=bk: g.dma_start(
                        out=x1_scr[dm * 128:(dm + 1) * 128,
                                   th * 1024 + pc * 512:th * 1024 + pc * 512 + 512],
                        in_=x1st[bk]),
                        waits=[(s_v, stt_hist[bk])])
                    x1_store_hist.append((ssem, t_s))
                    reads = [(ssem, t_s)]
                    if debug == "x1":
                        dsem, t_d = dbs.load(lambda g, th=th, dm=dm, pc=pc, bk=bk: g.dma_start(
                            out=dbg[dm * 128:(dm + 1) * 128,
                                    th * 1024 + pc * 512:th * 1024 + pc * 512 + 512],
                            in_=x1st[bk]), waits=[(s_v, stt_hist[bk])])
                        reads.append((dsem, t_d))
                    x1_read_hist.append(reads)
                    sq_w = [(s_v, stt_hist[bk])]
                    if half_sq_mm:
                        sq_w.append((s_mm, half_sq_mm[-1][0]))
                    p.op("scalar", lambda e, bk=bk: e.activation(
                        out=sqst, in_=x1st[bk], func=AF.Square, bias=0.0, scale=1.0),
                        waits=sq_w, inc=(s_s, 1))
                    t_sq = s_s.count
                    st_w = [(s_v, stt_hist[bk])]
                    if th == 1 and gi == 16 and pc == 0:
                        st_w.append((s_v, t_row_v))
                    p.op("tensor", lambda e, pc=pc, bk=bk, dm=dm: e.matmul(
                        bank_row(4 + pc), ones_col, x1st[bk],
                        start=(dm == 0), stop=(dm == 15)),
                        waits=st_w)
                    p.op("tensor", lambda e, pc=pc, dm=dm: e.matmul(
                        bank_row(6 + pc), ones_col, sqst,
                        start=(dm == 0), stop=(dm == 15)),
                        waits=[(s_s, t_sq)],
                        inc=(s_mm, 1))
                    t_smm = s_mm.count
                    half_sq_mm.append((t_smm, [(s_mm, t_smm)]))
                    sq_mm_hist.append((t_smm, [(s_mm, t_smm)]))
                wo_mm_hist.append(s_mm.count)
            stats_done_mm.append(s_mm.count)
            t_all = s_mm.count
            sA = slotA[:, 0:1024]
            sB = slotB[:, 0:1024]
            p.op("vector", lambda v: v.tensor_scalar(
                out=sB, in0=ps[0:1, 2048:3072], scalar1=float(1.0 / D),
                scalar2=None, op0=OP.mult), waits=[(s_mm, t_all)])
            p.op("vector", lambda v: v.tensor_scalar(
                out=ps[0:1, 3072:4096], in0=ps[0:1, 3072:4096],
                scalar1=float(1.0 / D), scalar2=None, op0=OP.mult))
            p.op("vector", lambda v: v.tensor_tensor(
                out=sA, in0=sB, in1=sB, op=OP.mult), sep=True)
            p.op("vector", lambda v: v.tensor_tensor(
                out=ps[0:1, 3072:4096], in0=ps[0:1, 3072:4096], in1=sA,
                op=OP.subtract), inc=(s_v, 1), sep=True)
            t_var = s_v.count
            p.op("scalar", lambda e: e.activation(
                out=sA, in_=ps[0:1, 3072:4096], func=AF.Sqrt,
                bias=eps11, scale=1.0), waits=[(s_v, t_var)], inc=(s_s, 1))
            t_sq2 = s_s.count
            p.op("vector", lambda v: v.reciprocal(out=sA, in_=sA),
                 waits=[(s_s, t_sq2)], inc=(s_v, 1))
            t_rows2 = s_v.count
            for which, (rowsrc, cdst) in enumerate([(sB, m2_col), (sA, r2_col)]):
                for cc in range(8):
                    k = which * 8 + cc
                    p.op("tensor", lambda e, rowsrc=rowsrc, cc=cc, k=k: e.matmul(
                        bank(2, 16)[:, k:k + 1],
                        rowsrc[:, cc * 128:(cc + 1) * 128], id11,
                        start=(k == 0), stop=(k == 15)),
                        waits=[(s_v, t_rows2)] if k == 0 else [],
                        inc=(s_mm, 1) if k == 15 else None)
            t_cmm = s_mm.count
            p.op("vector", lambda v, th=th: v.tensor_copy(
                out=m2_col[:, th * 8:th * 8 + 8], in_=bank(2, 16)[:, 0:8]),
                waits=[(s_mm, t_cmm)])
            p.op("vector", lambda v, th=th: v.tensor_copy(
                out=r2_col[:, th * 8:th * 8 + 8], in_=bank(2, 16)[:, 8:16]),
                inc=(s_v, 1))
            t_row_v = s_v.count
        return x1_store_hist, x1s.totals(), s_v.count, s_mm.count

    qsc = {}

    def alloc_quant_scratch():
        qsc["in"] = [OV_MAIN.alloc("qin0", [128, 2048]), OV_MAIN.alloc("qin1", [128, 2048])]
        qsc["tstg"] = [OV_MAIN.alloc("qtstg0", [128, 2048], BF16),
                       OV_MAIN.alloc("qtstg1", [128, 2048], BF16)]
        for si in range(2):
            st = {"prev_v": 0, "prev_s": 0, "prev_g": 0}
            for nm in ("y", "c", "d"):
                st[nm] = OV_Q.alloc(f"q{nm}{si}", [128, QW])
            for nm in ("amx", "lg", "s0", "cs"):
                st[nm] = OV_Q.alloc(f"q{nm}{si}", [128, 64])
            st["tt"] = st["d"][:, 0:64]
            st["gg"] = st["lg"]
            st["invs"] = st["cs"]
            st["scl"] = st["amx"]
            qsc[si] = st
        qsc["silu"] = qsc[1]["y"][:, 0:256]

    def q_pair(jobs):
        assert len(jobs) == 2
        for job in jobs:
            s = qsc[job["set"]]
            job["S"] = s
            job["src3"] = job["src"].rearrange("p (b k) -> p b k", k=32)
            job["velt"] = c_velt4 if job["mfrac"] == 3 else c_velt2
            job["fix"] = c_fix3 if job["mfrac"] == 3 else c_fix1
            job["mn"] = 7.5 if job["mfrac"] == 3 else 6.0
            job["vp"], job["sp"], job["gp"] = s["prev_v"], s["prev_s"], s["prev_g"]

        def opV(job, fn, extra=(), mark=None):
            w = ([(s_v, job["vp"])] if job["vp"] else []) + list(extra)
            p.op("vector", fn, waits=w, inc=(s_v, 1))
            job["vp"] = s_v.count
            if mark:
                job[mark] = s_v.count

        def opP(job, fn, extra=(), mark=None):
            w = ([(s_g, job["gp"])] if job["gp"] else []) + list(extra)
            p.op("gpsimd", fn, waits=w, inc=(s_g, 1))
            job["gp"] = s_g.count
            if mark:
                job[mark] = s_g.count

        def opS(job, fn, extra=(), mark=None):
            w = ([(s_s, job["sp"])] if job["sp"] else []) + list(extra)
            p.op("scalar", fn, waits=w, inc=(s_s, 1))
            job["sp"] = s_s.count
            if mark:
                job[mark] = s_s.count

        for job in jobs:
            extra = list(job.get("w_v", ()))
            if job["sp"]:
                extra.append((s_s, job["sp"]))
            if job["gp"]:
                extra.append((s_g, job["gp"]))
            opV(job, lambda v, s=job["S"], src3=job["src3"]: v.tensor_reduce(
                out=s["amx"], in_=src3, axis=AX.X, op=OP.max,
                apply_absolute_value=True), extra=extra)
        for job in jobs:
            opV(job, lambda v, s=job["S"]: v.tensor_scalar(
                out=s["amx"], in0=s["amx"], scalar1=float(2.0 ** -96),
                scalar2=None, op0=OP.max), mark="m_amx")
        for job in jobs:
            opS(job, lambda e, s=job["S"]: e.activation(
                out=s["lg"], in_=s["amx"], func=AF.Ln, bias=0.0, scale=1.0),
                extra=list(job.get("w_s", ())) + [(s_v, job["m_amx"])],
                mark="m_lg")
        for job in jobs:
            opV(job, lambda v, s=job["S"]: v.tensor_scalar(
                out=s["tt"], in0=s["lg"], scalar1=INV_LN2, scalar2=c_round,
                op0=OP.mult, op1=OP.add), extra=[(s_s, job["m_lg"])])
        for job in jobs:
            opV(job, lambda v, s=job["S"]: v.tensor_scalar(
                out=s["tt"], in0=s["tt"], scalar1=c_round, scalar2=None,
                op0=OP.subtract), mark="m_r")
        for job in jobs:
            opS(job, lambda e, s=job["S"]: e.activation(
                out=s["s0"], in_=s["tt"], func=AF.Exp, bias=0.0, scale=LN2C),
                extra=[(s_v, job["m_r"])], mark="m_s0")
        first = True
        for fn in (
            lambda v, s: v.tensor_scalar(out=s["cs"], in0=s["s0"],
                                         scalar1=c_velt_s, scalar2=None,
                                         op0=OP.mult),
            lambda v, s: v.tensor_tensor(out=s["tt"], in0=s["cs"], in1=s["s0"],
                                         op=OP.subtract),
            lambda v, s: v.tensor_tensor(out=s["s0"], in0=s["cs"], in1=s["tt"],
                                         op=OP.subtract),
            lambda v, s: v.tensor_tensor(out=s["gg"], in0=s["s0"], in1=s["amx"],
                                         op=OP.is_gt),
            lambda v, s: v.tensor_scalar(out=s["gg"], in0=s["gg"], scalar1=-0.5,
                                         scalar2=1.0, op0=OP.mult, op1=OP.add),
            lambda v, s: v.tensor_tensor(out=s["s0"], in0=s["s0"], in1=s["gg"],
                                         op=OP.mult),
            lambda v, s: v.tensor_scalar(out=s["tt"], in0=s["s0"], scalar1=2.0,
                                         scalar2=None, op0=OP.mult),
            lambda v, s: v.tensor_tensor(out=s["gg"], in0=s["tt"], in1=s["amx"],
                                         op=OP.is_le),
            lambda v, s: v.tensor_scalar(out=s["gg"], in0=s["gg"], scalar1=1.0,
                                         scalar2=None, op0=OP.add),
            lambda v, s: v.tensor_tensor(out=s["s0"], in0=s["s0"], in1=s["gg"],
                                         op=OP.mult),
            lambda v, s: v.reciprocal(out=s["invs"], in_=s["s0"]),
            lambda v, s: v.tensor_scalar(out=s["invs"], in0=s["invs"],
                                         scalar1=4.0, scalar2=None, op0=OP.mult),
            lambda v, s: v.tensor_scalar(out=s["scl"], in0=s["s0"], scalar1=0.25,
                                         scalar2=None, op0=OP.mult),
        ):
            for job in jobs:
                extra = [(s_s, job["m_s0"])] if first else ()
                opV(job, lambda v, fn=fn, s=job["S"]: fn(v, s), extra=extra)
            first = False
        for st_ in range(4):
            odd = (st_ % 2 == 1)
            for job in jobs:
                s = job["S"]
                bsl = slice(st_ * 16, (st_ + 1) * 16)
                job["y3"] = s["y"].rearrange("p (b k) -> p b k", k=32)
                job["src3s"] = job["src3"][:, bsl, :]
                job["qeF"] = job["src"][:, st_ * QW:(st_ + 1) * QW]
                job["invs3"] = s["invs"][:, bsl].unsqueeze(2).broadcast_to([128, 16, 32])
                job["scl3"] = s["scl"][:, bsl].unsqueeze(2).broadcast_to([128, 16, 32])
            for job in jobs:
                extra = [(s_g, job["gp"])] if job["gp"] else []
                opV(job, lambda v, y3=job["y3"], src3s=job["src3s"],
                         invs3=job["invs3"]: v.tensor_tensor(
                    out=y3, in0=src3s, in1=invs3,
                    op=OP.mult), extra=extra, mark="m1")
            for job in jobs:
                opS(job, lambda e, s=job["S"], velt=job["velt"]: e.activation(
                    out=s["c"], in_=s["y"], func=AF.Copy, bias=0.0,
                    scale=velt), extra=[(s_v, job["m1"])], mark="m2")
            for job in jobs:
                opS(job, lambda e, s=job["S"]: e.activation(
                    out=s["d"], in_=s["y"], func=AF.Square, bias=0.0, scale=1.0),
                    mark="m3")
            for job in jobs:
                opV(job, lambda x, s=job["S"], qeF=job["qeF"]: x.tensor_tensor(
                    out=qeF, in0=s["c"], in1=s["y"], op=OP.subtract),
                    extra=[(s_s, job["m2"])])
            for job in jobs:
                opV(job, lambda x, s=job["S"], qeF=job["qeF"]: x.tensor_tensor(
                    out=s["c"], in0=s["c"], in1=qeF, op=OP.subtract))
            for job in jobs:
                opV(job, lambda v, s=job["S"]: v.tensor_scalar(
                    out=s["d"], in0=s["d"], scalar1=1.0, scalar2=None,
                    op0=OP.is_ge), extra=[(s_s, job["m3"])])
            for job in jobs:
                opV(job, lambda v, s=job["S"], qeF=job["qeF"],
                         fx=job["fix"]: v.tensor_scalar(
                    out=qeF, in0=s["y"], scalar1=fx,
                    scalar2=fx, op0=OP.add, op1=OP.subtract))
            for job in jobs:
                opV(job, lambda x, s=job["S"], qeF=job["qeF"]: x.tensor_tensor(
                    out=s["y"], in0=s["c"], in1=qeF, op=OP.subtract))
            for job in jobs:
                opV(job, lambda x, s=job["S"]: x.tensor_tensor(
                    out=s["d"], in0=s["d"], in1=s["y"], op=OP.mult))
            for job in jobs:
                opV(job, lambda v, s=job["S"], qeF=job["qeF"]: v.tensor_tensor(
                    out=qeF, in0=qeF, in1=s["d"], op=OP.add))
            for job in jobs:
                opV(job, lambda v, qeF=job["qeF"], mn=float(job["mn"]):
                     v.tensor_scalar(
                    out=qeF, in0=qeF, scalar1=mn,
                    scalar2=-mn, op0=OP.min, op1=OP.max))
            for job in jobs:
                opV(job, lambda v, src3s=job["src3s"], scl3=job["scl3"]:
                     v.tensor_tensor(
                    out=src3s, in0=src3s, in1=scl3,
                    op=OP.mult))
        for job in jobs:
            s = job["S"]
            s["prev_v"], s["prev_s"], s["prev_g"] = job["vp"], job["sp"], job["gp"]
        return [job["vp"] for job in jobs]

    def emit_qtile_tr_only(srcbf, waits_t, bank_free):
        marks = []
        for b4 in range(4):
            bkk = 4 + b4
            for tt_ in range(4):
                idx = b4 * 4 + tt_
                w = []
                if tt_ == 0:
                    w = list(waits_t) if b4 == 0 else []
                    if bank_free:
                        w.append(bank_free[b4])
                p.op("tensor", lambda e, idx=idx, bkk=bkk, tt_=tt_, srcbf=srcbf: e.matmul(
                    ps[:, bkk * 512 + tt_ * 128:bkk * 512 + (tt_ + 1) * 128],
                    srcbf[:, idx * 128:(idx + 1) * 128], ident,
                    start=(tt_ == 0), stop=(tt_ == 3), is_transpose=True),
                    waits=w, inc=(s_mm, 1) if tt_ == 3 else None)
            marks.append(s_mm.count)
        return marks

    def emit_qtile_evacs(par, tr4, dst_dma_fn, store_stream, store_waits):
        tstg = qsc["tstg"][par]
        ev = []
        for b4 in range(4):
            bkk = 4 + b4
            evw = [(s_mm, tr4[b4])] + (list(store_waits) if b4 == 0 else [])
            p.op("scalar", lambda e, b4=b4, bkk=bkk, tstg=tstg: e.activation(
                out=tstg[:, b4 * 512:(b4 + 1) * 512],
                in_=ps[:, bkk * 512:bkk * 512 + 512],
                func=AF.Copy, bias=0.0, scale=1.0),
                waits=evw, inc=(s_s, 1))
            ev.append((s_s, s_s.count))
        ssem, t_s = store_stream.load(dst_dma_fn(tstg),
                                      waits=[ev[-1]] + list(store_waits))
        return ev, (ssem, t_s)

    def phase_wquant(entry_waits_v, entry_waits_mm):
        alloc_quant_scratch()
        qls = Stream(p, "qls", "sync", 2)
        qst = Stream(p, "qst", "gpsimd", 2)

        wgT3 = wgqT_scr.rearrange("(hc q) o -> hc q o", q=128)
        wuT3 = wuqT_scr.rearrange("(hc q) o -> hc q o", q=128)

        jobs = []
        for oc in range(IFF // 128):
            jobs.append((lambda oc=oc: wg_h[oc * 128:(oc + 1) * 128, :],
                         lambda tstg, oc=oc: (lambda g: g.dma_start(
                             out=wgT3[:, :, oc * 128:(oc + 1) * 128]
                                 .transpose([1, 0, 2]),
                             in_=tstg.rearrange("q (hc j) -> q hc j", j=128)))))
        for oc in range(IFF // 128):
            jobs.append((lambda oc=oc: wu_h[oc * 128:(oc + 1) * 128, :],
                         lambda tstg, oc=oc: (lambda g: g.dma_start(
                             out=wuT3[:, :, oc * 128:(oc + 1) * 128]
                                 .transpose([1, 0, 2]),
                             in_=tstg.rearrange("q (hc j) -> q hc j", j=128)))))
        for dc in range(16):
            wdT3 = wdqT_scr[dc].rearrange("(ic q) j -> ic q j", q=128)
            for islab in range(4):
                jobs.append((lambda dc=dc, islab=islab:
                             wd_h[dc * 128:(dc + 1) * 128,
                                  islab * 2048:(islab + 1) * 2048],
                             lambda tstg, dc=dc, islab=islab, wdT3=wdT3:
                             (lambda g: g.dma_start(
                                 out=wdT3[islab * 16:(islab + 1) * 16]
                                     .transpose([1, 0, 2]),
                                 in_=tstg.rearrange("q (ic j) -> q ic j", j=128)))))

        tr_marks = {}
        st_marks = {}
        pend = None
        ev_last = None
        for tp in range(0, len(jobs), 2):
            lsems = []
            for j in (0, 1):
                t = tp + j
                if t < 2:
                    lw = [(s_v, entry_waits_v), (s_mm, entry_waits_mm)]
                else:
                    lw = [(s_mm, tr_marks[t - 2])]
                lsems.append(qls.load(lambda e, srcfn=jobs[t][0], j=j: e.dma_start(
                    out=qsc["in"][j], in_=srcfn()), waits=lw))
            pj = []
            for j in (0, 1):
                wv = [lsems[j]]
                if tp + j < 2:
                    wv.append((s_v, entry_waits_v))
                pj.append(dict(src=qsc["in"][j], set=j, mfrac=1, w_v=wv, w_s=[]))
            mfin = q_pair(pj)
            if pend is not None:
                pt, ppar, ptr4, pdst = pend
                sw = [st_marks[pt - 2]] if pt - 2 in st_marks else []
                ev_last, st_marks[pt] = emit_qtile_evacs(ppar, ptr4, pdst, qst, sw)
            for j in (0, 1):
                t = tp + j
                tw = [(s_v, mfin[j])]
                if t == 0:
                    tw.append((s_mm, entry_waits_mm))
                tr4 = emit_qtile_tr_only(qsc["in"][j], tw, ev_last)
                tr_marks[t] = tr4[-1]
                if j == 0:
                    sw = [st_marks[t - 2]] if t - 2 in st_marks else []
                    ev_last, st_marks[t] = emit_qtile_evacs(j, tr4, jobs[t][1], qst, sw)
                else:
                    pend = (t, j, tr4, jobs[t][1])
        pt, ppar, ptr4, pdst = pend
        sw = [st_marks[pt - 2]] if pt - 2 in st_marks else []
        ev_last, st_marks[pt] = emit_qtile_evacs(ppar, ptr4, pdst, qst, sw)
        return qst.totals(), s_v.count, s_mm.count

    def phase_ffn(x1_done, wq_done, p5_v, p5_mm):
        x1r = [OV_MAIN.alloc("x1r0", [128, 512]), OV_MAIN.alloc("x1r1", [128, 512])]
        outst = [OV_MAIN.alloc("outst0", [128, 512]), OV_MAIN.alloc("outst1", [128, 512])]
        xT4 = OV_MAIN.alloc("xT4", [128, 8192])
        repB1 = OV_MAIN.alloc("repB1", [128, 2048])
        repsb = OV_MAIN.alloc("repsb", [128, 2048])
        h2q = OV_MAIN.alloc("h2q", [128, 16 * 512], BF16)
        interq = OV_MAIN.alloc("interq", [128, 64 * 512], BF16)
        wst = OV_MAIN.alloc("wst", [128, 16384], BF16)

        xls = Stream(p, "xls", "gpsimd", 2)
        xle = Stream(p, "xle", "gpsimd", 2)
        wgl = Stream(p, "wgl", "sync", 4)
        wdl = Stream(p, "wdl", "sync", 2)
        ous = Stream(p, "ous", "gpsimd", 2)

        wgT3 = wgqT_scr.rearrange("(hc q) o -> hc q o", q=128)
        wuT3 = wuqT_scr.rearrange("(hc q) o -> hc q o", q=128)
        wst3 = wst.rearrange("q (pr w hc c) -> q pr w hc c", pr=2, w=2, hc=16)
        wstd3 = wst.rearrange("q (pr ic c) -> q pr ic c", pr=2, ic=64)

        ent = [(s_v, p5_v), (s_mm, p5_mm)]
        t_prev_ev = 0
        for which, seg in ((0, 3), (1, 4)):
            for cc in range(NC):
                w = list(ent) if (which == 0 and cc == 0) else []
                if which == 1 and cc == 0:
                    w = [(s_v, t_prev_ev)]
                p.op("tensor", lambda e, which=which, seg=seg, cc=cc: e.matmul(
                    ps[0:1, cc * 128:(cc + 1) * 128],
                    mod_cols[:, seg * NC + cc:seg * NC + cc + 1], ident,
                    start=(cc % 4 == 0), stop=(cc % 4 == 3)),
                    waits=w, inc=(s_mm, 1) if cc == NC - 1 else None)
            t_row = s_mm.count
            dstrow = slotA if which == 0 else slotB
            p.op("vector", lambda v, dstrow=dstrow: v.tensor_copy(
                out=dstrow, in_=ps[0:1, 0:2048]),
                waits=[(s_mm, t_row)], inc=(s_v, 1))
            t_prev_ev = s_v.count
        p.op("vector", lambda v: v.tensor_scalar(
            out=slotB, in0=slotB, scalar1=1.0, scalar2=None, op0=OP.add),
            inc=(s_v, 1), sep=True)
        t_rows = s_v.count
        evh = []
        for krep in range(8):
            which, piece = divmod(krep, 4)
            src = slotB if which == 0 else slotA
            rdst = repB1 if which == 0 else repsb
            w = [(s_v, t_rows)]
            if krep >= 2:
                w.append((s_v, evh[krep - 2]))
            p.op("tensor", lambda e, src=src, piece=piece, krep=krep: e.matmul(
                bank(krep % 2), ones128, src[:, piece * 512:(piece + 1) * 512],
                start=True, stop=True), waits=w, inc=(s_mm, 1))
            t_k = s_mm.count
            p.op("vector", lambda v, rdst=rdst, piece=piece, krep=krep: v.tensor_copy(
                out=rdst[:, piece * 512:(piece + 1) * 512], in_=bank(krep % 2)),
                waits=[(s_mm, t_k)], inc=(s_v, 1))
            evh.append(s_v.count)
        t_reps = s_v.count

        out_stores = []
        wgl_hist = []
        dbg_mark = [0]
        dbg_rows = [256]
        t_prev_group_done = 0
        for g in range(4):
            gtc = [g * 4 + tc for tc in range(4)]
            ln_ev = []
            x1l_hist = []
            for fc in range(NC):
                lw = []
                if fc >= 2:
                    lw = [(s_mm, x1l_hist[fc - 2])]
                elif g == 0:
                    lw = list(x1_done) + ent
                elif g > 0:
                    lw = [(s_mm, t_prev_group_done)]
                lsem, t_l = xls.load(lambda gg_, fc=fc, g=g: gg_.dma_start(
                    out=x1r[fc % 2], in_=x1_scr[fc * 128:(fc + 1) * 128,
                                                g * 512:(g + 1) * 512]),
                    waits=lw)
                bkk = 4 + (fc % 2)
                tw = [(lsem, t_l)]
                if len(ln_ev) >= 2:
                    tw.append((s_v, ln_ev[-2]))
                if fc < 2 and g == 0:
                    tw.append((s_s, s_s.count))
                for tc in range(4):
                    p.op("tensor", lambda e, tc=tc, bkk=bkk, fc=fc: e.matmul(
                        ps[:, bkk * 512 + tc * 128:bkk * 512 + (tc + 1) * 128],
                        x1r[fc % 2][:, tc * 128:(tc + 1) * 128], ident,
                        start=(tc == 0), stop=(tc == 3), is_transpose=True),
                        waits=tw if tc == 0 else [],
                        inc=(s_mm, 1) if tc == 3 else None)
                t_tr = s_mm.count
                x1l_hist.append(t_tr)
                for tc in range(4):
                    p.op("vector", lambda v, tc=tc, fc=fc, bkk=bkk, g=g: v.tensor_scalar(
                        out=xT4[:, tc * 2048 + fc * 128:tc * 2048 + (fc + 1) * 128],
                        in0=ps[:, bkk * 512 + tc * 128:bkk * 512 + (tc + 1) * 128],
                        scalar1=m2_col[:, g * 4 + tc:g * 4 + tc + 1],
                        scalar2=r2_col[:, g * 4 + tc:g * 4 + tc + 1],
                        op0=OP.subtract, op1=OP.mult),
                        waits=[(s_mm, t_tr)] if tc == 0 else [],
                        inc=(s_v, 1) if tc == 3 else None)
                ln_ev.append(s_v.count)
            t_u = s_v.count
            for tc in range(4):
                p.op("gpsimd", lambda gg_, tc=tc: gg_.tensor_tensor(
                    out=xT4[:, tc * 2048:(tc + 1) * 2048],
                    in0=xT4[:, tc * 2048:(tc + 1) * 2048],
                    in1=repB1, op=OP.mult),
                    waits=([(s_v, t_u), (s_v, t_reps)] if tc == 0 else []),
                    inc=(s_g, 1))
                t_m = s_g.count
                p.op("vector", lambda v, tc=tc: v.tensor_tensor(
                    out=xT4[:, tc * 2048:(tc + 1) * 2048],
                    in0=xT4[:, tc * 2048:(tc + 1) * 2048],
                    in1=repsb, op=OP.add),
                    waits=[(s_g, t_m)], inc=(s_v, 1))
            t_h2add = s_v.count
            hq_ev = 0
            h2q3 = h2q.rearrange("q (fc t) -> q fc t", t=512)
            for tcp in (0, 2):
                mfin = q_pair([dict(src=xT4[:, tc * 2048:(tc + 1) * 2048],
                                    set=tc - tcp, mfrac=3,
                                    w_v=[(s_v, t_h2add)], w_s=[])
                               for tc in (tcp, tcp + 1)])
                for tc in (tcp, tcp + 1):
                    ev_by_b4 = []
                    for b4 in range(4):
                        bkk = 4 + (b4 % 2)
                        tw2 = []
                        if b4 == 0:
                            tw2 = [(s_v, mfin[tc - tcp])]
                            if hq_ev:
                                tw2.append((s_v, hq_ev))
                            if ln_ev:
                                tw2.append((s_v, ln_ev[-1]))
                        elif b4 >= 2:
                            tw2 = [(s_v, ev_by_b4[b4 - 2])]
                        for tt_ in range(4):
                            idx = b4 * 4 + tt_
                            p.op("tensor", lambda e, idx=idx, bkk=bkk, tt_=tt_, tc=tc: e.matmul(
                                ps[:, bkk * 512 + tt_ * 128:bkk * 512 + (tt_ + 1) * 128],
                                xT4[:, tc * 2048 + idx * 128:tc * 2048 + (idx + 1) * 128],
                                ident,
                                start=(tt_ == 0), stop=(tt_ == 3), is_transpose=True),
                                waits=tw2 if tt_ == 0 else [],
                                inc=(s_mm, 1) if tt_ == 3 else None)
                        t_tr2 = s_mm.count
                        h2ev_w = [(s_mm, t_tr2)]
                        if debug == "ffn" and tc == 0 and b4 == 0 and dbg_mark[0]:
                            h2ev_w.append((p.sem("dbgst"), dbg_mark[0]))
                        p.op("vector", lambda v, b4=b4, tc=tc, bkk=bkk, h2q3=h2q3: v.tensor_copy(
                            out=h2q3[:, b4 * 4:(b4 + 1) * 4, tc * 128:(tc + 1) * 128],
                            in_=ps[:, bkk * 512:bkk * 512 + 512]
                                .rearrange("q (i c) -> q i c", c=128)),
                            waits=h2ev_w, inc=(s_v, 1))
                        ev_by_b4.append(s_v.count)
                        hq_ev = s_v.count
            t_h2q_v = s_v.count
            t_h2q_mm = s_mm.count
            if debug == "ffn":
                p.op("gpsimd", lambda gg_, g=g: gg_.dma_start(
                    out=h2q_dbg[g], in_=h2q), waits=[(s_v, t_h2q_v)],
                    inc=(p.sem("dbgst"), 16))
            dbg_mark[0] = p.sem("dbgst").count

            iq_tr_mm = 0
            iq_ev_v = 0
            ring = [x1r[0], x1r[1], outst[0], outst[1]]
            cp_marks = []
            pm_marks = []
            trtc = {tc: t_h2q_mm for tc in range(4)}
            for oq in range(4):
                for op8 in range(8):
                    opi = oq * 8 + op8
                    gop = g * 32 + opi
                    if opi < 2:
                        lw = (list(wq_done) + [(s_mm, t_h2q_mm)]) if g == 0 else \
                            [(s_mm, t_prev_group_done)]
                    else:
                        lw = [(s_mm, wgl_hist[gop - 2])]
                    par = gop % 2
                    lsg, t_lg = wgl.load(lambda e, opi=opi, par=par: e.dma_start(
                        out=wst3[:, par, 0],
                        in_=wgT3[:, :, opi * 256:(opi + 1) * 256].transpose([1, 0, 2])),
                        waits=lw)
                    lsu, t_lu = wgl.load(lambda e, opi=opi, par=par: e.dma_start(
                        out=wst3[:, par, 1],
                        in_=wuT3[:, :, opi * 256:(opi + 1) * 256].transpose([1, 0, 2])))
                    for tc in range(4):
                        nn = len(cp_marks)
                        bko = 2 * (nn % 2)
                        slot = ring[nn % 4]
                        mm_w = [(lsg, t_lg), (lsu, t_lu), (s_v, t_h2q_v)]
                        if nn >= 2:
                            mm_w.append((s_s, cp_marks[nn - 2]))
                        for kc in range(NC):
                            p.op("tensor", lambda e, kc=kc, tc=tc, par=par, bko=bko: e.matmul(
                                bank(bko, 256), h2q[:, kc * 512 + tc * 128:kc * 512 + (tc + 1) * 128],
                                wst3[:, par, 0, kc], start=(kc == 0), stop=(kc == NC - 1)),
                                waits=mm_w if kc == 0 else [])
                        for kc in range(NC):
                            p.op("tensor", lambda e, kc=kc, tc=tc, par=par, bko=bko: e.matmul(
                                bank(bko + 1, 256), h2q[:, kc * 512 + tc * 128:kc * 512 + (tc + 1) * 128],
                                wst3[:, par, 1, kc], start=(kc == 0), stop=(kc == NC - 1)),
                                waits=[], inc=(s_mm, 1) if kc == NC - 1 else None)
                        t_gu = s_mm.count
                        si_w = [(s_mm, t_gu)]
                        if nn >= 4:
                            si_w.append((s_g, pm_marks[nn - 4]))
                        elif nn >= 2 and len(out_stores) >= 2:
                            si_w += out_stores[-2:]
                        p.op("scalar", lambda e, bko=bko, slot=slot: e.activation(
                            out=slot[:, 0:256], in_=bank(bko, 256), func=AF.Silu,
                            bias=0.0, scale=1.0),
                            waits=si_w, inc=(s_s, 1))
                        p.op("scalar", lambda e, bko=bko, slot=slot: e.activation(
                            out=slot[:, 256:512], in_=bank(bko + 1, 256),
                            func=AF.Copy, bias=0.0, scale=1.0),
                            inc=(s_s, 1))
                        cp_marks.append(s_s.count)
                        p.op("gpsimd", lambda x, tc=tc, op8=op8, slot=slot: x.tensor_tensor(
                            out=xT4[:, tc * 2048 + op8 * 256:tc * 2048 + (op8 + 1) * 256],
                            in0=slot[:, 0:256], in1=slot[:, 256:512], op=OP.mult),
                            waits=[(s_s, cp_marks[-1]), (s_mm, trtc[tc])],
                            inc=(s_g, 1))
                        pm_marks.append(s_g.count)
                    wgl_hist.append(s_mm.count)
                iq3 = interq.rearrange("q (ic t) -> q ic t", t=512)
                mfin4 = [0] * 4
                for tcp in (0, 2):
                    mm_ = q_pair([dict(src=xT4[:, tc * 2048:(tc + 1) * 2048],
                                       set=tc - tcp, mfrac=3,
                                       w_v=[(s_g, pm_marks[-1])], w_s=[])
                                  for tc in (tcp, tcp + 1)])
                    mfin4[tcp], mfin4[tcp + 1] = mm_
                for tc in range(4):
                    last_v = mfin4[tc]
                    ev_by_b4 = []
                    for b4 in range(4):
                        bkk = 4 + (b4 % 2)
                        tw2 = []
                        if b4 == 0:
                            tw2 = [(s_v, last_v)]
                            if iq_ev_v:
                                tw2.append((s_v, iq_ev_v))
                        elif b4 >= 2:
                            tw2 = [(s_v, ev_by_b4[b4 - 2])]
                        for tt_ in range(4):
                            idx = b4 * 4 + tt_
                            p.op("tensor", lambda e, idx=idx, bkk=bkk, tt_=tt_, tc=tc: e.matmul(
                                ps[:, bkk * 512 + tt_ * 128:bkk * 512 + (tt_ + 1) * 128],
                                xT4[:, tc * 2048 + idx * 128:tc * 2048 + (idx + 1) * 128],
                                ident,
                                start=(tt_ == 0), stop=(tt_ == 3), is_transpose=True),
                                waits=tw2 if tt_ == 0 else [],
                                inc=(s_mm, 1) if tt_ == 3 else None)
                        t_tr2 = s_mm.count
                        iqev_w = [(s_mm, t_tr2)]
                        if debug == "ffn" and oq == 0 and tc == 0 and b4 == 0 and dbg_mark[0]:
                            iqev_w.append((p.sem("dbgst"), dbg_mark[0]))
                        p.op("vector", lambda v, b4=b4, tc=tc, oq=oq, bkk=bkk, iq3=iq3: v.tensor_copy(
                            out=iq3[:, oq * 16 + b4 * 4:oq * 16 + (b4 + 1) * 4,
                                    tc * 128:(tc + 1) * 128],
                            in_=ps[:, bkk * 512:bkk * 512 + 512]
                                .rearrange("q (i c) -> q i c", c=128)),
                            waits=iqev_w, inc=(s_v, 1))
                        ev_by_b4.append(s_v.count)
                        iq_ev_v = s_v.count
                    iq_tr_mm = s_mm.count
                    trtc[tc] = s_mm.count
            t_stage_g = s_g.count
            t_iq_v = s_v.count
            t_iq_mm = s_mm.count
            if debug == "ffn":
                p.op("gpsimd", lambda gg_, g=g: gg_.dma_start(
                    out=iq_dbg[g], in_=interq), waits=[(s_v, t_iq_v)],
                    inc=(p.sem("dbgst"), 16))
            dbg_mark[0] = p.sem("dbgst").count

            dn_stt = {}
            for dm in range(16):
                gdm = g * 16 + dm
                par = gdm % 2
                if dm < 2:
                    lw = (list(wq_done) if g == 0 else []) + [(s_mm, t_iq_mm)]
                else:
                    lw = [(s_mm, dn_mm_hist[gdm - 2])]
                wdT3 = wdqT_scr[dm].rearrange("(ic q) j -> ic q j", q=128)
                lsd, t_ld = wdl.load(lambda e, dm=dm, par=par, wdT3=wdT3: e.dma_start(
                    out=wstd3[:, par], in_=wdT3.transpose([1, 0, 2])), waits=lw)
                xw = [(s_v, dn_stt[par])] if par in dn_stt else []
                if len(out_stores) >= 2:
                    xw.append(out_stores[-2])
                if dm < 2:
                    xw.append((s_g, t_stage_g))
                if gdm < 2 and g == 0:
                    xw = list(x1_done) + [(s_g, t_stage_g)]
                lsx, t_lx = xle.load(lambda gg_, dm=dm, g=g, par=par: gg_.dma_start(
                    out=x1r[par] if False else outst[par],
                    in_=x1_scr[dm * 128:(dm + 1) * 128, g * 512:(g + 1) * 512]),
                    waits=xw)
                bkd = par
                mm_w = [(lsd, t_ld), (s_v, t_iq_v)]
                if bkd in dn_stt:
                    mm_w.append((s_v, dn_stt[bkd]))
                for ic in range(64):
                    p.op("tensor", lambda e, ic=ic, par=par, bkd=bkd: e.matmul(
                        bank(bkd), wstd3[:, par, ic],
                        interq[:, ic * 512:(ic + 1) * 512],
                        start=(ic == 0), stop=(ic == 63)),
                        waits=mm_w if ic == 0 else [],
                        inc=(s_mm, 1) if ic == 63 else None)
                t_dn = s_mm.count
                dn_mm_hist.append(t_dn)
                st_w = [(s_mm, t_dn), (lsx, t_lx)]
                if len(out_stores) >= 2:
                    st_w.append(out_stores[-2])
                p.op("vector", lambda v, dm=dm, par=par, bkd=bkd: v.scalar_tensor_tensor(
                    out=outst[par], in0=bank(bkd), scalar=gb_col[:, dm:dm + 1],
                    in1=outst[par], op0=OP.mult, op1=OP.add),
                    waits=st_w, inc=(s_v, 1))
                dn_stt[par] = s_v.count
                osem, t_os = ous.load(lambda gg_, dm=dm, g=g, par=par: gg_.dma_start(
                    out=outT[dm * 128:(dm + 1) * 128, g * 512:(g + 1) * 512],
                    in_=outst[par]), waits=[(s_v, dn_stt[par])])
                out_stores.append((osem, t_os))
            t_prev_group_done = s_mm.count
        return ous.totals()

    seg_slab_mm = []
    pair_proj_mm = []
    wo_mm_hist = []
    xpc_free_v = []
    stats_done_mm = []
    dn_mm_hist = []
    init_consts()
    mod_v, mod_mm, slab_done = phase_mod()
    t_h_done = []
    val_done = []
    if phases >= 2:
        OV_MAIN.reset()
        t_h_done = phase_ln1_h(mod_v, mod_mm, slab_done)
    x1_done = []
    if phases >= 3:
        OV_MAIN.reset()
        val_done, p3_v, p3_mm = phase_attn(t_h_done, p.sem("s_v").count,
                                           p.sem("s_mm").count)
    if phases >= 4:
        OV_MAIN.reset()
        _, x1_done, p4_v, p4_mm = phase_wo(val_done, p3_v, p3_mm)
    out_done = []
    if phases >= 5:
        wq_done, wq_v, wq_mm = phase_wquant(p4_v, p4_mm)
        OV_MAIN.reset()
        out_done = phase_ffn(x1_done, wq_done, wq_v, wq_mm)

    if debug == "mod":
        p.op("gpsimd", lambda g: g.dma_start(out=dbg[0:128, 0:6 * NC], in_=mod_cols),
             waits=[(s_v, s_v.count)], inc=(s_st, 16))
        p.op("gpsimd", lambda g: g.dma_start(out=dbg[0:128, 96:96 + NC], in_=sc_col),
             inc=(s_st, 16))
        p.op("gpsimd", lambda g: g.dma_start(out=dbg[200:201, 0:2048], in_=prow[:, 0:2048]),
             inc=(s_st, 16))

    if out_done:
        p.op("tensor", lambda e: e.nop(), waits=list(out_done))

    if phases < 5:
        for kc in range(NC):
            p.op("sync", lambda e, kc=kc: e.dma_start(
                out=outT[kc * 128:(kc + 1) * 128, 0:96], in_=mod_cols),
                waits=[(s_v, s_v.count)], inc=(s_out, 16))
        p.op("tensor", lambda e: e.nop(), waits=[(s_out, s_out.count)])
    if t_h_done:
        p.op("tensor", lambda e: e.nop(), waits=list(t_h_done))
    if val_done:
        p.op("tensor", lambda e: e.nop(), waits=list(val_done))
    if x1_done:
        p.op("tensor", lambda e: e.nop(), waits=list(x1_done))

    emit_program(nc, p)
    return nc


def _prep_inputs(L0_attn_x, L0_attn_w_qkv, L0_attn_w_o, L0_ffn_w_gate,
                 L0_ffn_w_up, L0_ffn_w_down, L0_c, L0_wmod):
    x = np.asarray(L0_attn_x, np.float32)
    wqkv_b = np.ascontiguousarray(
        np.asarray(L0_attn_w_qkv, np.float32).reshape(D, 48, 128).transpose(1, 0, 2))
    wo_b = np.ascontiguousarray(
        np.asarray(L0_attn_w_o, np.float32).reshape(D, 16, 128).transpose(1, 0, 2))
    wmod6_ = np.ascontiguousarray(
        np.asarray(L0_wmod, np.float32).reshape(D, 6, D).transpose(1, 0, 2))
    wg = np.ascontiguousarray(np.asarray(L0_ffn_w_gate, np.float32))
    wu = np.ascontiguousarray(np.asarray(L0_ffn_w_up, np.float32))
    wd = np.ascontiguousarray(np.asarray(L0_ffn_w_down, np.float32))
    c = np.asarray(L0_c, np.float32)
    maps = []
    for b in range(8):
        maps.append({
            "xT": np.ascontiguousarray(x[b].T),
            "wqkv": wqkv_b, "wo": wo_b, "wg": wg, "wu": wu, "wd": wd,
            "ccol": np.ascontiguousarray(c[b].reshape(NC, 128).T),
            "wmod6": wmod6_,
        })
    return maps


def kernel(**inputs):
    maps = _prep_inputs(**inputs)
    nc = build_nc(debug=DEBUG, phases=PHASES)
    res = run_bass_kernel_spmd(nc, maps, core_ids=list(range(8)))
    out = np.stack([np.ascontiguousarray(r["outT"].T) for r in res.results])
    return out


# version 10
# speedup vs baseline: 1.0635x; 1.0102x over previous
import contextlib
import numpy as np
import ml_dtypes
import concourse.bass as bass
import concourse.mybir as mybir
from concourse.bass_utils import run_bass_kernel_spmd

F32 = mybir.dt.float32
BF16 = mybir.dt.bfloat16
U8 = mybir.dt.uint8
OP = mybir.AluOpType
AF = mybir.ActivationFunctionType
AX = mybir.AxisListType

D = 2048
SEQ = 2048
NH = 16
IFF = 8192
NC = D // 128
INV_SQRT_DH = float(1.0 / np.sqrt(np.float64(128.0)))
LN2C = float(np.log(2.0))
INV_LN2 = float(1.0 / np.log(2.0))

PHASES = 5
DEBUG = None
QW = 512


class Sem:
    __slots__ = ("name", "count", "handle")

    def __init__(self, name):
        self.name, self.count, self.handle = name, 0, None

    def inc(self, n=1):
        self.count += n
        return self.count


class Prog:
    ENGINES = ("sync", "tensor", "vector", "scalar", "gpsimd")

    def __init__(self):
        self.ops = []
        self.sems = {}

    def sem(self, name):
        if name not in self.sems:
            self.sems[name] = Sem(name)
        return self.sems[name]

    def op(self, engine, fn, waits=(), inc=None, sep=False):
        if inc is not None:
            inc[0].inc(inc[1])
        self.ops.append((engine, tuple(waits), inc, fn, sep))


class Stream:

    def __init__(self, prog, name, engine, nbuf):
        self.prog, self.engine, self.nbuf = prog, engine, nbuf
        self.sems = [prog.sem(f"{name}_{i}") for i in range(nbuf)]
        self.idx = 0

    def load(self, fn, waits=()):
        sem = self.sems[self.idx % self.nbuf]
        self.idx += 1
        self.prog.op(self.engine, fn, waits=waits, inc=(sem, 16))
        return sem, sem.count

    def totals(self):
        return [(sem, sem.count) for sem in self.sems]


def emit_program(nc, prog):
    with contextlib.ExitStack() as stack:
        for s in prog.sems.values():
            s.handle = stack.enter_context(nc.semaphore(s.name))
        block = stack.enter_context(nc.Block())
        per_engine = {e: [] for e in Prog.ENGINES}
        for rec in prog.ops:
            per_engine[rec[0]].append(rec)

        def runner(engname):
            def run(eng):
                marks = {}
                for _, waits, inc, fn, sep in per_engine[engname]:
                    for sem, val in waits:
                        if val > 0 and marks.get(sem.name, -1) < val:
                            eng.wait_ge(sem.handle, val)
                            marks[sem.name] = val
                    if sep:
                        eng.drain()
                    ins = fn(eng)
                    if inc is not None:
                        sem, amount = inc
                        ins.then_inc(sem.handle, amount)
            return run

        for e in Prog.ENGINES:
            getattr(block, e)(runner(e))


class Overlay:
    def __init__(self, nc, base_bytes, size_bytes, tag):
        self.nc, self.base, self.size, self.tag = nc, base_bytes, size_bytes, tag
        self.cur = base_bytes
        self.n = 0

    def reset(self):
        self.cur = self.base

    def alloc(self, name, shape, dt=F32):
        esz = {F32: 4, BF16: 2, U8: 1}[dt]
        nbytes = int(np.prod(shape[1:])) * esz
        off = (self.cur + 31) & ~31
        self.cur = off + nbytes
        assert self.cur <= self.base + self.size, \
            f"overlay {self.tag} overflow at {name}: {self.cur - self.base} > {self.size}"
        self.n += 1
        return self.nc.alloc_sbuf_tensor_at(f"{self.tag}{self.n}_{name}", shape, dt,
                                            offset=off).ap()


def build_nc(debug=None, phases=PHASES):
    nc = bass.Bass("TRN2", target_bir_lowering=False, debug=False)
    p = Prog()

    xT = nc.dram_tensor("xT", [D, SEQ], F32, kind="ExternalInput").ap()
    wqkv = nc.dram_tensor("wqkv", [48, D, 128], F32, kind="ExternalInput").ap()
    wo = nc.dram_tensor("wo", [16, D, 128], F32, kind="ExternalInput").ap()
    wg_h = nc.dram_tensor("wg", [IFF, D], F32, kind="ExternalInput").ap()
    wu_h = nc.dram_tensor("wu", [IFF, D], F32, kind="ExternalInput").ap()
    wd_h = nc.dram_tensor("wd", [D, IFF], F32, kind="ExternalInput").ap()
    ccol_h = nc.dram_tensor("ccol", [128, NC], F32, kind="ExternalInput").ap()
    wmod6 = nc.dram_tensor("wmod6", [6, D, D], F32, kind="ExternalInput").ap()

    outT = nc.dram_tensor("outT", [D, SEQ], F32, kind="ExternalOutput").ap()
    dbg = nc.dram_tensor("dbg", [D, SEQ], F32, kind="ExternalOutput").ap() if debug else None

    h_scr = nc.dram_tensor("h_scr", [D, SEQ], F32).ap()
    val_scr = nc.dram_tensor("val_scr", [D, SEQ], F32).ap()
    x1_scr = nc.dram_tensor("x1_scr", [D, SEQ], F32).ap()
    wgqT_scr = nc.dram_tensor("wgqT_scr", [D, IFF], BF16).ap()
    h2q_dbg = nc.dram_tensor("h2q_dbg", [4, 128, 16 * 512], BF16).ap()
    iq_dbg = nc.dram_tensor("iq_dbg", [4, 128, 64 * 512], BF16).ap()
    wuqT_scr = nc.dram_tensor("wuqT_scr", [D, IFF], BF16).ap()
    wdqT_scr = nc.dram_tensor("wdqT_scr", [16, IFF, 128], BF16).ap()

    sb = lambda name, shape, dt=F32: nc.alloc_sbuf_tensor(name, shape, dt).ap()
    ones_col = sb("ones_col", [128, 1])
    flagc = sb("flagc", [128, 1])
    c_round = sb("c_round", [128, 1])
    c_velt_s = sb("c_velt_s", [128, 1])
    c_fix3 = sb("c_fix3", [128, 1])
    c_velt4 = sb("c_velt4", [128, 1])
    c_fix1 = sb("c_fix1", [128, 1])
    c_velt2 = sb("c_velt2", [128, 1])
    ident = sb("ident", [128, 128])
    identb = sb("identb", [128, 128], BF16)
    sc_col = sb("sc_col", [128, NC])
    mod_cols = sb("mod_cols", [128, 6 * NC])
    a1_col = sb("a1_col", [128, NC])
    b1_col = sb("b1_col", [128, NC])
    ga_col = sb("ga_col", [128, NC])
    gb_col = sb("gb_col", [128, NC])
    m2_col = sb("m2_col", [128, NC])
    r2_col = sb("r2_col", [128, NC])
    gcol_scr = sb("gcol_scr", [128, NC])
    negm_col = sb("negm_col", [128, 1])
    zrec_col = sb("zrec_col", [128, 1])
    zsum_col = sb("zsum_col", [128, 1])
    prow = sb("prow", [1, 4480])
    slotA = prow[:, 0:2048]
    slotB = prow[:, 2048:4096]
    ones128 = prow[:, 4096:4224]
    id11 = prow[:, 4224:4225]
    eps11 = prow[:, 4225:4226]

    TOP = 229344
    OV_MAIN = Overlay(nc, TOP - 168 * 1024, 168 * 1024, "M")
    OV_Q = Overlay(nc, TOP - 182 * 1024, 14 * 1024, "Q")

    ps = nc.alloc_psum_tensor("ps", [128, 4096], F32).ap()

    def bank(i, cols=512, off=0):
        return ps[:, i * 512 + off:i * 512 + off + cols]

    def bank_row(i, cols=512):
        return ps[0:1, i * 512:i * 512 + cols]

    s_wld = p.sem("s_wld")
    s_xld = p.sem("s_xld")
    s_st = p.sem("s_st")
    s_sld = p.sem("s_sld")
    s_mm = p.sem("s_mm")
    s_v = p.sem("s_v")
    s_s = p.sem("s_s")
    s_g = p.sem("s_g")
    s_out = p.sem("s_out")

    def init_consts():
        p.op("vector", lambda v: v.memset(ones_col, 1.0))
        p.op("vector", lambda v: v.memset(prow, 0.0))
        p.op("vector", lambda v: v.memset(ones128, 1.0), sep=True)
        p.op("vector", lambda v: v.memset(id11, 1.0))
        p.op("vector", lambda v: v.memset(eps11, 1e-5))
        p.op("vector", lambda v: v.tensor_scalar(
            out=flagc, in0=ones_col, scalar1=float(2.0 ** -30), scalar2=None,
            op0=OP.add))
        p.op("vector", lambda v: v.drain())
        p.op("vector", lambda v: v.tensor_tensor(
            out=flagc, in0=flagc, in1=ones_col, op=OP.is_equal))
        p.op("vector", lambda v: v.drain())

        def mkconst(dst, c32, c64):
            p.op("vector", lambda v, dst=dst: v.tensor_scalar(
                out=dst, in0=flagc, scalar1=-1.0, scalar2=1.0,
                op0=OP.mult, op1=OP.add))
            p.op("vector", lambda v, dst=dst, c64=c64: v.tensor_scalar(
                out=dst, in0=dst, scalar1=float(c64), scalar2=None, op0=OP.mult),
                sep=True)
            p.op("vector", lambda v, dst=dst, c32=c32: v.scalar_tensor_tensor(
                out=dst, in0=flagc, scalar=float(c32), in1=dst,
                op0=OP.mult, op1=OP.add), sep=True)

        mkconst(c_round, 1.5 * 2.0 ** 23, 1.5 * 2.0 ** 52)
        mkconst(c_velt_s, 2.0 ** 22 + 1, 2.0 ** 51 + 1)
        mkconst(c_fix3, 1.5 * 2.0 ** 20, 1.5 * 2.0 ** 49)
        mkconst(c_velt4, 2.0 ** 20 + 1, 2.0 ** 49 + 1)
        mkconst(c_fix1, 1.5 * 2.0 ** 22, 1.5 * 2.0 ** 51)
        mkconst(c_velt2, 2.0 ** 22 + 1, 2.0 ** 51 + 1)
        p.op("vector", lambda v: v.drain(), inc=(s_v, 1))
        p.op("gpsimd", lambda g: g.memset(ident, 0.0))
        p.op("gpsimd", lambda g: g.affine_select(
            out=ident, in_=ident, compare_op=OP.not_equal, fill=1.0, base=0,
            pattern=[[-1, 128]], channel_multiplier=1), sep=True)
        p.op("gpsimd", lambda g: g.memset(identb, 0.0))
        p.op("gpsimd", lambda g: g.affine_select(
            out=identb, in_=identb, compare_op=OP.not_equal, fill=1.0, base=0,
            pattern=[[-1, 128]], channel_multiplier=1), inc=(s_g, 1), sep=True)

    def phase_mod():
        slabs = [OV_MAIN.alloc("slab0", [128, 2048]),
                 OV_MAIN.alloc("slab1", [128, 2048])]
        p.op("sync", lambda e: e.dma_start(out=sc_col, in_=ccol_h), inc=(s_sld, 16))
        t_cld = s_sld.count
        p.op("scalar", lambda e: e.activation(
            out=sc_col, in_=sc_col, func=AF.Silu, bias=0.0, scale=1.0),
            waits=[(s_sld, t_cld), (s_v, 1)], inc=(s_s, 1))
        t_silu = s_s.count

        t_evac_v = 0
        t_tr_mm = 0
        slab_stream = Stream(p, "sw", "sync", 2)
        for seg in range(6):
            for kc in range(NC):
                i = seg * NC + kc
                waits = [(s_mm, seg_slab_mm[i - 2])] if i >= 2 else []
                lsem, t_ld = slab_stream.load(
                    lambda e, seg=seg, kc=kc, i=i: e.dma_start(
                        out=slabs[i % 2], in_=wmod6[seg, kc * 128:(kc + 1) * 128, :]),
                    waits=waits)
                mm_w = [(lsem, t_ld), (s_s, t_silu)]
                if kc == 0 and seg > 0:
                    mm_w.append((s_v, t_evac_v))
                for piece in range(4):
                    p.op("tensor", lambda e, kc=kc, piece=piece, i=i: e.matmul(
                        bank_row(piece), sc_col[:, kc:kc + 1],
                        slabs[i % 2][:, piece * 512:(piece + 1) * 512],
                        start=(kc == 0), stop=(kc == NC - 1)),
                        waits=mm_w if piece == 0 else [],
                        inc=(s_mm, 1) if piece == 3 else None)
                seg_slab_mm.append(s_mm.count)
            t_seg_mm = s_mm.count
            p.op("vector", lambda v: v.tensor_copy(out=slotA, in_=ps[0:1, 0:2048]),
                 waits=[(s_mm, t_seg_mm), (s_mm, t_tr_mm)], inc=(s_v, 1))
            t_evac_v = s_v.count
            for j in range(16):
                jj = seg * 16 + j
                p.op("tensor", lambda e, j=j, jj=jj: e.matmul(
                    bank(4, 96)[:, jj:jj + 1], slotA[:, j * 128:(j + 1) * 128],
                    id11, start=(jj == 0), stop=(jj == 95)),
                    waits=[(s_v, t_evac_v), (s_g, 1)] if j == 0 else [],
                    inc=(s_mm, 1) if j == 15 else None)
            t_tr_mm = s_mm.count

        p.op("vector", lambda v: v.tensor_copy(out=mod_cols, in_=bank(4, 96)),
             waits=[(s_mm, t_tr_mm)], inc=(s_v, 1))
        t_cols_v = s_v.count

        p.op("vector", lambda v: v.tensor_scalar(
            out=a1_col, in0=mod_cols[:, NC:2 * NC], scalar1=1.0, scalar2=None,
            op0=OP.add), waits=[(s_v, t_cols_v)])
        p.op("vector", lambda v: v.tensor_scalar(
            out=b1_col, in0=mod_cols[:, 4 * NC:5 * NC], scalar1=1.0, scalar2=None,
            op0=OP.add))

        def gtransform(dst, src):
            p.op("vector", lambda v, dst=dst, src=src: v.tensor_scalar(
                out=dst, in0=src, scalar1=0.0, scalar2=2.0,
                op0=OP.is_ge, op1=OP.mult))
            p.op("vector", lambda v, dst=dst: v.tensor_scalar(
                out=dst, in0=dst, scalar1=-1.0, scalar2=None, op0=OP.add), sep=True)
            p.op("vector", lambda v, dst=dst, src=src: v.tensor_tensor(
                out=gcol_scr, in0=dst, in1=src, op=OP.mult), sep=True)
            p.op("vector", lambda v: v.tensor_scalar(
                out=gcol_scr, in0=gcol_scr, scalar1=0.25, scalar2=None, op0=OP.max),
                sep=True)
            p.op("vector", lambda v, dst=dst: v.tensor_tensor(
                out=dst, in0=dst, in1=gcol_scr, op=OP.mult), sep=True)

        gtransform(ga_col, mod_cols[:, 2 * NC:3 * NC])
        gtransform(gb_col, mod_cols[:, 5 * NC:6 * NC])
        p.op("vector", lambda v: v.drain(), inc=(s_v, 1))
        return s_v.count, t_tr_mm, slab_stream.totals()

    def phase_ln1_h(mod_v, mod_mm, slab_done):
        xb = [OV_MAIN.alloc("xb0", [128, 2048]), OV_MAIN.alloc("xb1", [128, 2048])]
        sqb = OV_MAIN.alloc("sqb", [128, 2048])
        hb = [OV_MAIN.alloc("hb0", [128, 2048]), OV_MAIN.alloc("hb1", [128, 2048])]
        repa = OV_MAIN.alloc("repa", [128, 2048])
        repb = OV_MAIN.alloc("repb", [128, 2048])

        chunk_mm = []
        xs = Stream(p, "xs", "gpsimd", 2)
        for kc in range(NC):
            waits = [(s_mm, chunk_mm[kc - 2])] if kc >= 2 else \
                (slab_done + [(s_mm, mod_mm)])
            lsem, t_ld = xs.load(lambda g, kc=kc: g.dma_start(
                out=xb[kc % 2], in_=xT[kc * 128:(kc + 1) * 128, :]),
                waits=waits)
            sq_w = [(lsem, t_ld)]
            if kc >= 1:
                sq_w.append((s_mm, chunk_mm[kc - 1]))
            p.op("scalar", lambda e, kc=kc: e.activation(
                out=sqb, in_=xb[kc % 2], func=AF.Square, bias=0.0, scale=1.0),
                waits=sq_w, inc=(s_s, 1))
            t_sq = s_s.count
            mm_w = [(lsem, t_ld), (s_v, 1)]
            if kc == 0:
                mm_w += [(s_mm, mod_mm), (s_v, mod_v)]
            for piece in range(4):
                p.op("tensor", lambda e, kc=kc, piece=piece: e.matmul(
                    bank_row(piece), ones_col,
                    xb[kc % 2][:, piece * 512:(piece + 1) * 512],
                    start=(kc == 0), stop=(kc == NC - 1)),
                    waits=mm_w if piece == 0 else [])
            for piece in range(4):
                p.op("tensor", lambda e, kc=kc, piece=piece: e.matmul(
                    bank_row(4 + piece), ones_col,
                    sqb[:, piece * 512:(piece + 1) * 512],
                    start=(kc == 0), stop=(kc == NC - 1)),
                    waits=[(s_s, t_sq)] if piece == 0 else [],
                    inc=(s_mm, 1) if piece == 3 else None)
            chunk_mm.append(s_mm.count)
        t_stats = s_mm.count

        p.op("vector", lambda v: v.tensor_scalar(
            out=slotB, in0=ps[0:1, 0:2048], scalar1=float(1.0 / D),
            scalar2=None, op0=OP.mult), waits=[(s_mm, t_stats)])
        p.op("vector", lambda v: v.tensor_scalar(
            out=ps[0:1, 2048:4096], in0=ps[0:1, 2048:4096], scalar1=float(1.0 / D),
            scalar2=None, op0=OP.mult))
        p.op("vector", lambda v: v.tensor_tensor(
            out=slotA, in0=slotB, in1=slotB, op=OP.mult), sep=True)
        p.op("vector", lambda v: v.tensor_tensor(
            out=ps[0:1, 2048:4096], in0=ps[0:1, 2048:4096], in1=slotA,
            op=OP.subtract), inc=(s_v, 1), sep=True)
        t_var = s_v.count
        p.op("scalar", lambda e: e.activation(
            out=slotA, in_=ps[0:1, 2048:4096], func=AF.Sqrt,
            bias=eps11, scale=1.0),
            waits=[(s_v, t_var)], inc=(s_s, 1))
        t_rstd = s_s.count
        p.op("vector", lambda v: v.reciprocal(out=slotA, in_=slotA),
             waits=[(s_s, t_rstd)])
        p.op("vector", lambda v: v.scalar_tensor_tensor(
            out=slotB, in0=slotB, scalar=-1.0, in1=slotA,
            op0=OP.mult, op1=OP.mult), inc=(s_v, 1), sep=True)
        t_rows = s_v.count

        evac_hist = []
        for k in range(8):
            which, piece = divmod(k, 4)
            src = slotA if which == 0 else slotB
            w = [(s_v, t_rows)]
            if k >= 2:
                w.append((s_v, evac_hist[k - 2]))
            p.op("tensor", lambda e, src=src, piece=piece, k=k: e.matmul(
                bank(k % 2), ones128, src[:, piece * 512:(piece + 1) * 512],
                start=True, stop=True),
                waits=w, inc=(s_mm, 1))
            t_k = s_mm.count
            rdst = repa if which == 0 else repb
            p.op("vector", lambda v, rdst=rdst, piece=piece, k=k: v.tensor_copy(
                out=rdst[:, piece * 512:(piece + 1) * 512], in_=bank(k % 2)),
                waits=[(s_mm, t_k)], inc=(s_v, 1))
            evac_hist.append(s_v.count)
        t_reps = s_v.count

        t1h, t2h, sth = [], [], []
        hst = Stream(p, "hst", "gpsimd", 2)
        for kc in range(NC):
            ld_w = [(s_v, t1h[kc - 2])] if kc >= 2 else [(s_mm, t_stats)]
            lsem, t_ld = xs.load(lambda g, kc=kc: g.dma_start(
                out=xb[kc % 2], in_=xT[kc * 128:(kc + 1) * 128, :]),
                waits=ld_w)
            t1_w = [(lsem, t_ld), (s_v, t_reps)]
            if kc >= 1:
                t1_w.append((s_g, t2h[kc - 1]))
            p.op("vector", lambda v, kc=kc: v.tensor_tensor(
                out=sqb, in0=xb[kc % 2], in1=repa, op=OP.mult),
                waits=t1_w, inc=(s_v, 1))
            t1h.append(s_v.count)
            t2_w = [(s_v, t1h[-1])]
            if kc >= 2:
                t2_w.append(sth[kc - 2])
            p.op("gpsimd", lambda g, kc=kc: g.tensor_tensor(
                out=hb[kc % 2], in0=sqb, in1=repb, op=OP.add),
                waits=t2_w, inc=(s_g, 1))
            t2h.append(s_g.count)
            p.op("vector", lambda v, kc=kc: v.tensor_scalar(
                out=hb[kc % 2], in0=hb[kc % 2],
                scalar1=a1_col[:, kc:kc + 1], scalar2=mod_cols[:, kc:kc + 1],
                op0=OP.mult, op1=OP.add),
                waits=[(s_g, t2h[-1])], inc=(s_v, 1))
            t_h = s_v.count
            ssem, t_s = hst.load(lambda g, kc=kc: g.dma_start(
                out=h_scr[kc * 128:(kc + 1) * 128, :], in_=hb[kc % 2]),
                waits=[(s_v, t_h)])
            if debug == "h":
                ssem, t_s = hst.load(lambda g, kc=kc: g.dma_start(
                    out=dbg[kc * 128:(kc + 1) * 128, :], in_=hb[kc % 2]))
            sth.append((ssem, t_s))
        return hst.totals()

    def phase_attn(h_done, p2_v, p2_mm):
        hq8 = [OV_MAIN.alloc("hq0", [128, 4096]), OV_MAIN.alloc("hq1", [128, 4096])]
        qkv6 = OV_MAIN.alloc("qkv6", [128, 6 * 2048])
        wqp = OV_MAIN.alloc("wqp", [128, 6 * 2048])
        Ebuf = OV_MAIN.alloc("Ebuf", [128, 2048])

        h8 = Stream(p, "h8", "gpsimd", 2)
        wqs = Stream(p, "wqs", "sync", 6)
        vst = Stream(p, "vst", "gpsimd", 2)

        h_view = h_scr.rearrange("(c p) t -> c p t", p=128)
        val_view = val_scr.rearrange("(c p) t -> c p t", p=128)

        proj_done_hist = []
        wq_done = {}
        evac_hist = {}
        last_exp_s = 0
        last_val_v = 0
        t_store_hist = []
        prev_pair_stores = []

        for hp in range(8):
            blocks = [2 * hp, 16 + 2 * hp, 32 + 2 * hp,
                      2 * hp + 1, 16 + 2 * hp + 1, 32 + 2 * hp + 1]
            for slot, blk in enumerate(blocks):
                w = []
                if hp == 0:
                    w = list(h_done)
                    w.append((s_v, p2_v))
                else:
                    w = [(s_mm, pair_proj_mm[hp - 1])]
                wqp3 = wqp.rearrange("p (s k f) -> p s k f", s=6, k=NC)
                wq_done[slot] = wqs.load(
                    lambda e, slot=slot, blk=blk, wqp3=wqp3: e.dma_start(
                        out=wqp3[:, slot], in_=wqkv[blk].rearrange("(k p) f -> k p f", p=128)),
                    waits=w if slot < 2 else [])
            for e8 in range(8):
                ge = hp * 8 + e8
                w = []
                if ge >= 2:
                    w = [(s_mm, proj_done_hist[ge - 2])]
                elif hp == 0:
                    w = list(h_done) + [(s_v, p2_v), (s_mm, p2_mm)]
                hsem, t_h = h8.load(
                    lambda g, e8=e8, par=ge % 2: g.dma_start(
                        out=hq8[par].rearrange("p (k t) -> p k t", t=256),
                        in_=h_view[:, :, e8 * 256:(e8 + 1) * 256]),
                    waits=w)
                for slot in range(6):
                    bk = slot % 2
                    mm_w = [(hsem, t_h), wq_done[slot]]
                    if bk in evac_hist:
                        mm_w.append((s_v, evac_hist[bk]))
                    if ge == 0 and slot == 0:
                        mm_w.append((s_mm, p2_mm))
                    for kc in range(NC):
                        p.op("tensor", lambda e, slot=slot, kc=kc, bk=bk, par=ge % 2: e.matmul(
                            bank(bk, 256), wqp[:, slot * 2048 + kc * 128:slot * 2048 + (kc + 1) * 128],
                            hq8[par][:, kc * 256:(kc + 1) * 256],
                            start=(kc == 0), stop=(kc == NC - 1)),
                            waits=mm_w if kc == 0 else [],
                            inc=(s_mm, 1) if kc == NC - 1 else None)
                    t_g = s_mm.count
                    ev_w = [(s_mm, t_g)]
                    if e8 == 0 and slot == 0 and prev_pair_stores:
                        ev_w += prev_pair_stores
                    p.op("vector", lambda v, slot=slot, e8=e8, bk=bk: v.tensor_copy(
                        out=qkv6[:, slot * 2048 + e8 * 256:slot * 2048 + (e8 + 1) * 256],
                        in_=bank(bk, 256)),
                        waits=ev_w, inc=(s_v, 1))
                    evac_hist[bk] = s_v.count
                proj_done_hist.append(s_mm.count)
            pair_proj_mm.append(s_mm.count)
            t_proj_v = s_v.count

            for hh in range(2):
                qs = (3 * hh) * 2048
                ks = (3 * hh + 1) * 2048
                vs = (3 * hh + 2) * 2048
                head = 2 * hp + hh
                for j in range(16):
                    sc_w = [(s_v, t_proj_v)]
                    if last_exp_s:
                        sc_w.append((s_s, last_exp_s))
                    if j == 0 and last_val_v:
                        sc_w.append((s_v, last_val_v))
                    for pc in range(4):
                        p.op("tensor", lambda e, j=j, pc=pc, qs=qs, ks=ks: e.matmul(
                            bank(pc), qkv6[:, qs + j * 128:qs + (j + 1) * 128],
                            qkv6[:, ks + pc * 512:ks + (pc + 1) * 512],
                            start=True, stop=True),
                            waits=sc_w if pc == 0 else [],
                            inc=(s_mm, 1) if pc == 3 else None)
                    t_sc = s_mm.count
                    p.op("vector", lambda v: v.tensor_reduce(
                        out=negm_col, in_=ps[:, 0:2048], axis=AX.X, op=OP.max,
                        negate=True), waits=[(s_mm, t_sc)])
                    p.op("vector", lambda v: v.tensor_scalar(
                        out=negm_col, in0=negm_col, scalar1=INV_SQRT_DH,
                        scalar2=None, op0=OP.mult), inc=(s_v, 1), sep=True)
                    t_nm = s_v.count
                    p.op("scalar", lambda e: e.activation(
                        out=Ebuf, in_=ps[:, 0:2048], func=AF.Exp,
                        bias=negm_col, scale=INV_SQRT_DH, accum_out=zsum_col),
                        waits=[(s_v, t_nm)], inc=(s_s, 1))
                    last_exp_s = s_s.count
                    p.op("vector", lambda v: v.reciprocal(
                        out=zrec_col, in_=zsum_col),
                        waits=[(s_s, last_exp_s)], inc=(s_v, 1))
                    t_zr = s_v.count
                    for pc in range(4):
                        p.op("tensor", lambda e, j=j, pc=pc: e.matmul(
                            bank_row(4 + pc), zrec_col,
                            Ebuf[:, pc * 512:(pc + 1) * 512],
                            start=(j == 0), stop=(j == 15)),
                            waits=[(s_v, t_zr)] if pc == 0 else [],
                            inc=(s_mm, 1) if pc == 3 else None)
                t_cs = s_mm.count
                p.op("vector", lambda v: v.tensor_copy(
                    out=slotA, in_=ps[0:1, 2048:4096]),
                    waits=[(s_mm, t_cs)], inc=(s_v, 1))
                t_csv = s_v.count
                for pc in range(4):
                    w = [(s_v, t_csv), (s_s, last_exp_s)] if pc == 0 else []
                    p.op("tensor", lambda e, pc=pc: e.matmul(
                        bank(pc), ones128, slotA[:, pc * 512:(pc + 1) * 512],
                        start=True, stop=True),
                        waits=w, inc=(s_mm, 1) if pc == 3 else None)
                t_crep = s_mm.count
                for pc in range(4):
                    p.op("vector", lambda v, pc=pc, vs=vs: v.tensor_tensor(
                        out=qkv6[:, vs + pc * 512:vs + (pc + 1) * 512],
                        in0=qkv6[:, vs + pc * 512:vs + (pc + 1) * 512],
                        in1=bank(pc), op=OP.mult),
                        waits=[(s_mm, t_crep)] if pc == 0 else [],
                        inc=(s_v, 1) if pc == 3 else None)
                t_val = s_v.count
                last_val_v = t_val
                ssem, t_s = vst.load(lambda g, head=head, vs=vs: g.dma_start(
                    out=val_view[head], in_=qkv6[:, vs:vs + 2048]),
                    waits=[(s_v, t_val)])
                t_store_hist.append((ssem, t_s))
                if debug == "values":
                    vst.load(lambda g, head=head, vs=vs: g.dma_start(
                        out=dbg[head * 128:(head + 1) * 128, :],
                        in_=qkv6[:, vs:vs + 2048]))
            prev_pair_stores = vst.totals()
        return vst.totals(), s_v.count, s_mm.count

    def phase_wo(val_done, p3_v, p3_mm):
        valh = OV_MAIN.alloc("valh", [128, 16 * 1024])
        wob = [OV_MAIN.alloc("wob0", [128, 2048]), OV_MAIN.alloc("wob1", [128, 2048])]
        xpc = [OV_MAIN.alloc("xpc0", [128, 1024]), OV_MAIN.alloc("xpc1", [128, 1024])]
        x1st = [OV_MAIN.alloc("x1st0", [128, 512]), OV_MAIN.alloc("x1st1", [128, 512])]
        sqst = OV_MAIN.alloc("sqst", [128, 512])

        vls = Stream(p, "vls", "gpsimd", 1)
        dbs = Stream(p, "dbs", "gpsimd", 2)
        wos = Stream(p, "wos", "sync", 2)
        xps = Stream(p, "xps", "gpsimd", 2)
        x1s = Stream(p, "x1s", "gpsimd", 2)

        val_view = val_scr.rearrange("(c p) t -> c p t", p=128)
        valh3 = valh.rearrange("p (c t) -> p c t", t=1024)
        wob3 = [w.rearrange("p (c f) -> p c f", f=128) for w in wob]

        stt_hist = {}
        sq_mm_hist = []
        x1_store_hist = []
        x1_read_hist = []
        t_row_v = 0
        for th in range(2):
            w = list(val_done) + [(s_v, p3_v), (s_mm, p3_mm)] if th == 0 else \
                [(s_mm, stats_done_mm[-1])]
            vsem, t_vl = vls.load(lambda g, th=th: g.dma_start(
                out=valh3, in_=val_view[:, :, th * 1024:(th + 1) * 1024]),
                waits=w)
            half_sq_mm = []
            for dm in range(16):
                gi = th * 16 + dm
                w = [(s_mm, p3_mm), (s_v, p3_v)] + list(val_done) if gi < 2 else []
                if gi >= 2:
                    w = [(s_mm, wo_mm_hist[gi - 2])]
                wsem, t_wo = wos.load(lambda e, dm=dm, par=gi % 2: e.dma_start(
                    out=wob3[par], in_=wo[dm].rearrange("(c p) f -> c p f", p=128)),
                    waits=w)
                xw = []
                if gi >= 2:
                    xw = [(s_v, xpc_free_v[gi - 2])]
                elif th == 0 and dm < 2:
                    xw = [(s_v, p3_v), (s_mm, p3_mm)] + list(val_done)
                xsem, t_xp = xps.load(lambda g, th=th, dm=dm, par=gi % 2: g.dma_start(
                    out=xpc[par], in_=xT[dm * 128:(dm + 1) * 128,
                                         th * 1024:(th + 1) * 1024]),
                    waits=xw)
                for pc in range(2):
                    gpc = gi * 2 + pc
                    bk = gpc % 2
                    mm_w = [(vsem, t_vl), (wsem, t_wo)]
                    if bk in stt_hist:
                        mm_w.append((s_v, stt_hist[bk]))
                    if gi == 0 and pc == 0:
                        mm_w.append((s_s, s_s.count))
                    for vf in range(16):
                        p.op("tensor", lambda e, vf=vf, pc=pc, bk=bk, par=gi % 2: e.matmul(
                            bank(bk), wob[par][:, vf * 128:(vf + 1) * 128],
                            valh[:, vf * 1024 + pc * 512:vf * 1024 + pc * 512 + 512],
                            start=(vf == 0), stop=(vf == 15)),
                            waits=mm_w if vf == 0 else [],
                            inc=(s_mm, 1) if vf == 15 else None)
                    t_womm = s_mm.count
                    x1_w = [(s_mm, t_womm), (xsem, t_xp)]
                    if len(sq_mm_hist) >= 2:
                        x1_w += sq_mm_hist[-2][1]
                    if len(x1_read_hist) >= 2:
                        x1_w += x1_read_hist[-2]
                    p.op("vector", lambda v, pc=pc, dm=dm, bk=bk, par=gi % 2: v.scalar_tensor_tensor(
                        out=x1st[bk], in0=bank(bk), scalar=ga_col[:, dm:dm + 1],
                        in1=xpc[par][:, pc * 512:(pc + 1) * 512],
                        op0=OP.mult, op1=OP.add),
                        waits=x1_w, inc=(s_v, 1))
                    stt_hist[bk] = s_v.count
                    if pc == 1:
                        xpc_free_v.append(s_v.count)
                    ssem, t_s = x1s.load(lambda g, th=th, dm=dm, pc=pc, bk=bk: g.dma_start(
                        out=x1_scr[dm * 128:(dm + 1) * 128,
                                   th * 1024 + pc * 512:th * 1024 + pc * 512 + 512],
                        in_=x1st[bk]),
                        waits=[(s_v, stt_hist[bk])])
                    x1_store_hist.append((ssem, t_s))
                    reads = [(ssem, t_s)]
                    if debug == "x1":
                        dsem, t_d = dbs.load(lambda g, th=th, dm=dm, pc=pc, bk=bk: g.dma_start(
                            out=dbg[dm * 128:(dm + 1) * 128,
                                    th * 1024 + pc * 512:th * 1024 + pc * 512 + 512],
                            in_=x1st[bk]), waits=[(s_v, stt_hist[bk])])
                        reads.append((dsem, t_d))
                    x1_read_hist.append(reads)
                    sq_w = [(s_v, stt_hist[bk])]
                    if half_sq_mm:
                        sq_w.append((s_mm, half_sq_mm[-1][0]))
                    p.op("scalar", lambda e, bk=bk: e.activation(
                        out=sqst, in_=x1st[bk], func=AF.Square, bias=0.0, scale=1.0),
                        waits=sq_w, inc=(s_s, 1))
                    t_sq = s_s.count
                    st_w = [(s_v, stt_hist[bk])]
                    if th == 1 and gi == 16 and pc == 0:
                        st_w.append((s_v, t_row_v))
                    p.op("tensor", lambda e, pc=pc, bk=bk, dm=dm: e.matmul(
                        bank_row(4 + pc), ones_col, x1st[bk],
                        start=(dm == 0), stop=(dm == 15)),
                        waits=st_w)
                    p.op("tensor", lambda e, pc=pc, dm=dm: e.matmul(
                        bank_row(6 + pc), ones_col, sqst,
                        start=(dm == 0), stop=(dm == 15)),
                        waits=[(s_s, t_sq)],
                        inc=(s_mm, 1))
                    t_smm = s_mm.count
                    half_sq_mm.append((t_smm, [(s_mm, t_smm)]))
                    sq_mm_hist.append((t_smm, [(s_mm, t_smm)]))
                wo_mm_hist.append(s_mm.count)
            stats_done_mm.append(s_mm.count)
            t_all = s_mm.count
            sA = slotA[:, 0:1024]
            sB = slotB[:, 0:1024]
            p.op("vector", lambda v: v.tensor_scalar(
                out=sB, in0=ps[0:1, 2048:3072], scalar1=float(1.0 / D),
                scalar2=None, op0=OP.mult), waits=[(s_mm, t_all)])
            p.op("vector", lambda v: v.tensor_scalar(
                out=ps[0:1, 3072:4096], in0=ps[0:1, 3072:4096],
                scalar1=float(1.0 / D), scalar2=None, op0=OP.mult))
            p.op("vector", lambda v: v.tensor_tensor(
                out=sA, in0=sB, in1=sB, op=OP.mult), sep=True)
            p.op("vector", lambda v: v.tensor_tensor(
                out=ps[0:1, 3072:4096], in0=ps[0:1, 3072:4096], in1=sA,
                op=OP.subtract), inc=(s_v, 1), sep=True)
            t_var = s_v.count
            p.op("scalar", lambda e: e.activation(
                out=sA, in_=ps[0:1, 3072:4096], func=AF.Sqrt,
                bias=eps11, scale=1.0), waits=[(s_v, t_var)], inc=(s_s, 1))
            t_sq2 = s_s.count
            p.op("vector", lambda v: v.reciprocal(out=sA, in_=sA),
                 waits=[(s_s, t_sq2)], inc=(s_v, 1))
            t_rows2 = s_v.count
            for which, (rowsrc, cdst) in enumerate([(sB, m2_col), (sA, r2_col)]):
                for cc in range(8):
                    k = which * 8 + cc
                    p.op("tensor", lambda e, rowsrc=rowsrc, cc=cc, k=k: e.matmul(
                        bank(2, 16)[:, k:k + 1],
                        rowsrc[:, cc * 128:(cc + 1) * 128], id11,
                        start=(k == 0), stop=(k == 15)),
                        waits=[(s_v, t_rows2)] if k == 0 else [],
                        inc=(s_mm, 1) if k == 15 else None)
            t_cmm = s_mm.count
            p.op("vector", lambda v, th=th: v.tensor_copy(
                out=m2_col[:, th * 8:th * 8 + 8], in_=bank(2, 16)[:, 0:8]),
                waits=[(s_mm, t_cmm)])
            p.op("vector", lambda v, th=th: v.tensor_copy(
                out=r2_col[:, th * 8:th * 8 + 8], in_=bank(2, 16)[:, 8:16]),
                inc=(s_v, 1))
            t_row_v = s_v.count
        return x1_store_hist, x1s.totals(), s_v.count, s_mm.count

    qsc = {}

    def alloc_quant_scratch():
        qsc["in"] = [OV_MAIN.alloc("qin0", [128, 2048]), OV_MAIN.alloc("qin1", [128, 2048])]
        qsc["tstg"] = [OV_MAIN.alloc("qtstg0", [128, 2048], BF16),
                       OV_MAIN.alloc("qtstg1", [128, 2048], BF16)]
        for si in range(2):
            st = {"prev_v": 0, "prev_s": 0, "prev_g": 0}
            for nm in ("y", "c", "d"):
                st[nm] = OV_Q.alloc(f"q{nm}{si}", [128, QW])
            for nm in ("amx", "lg", "s0", "cs"):
                st[nm] = OV_Q.alloc(f"q{nm}{si}", [128, 64])
            st["tt"] = st["d"][:, 0:64]
            st["gg"] = st["lg"]
            st["invs"] = st["cs"]
            st["scl"] = st["amx"]
            qsc[si] = st
        qsc["silu"] = qsc[1]["y"][:, 0:256]

    def q_pair(jobs):
        assert len(jobs) == 2
        for job in jobs:
            s = qsc[job["set"]]
            job["S"] = s
            job["src3"] = job["src"].rearrange("p (b k) -> p b k", k=32)
            job["velt"] = c_velt4 if job["mfrac"] == 3 else c_velt2
            job["fix"] = c_fix3 if job["mfrac"] == 3 else c_fix1
            job["mn"] = 7.5 if job["mfrac"] == 3 else 6.0
            job["vp"], job["sp"], job["gp"] = s["prev_v"], s["prev_s"], s["prev_g"]

        def opV(job, fn, extra=(), mark=None):
            w = ([(s_v, job["vp"])] if job["vp"] else []) + list(extra)
            p.op("vector", fn, waits=w, inc=(s_v, 1))
            job["vp"] = s_v.count
            if mark:
                job[mark] = s_v.count

        def opP(job, fn, extra=(), mark=None):
            w = ([(s_g, job["gp"])] if job["gp"] else []) + list(extra)
            p.op("gpsimd", fn, waits=w, inc=(s_g, 1))
            job["gp"] = s_g.count
            if mark:
                job[mark] = s_g.count

        def opS(job, fn, extra=(), mark=None):
            w = ([(s_s, job["sp"])] if job["sp"] else []) + list(extra)
            p.op("scalar", fn, waits=w, inc=(s_s, 1))
            job["sp"] = s_s.count
            if mark:
                job[mark] = s_s.count

        for job in jobs:
            extra = list(job.get("w_v", ()))
            if job["sp"]:
                extra.append((s_s, job["sp"]))
            if job["gp"]:
                extra.append((s_g, job["gp"]))
            opV(job, lambda v, s=job["S"], src3=job["src3"]: v.tensor_reduce(
                out=s["amx"], in_=src3, axis=AX.X, op=OP.max,
                apply_absolute_value=True), extra=extra)
        for job in jobs:
            opV(job, lambda v, s=job["S"]: v.tensor_scalar(
                out=s["amx"], in0=s["amx"], scalar1=float(2.0 ** -96),
                scalar2=None, op0=OP.max), mark="m_amx")
        for job in jobs:
            opS(job, lambda e, s=job["S"]: e.activation(
                out=s["lg"], in_=s["amx"], func=AF.Ln, bias=0.0, scale=1.0),
                extra=list(job.get("w_s", ())) + [(s_v, job["m_amx"])],
                mark="m_lg")
        for job in jobs:
            opV(job, lambda v, s=job["S"]: v.tensor_scalar(
                out=s["tt"], in0=s["lg"], scalar1=INV_LN2, scalar2=c_round,
                op0=OP.mult, op1=OP.add), extra=[(s_s, job["m_lg"])])
        for job in jobs:
            opV(job, lambda v, s=job["S"]: v.tensor_scalar(
                out=s["tt"], in0=s["tt"], scalar1=c_round, scalar2=None,
                op0=OP.subtract), mark="m_r")
        for job in jobs:
            opS(job, lambda e, s=job["S"]: e.activation(
                out=s["s0"], in_=s["tt"], func=AF.Exp, bias=0.0, scale=LN2C),
                extra=[(s_v, job["m_r"])], mark="m_s0")
        first = True
        for fn in (
            lambda v, s: v.tensor_scalar(out=s["cs"], in0=s["s0"],
                                         scalar1=c_velt_s, scalar2=None,
                                         op0=OP.mult),
            lambda v, s: v.tensor_tensor(out=s["tt"], in0=s["cs"], in1=s["s0"],
                                         op=OP.subtract),
            lambda v, s: v.tensor_tensor(out=s["s0"], in0=s["cs"], in1=s["tt"],
                                         op=OP.subtract),
            lambda v, s: v.tensor_tensor(out=s["gg"], in0=s["s0"], in1=s["amx"],
                                         op=OP.is_gt),
            lambda v, s: v.tensor_scalar(out=s["gg"], in0=s["gg"], scalar1=-0.5,
                                         scalar2=1.0, op0=OP.mult, op1=OP.add),
            lambda v, s: v.tensor_tensor(out=s["s0"], in0=s["s0"], in1=s["gg"],
                                         op=OP.mult),
            lambda v, s: v.tensor_scalar(out=s["tt"], in0=s["s0"], scalar1=2.0,
                                         scalar2=None, op0=OP.mult),
            lambda v, s: v.tensor_tensor(out=s["gg"], in0=s["tt"], in1=s["amx"],
                                         op=OP.is_le),
            lambda v, s: v.tensor_scalar(out=s["gg"], in0=s["gg"], scalar1=1.0,
                                         scalar2=None, op0=OP.add),
            lambda v, s: v.tensor_tensor(out=s["s0"], in0=s["s0"], in1=s["gg"],
                                         op=OP.mult),
            lambda v, s: v.reciprocal(out=s["invs"], in_=s["s0"]),
            lambda v, s: v.tensor_scalar(out=s["invs"], in0=s["invs"],
                                         scalar1=4.0, scalar2=None, op0=OP.mult),
            lambda v, s: v.tensor_scalar(out=s["scl"], in0=s["s0"], scalar1=0.25,
                                         scalar2=None, op0=OP.mult),
        ):
            for job in jobs:
                extra = [(s_s, job["m_s0"])] if first else ()
                opV(job, lambda v, fn=fn, s=job["S"]: fn(v, s), extra=extra)
            first = False
        for st_ in range(4):
            odd = (st_ % 2 == 1)
            for job in jobs:
                s = job["S"]
                bsl = slice(st_ * 16, (st_ + 1) * 16)
                job["y3"] = s["y"].rearrange("p (b k) -> p b k", k=32)
                job["src3s"] = job["src3"][:, bsl, :]
                job["qeF"] = job["src"][:, st_ * QW:(st_ + 1) * QW]
                job["invs3"] = s["invs"][:, bsl].unsqueeze(2).broadcast_to([128, 16, 32])
                job["scl3"] = s["scl"][:, bsl].unsqueeze(2).broadcast_to([128, 16, 32])
            for job in jobs:
                extra = [(s_g, job["gp"])] if job["gp"] else []
                opV(job, lambda v, y3=job["y3"], src3s=job["src3s"],
                         invs3=job["invs3"]: v.tensor_tensor(
                    out=y3, in0=src3s, in1=invs3,
                    op=OP.mult), extra=extra, mark="m1")
            for job in jobs:
                opS(job, lambda e, s=job["S"], velt=job["velt"]: e.activation(
                    out=s["c"], in_=s["y"], func=AF.Copy, bias=0.0,
                    scale=velt), extra=[(s_v, job["m1"])], mark="m2")
            for job in jobs:
                opS(job, lambda e, s=job["S"]: e.activation(
                    out=s["d"], in_=s["y"], func=AF.Square, bias=0.0, scale=1.0),
                    mark="m3")
            for job in jobs:
                opV(job, lambda x, s=job["S"], qeF=job["qeF"]: x.tensor_tensor(
                    out=qeF, in0=s["c"], in1=s["y"], op=OP.subtract),
                    extra=[(s_s, job["m2"])])
            for job in jobs:
                opV(job, lambda x, s=job["S"], qeF=job["qeF"]: x.tensor_tensor(
                    out=s["c"], in0=s["c"], in1=qeF, op=OP.subtract))
            for job in jobs:
                opV(job, lambda v, s=job["S"]: v.tensor_scalar(
                    out=s["d"], in0=s["d"], scalar1=1.0, scalar2=None,
                    op0=OP.is_ge), extra=[(s_s, job["m3"])])
            for job in jobs:
                opV(job, lambda v, s=job["S"], qeF=job["qeF"],
                         fx=job["fix"]: v.tensor_scalar(
                    out=qeF, in0=s["y"], scalar1=fx,
                    scalar2=fx, op0=OP.add, op1=OP.subtract))
            for job in jobs:
                opV(job, lambda x, s=job["S"], qeF=job["qeF"]: x.tensor_tensor(
                    out=s["y"], in0=s["c"], in1=qeF, op=OP.subtract))
            for job in jobs:
                opV(job, lambda x, s=job["S"]: x.tensor_tensor(
                    out=s["d"], in0=s["d"], in1=s["y"], op=OP.mult))
            for job in jobs:
                opV(job, lambda v, s=job["S"], qeF=job["qeF"]: v.tensor_tensor(
                    out=qeF, in0=qeF, in1=s["d"], op=OP.add))
            for job in jobs:
                opV(job, lambda v, qeF=job["qeF"], mn=float(job["mn"]):
                     v.tensor_scalar(
                    out=qeF, in0=qeF, scalar1=mn,
                    scalar2=-mn, op0=OP.min, op1=OP.max))
            for job in jobs:
                opV(job, lambda v, src3s=job["src3s"], scl3=job["scl3"]:
                     v.tensor_tensor(
                    out=src3s, in0=src3s, in1=scl3,
                    op=OP.mult))
        for job in jobs:
            s = job["S"]
            s["prev_v"], s["prev_s"], s["prev_g"] = job["vp"], job["sp"], job["gp"]
        return [job["vp"] for job in jobs]

    def emit_qtile_tr_only(srcbf, waits_t, bank_free):
        marks = []
        for b4 in range(4):
            bkk = 4 + b4
            for tt_ in range(4):
                idx = b4 * 4 + tt_
                w = []
                if tt_ == 0:
                    w = list(waits_t) if b4 == 0 else []
                    if bank_free:
                        w.append(bank_free[b4])
                p.op("tensor", lambda e, idx=idx, bkk=bkk, tt_=tt_, srcbf=srcbf: e.matmul(
                    ps[:, bkk * 512 + tt_ * 128:bkk * 512 + (tt_ + 1) * 128],
                    srcbf[:, idx * 128:(idx + 1) * 128], ident,
                    start=(tt_ == 0), stop=(tt_ == 3), is_transpose=True),
                    waits=w, inc=(s_mm, 1) if tt_ == 3 else None)
            marks.append(s_mm.count)
        return marks

    def emit_qtile_evacs(par, tr4, dst_dma_fn, store_stream, store_waits):
        tstg = qsc["tstg"][par]
        ev = []
        for b4 in range(4):
            bkk = 4 + b4
            evw = [(s_mm, tr4[b4])] + (list(store_waits) if b4 == 0 else [])
            p.op("scalar", lambda e, b4=b4, bkk=bkk, tstg=tstg: e.activation(
                out=tstg[:, b4 * 512:(b4 + 1) * 512],
                in_=ps[:, bkk * 512:bkk * 512 + 512],
                func=AF.Copy, bias=0.0, scale=1.0),
                waits=evw, inc=(s_s, 1))
            ev.append((s_s, s_s.count))
        ssem, t_s = store_stream.load(dst_dma_fn(tstg),
                                      waits=[ev[-1]] + list(store_waits))
        return ev, (ssem, t_s)

    def phase_wquant(entry_waits_v, entry_waits_mm):
        alloc_quant_scratch()
        qls = Stream(p, "qls", "sync", 2)
        qst = Stream(p, "qst", "gpsimd", 2)

        wgT3 = wgqT_scr.rearrange("(hc q) o -> hc q o", q=128)
        wuT3 = wuqT_scr.rearrange("(hc q) o -> hc q o", q=128)

        jobs = []
        for oc in range(IFF // 128):
            jobs.append((lambda oc=oc: wg_h[oc * 128:(oc + 1) * 128, :],
                         lambda tstg, oc=oc: (lambda g: g.dma_start(
                             out=wgT3[:, :, oc * 128:(oc + 1) * 128]
                                 .transpose([1, 0, 2]),
                             in_=tstg.rearrange("q (hc j) -> q hc j", j=128)))))
        for oc in range(IFF // 128):
            jobs.append((lambda oc=oc: wu_h[oc * 128:(oc + 1) * 128, :],
                         lambda tstg, oc=oc: (lambda g: g.dma_start(
                             out=wuT3[:, :, oc * 128:(oc + 1) * 128]
                                 .transpose([1, 0, 2]),
                             in_=tstg.rearrange("q (hc j) -> q hc j", j=128)))))
        for dc in range(16):
            wdT3 = wdqT_scr[dc].rearrange("(ic q) j -> ic q j", q=128)
            for islab in range(4):
                jobs.append((lambda dc=dc, islab=islab:
                             wd_h[dc * 128:(dc + 1) * 128,
                                  islab * 2048:(islab + 1) * 2048],
                             lambda tstg, dc=dc, islab=islab, wdT3=wdT3:
                             (lambda g: g.dma_start(
                                 out=wdT3[islab * 16:(islab + 1) * 16]
                                     .transpose([1, 0, 2]),
                                 in_=tstg.rearrange("q (ic j) -> q ic j", j=128)))))

        tr_marks = {}
        st_marks = {}
        pend = None
        ev_last = None
        for tp in range(0, len(jobs), 2):
            lsems = []
            for j in (0, 1):
                t = tp + j
                if t < 2:
                    lw = [(s_v, entry_waits_v), (s_mm, entry_waits_mm)]
                else:
                    lw = [(s_mm, tr_marks[t - 2])]
                lsems.append(qls.load(lambda e, srcfn=jobs[t][0], j=j: e.dma_start(
                    out=qsc["in"][j], in_=srcfn()), waits=lw))
            pj = []
            for j in (0, 1):
                wv = [lsems[j]]
                if tp + j < 2:
                    wv.append((s_v, entry_waits_v))
                pj.append(dict(src=qsc["in"][j], set=j, mfrac=1, w_v=wv, w_s=[]))
            mfin = q_pair(pj)
            if pend is not None:
                pt, ppar, ptr4, pdst = pend
                sw = [st_marks[pt - 2]] if pt - 2 in st_marks else []
                ev_last, st_marks[pt] = emit_qtile_evacs(ppar, ptr4, pdst, qst, sw)
            for j in (0, 1):
                t = tp + j
                tw = [(s_v, mfin[j])]
                if t == 0:
                    tw.append((s_mm, entry_waits_mm))
                tr4 = emit_qtile_tr_only(qsc["in"][j], tw, ev_last)
                tr_marks[t] = tr4[-1]
                if j == 0:
                    sw = [st_marks[t - 2]] if t - 2 in st_marks else []
                    ev_last, st_marks[t] = emit_qtile_evacs(j, tr4, jobs[t][1], qst, sw)
                else:
                    pend = (t, j, tr4, jobs[t][1])
        pt, ppar, ptr4, pdst = pend
        sw = [st_marks[pt - 2]] if pt - 2 in st_marks else []
        ev_last, st_marks[pt] = emit_qtile_evacs(ppar, ptr4, pdst, qst, sw)
        return qst.totals(), s_v.count, s_mm.count

    def phase_ffn(x1_done, wq_done, p5_v, p5_mm):
        x1r = [OV_MAIN.alloc("x1r0", [128, 512]), OV_MAIN.alloc("x1r1", [128, 512])]
        outst = [OV_MAIN.alloc("outst0", [128, 512]), OV_MAIN.alloc("outst1", [128, 512])]
        xT4 = OV_MAIN.alloc("xT4", [128, 8192])
        repB1 = OV_MAIN.alloc("repB1", [128, 2048])
        repsb = OV_MAIN.alloc("repsb", [128, 2048])
        h2q = OV_MAIN.alloc("h2q", [128, 16 * 512], BF16)
        interq = OV_MAIN.alloc("interq", [128, 64 * 512], BF16)
        wst = OV_MAIN.alloc("wst", [128, 16384], BF16)

        xls = Stream(p, "xls", "gpsimd", 2)
        xle = Stream(p, "xle", "gpsimd", 2)
        wgl = Stream(p, "wgl", "sync", 4)
        wdl = Stream(p, "wdl", "sync", 2)
        ous = Stream(p, "ous", "gpsimd", 2)

        wgT3 = wgqT_scr.rearrange("(hc q) o -> hc q o", q=128)
        wuT3 = wuqT_scr.rearrange("(hc q) o -> hc q o", q=128)
        wst3 = wst.rearrange("q (pr w hc c) -> q pr w hc c", pr=2, w=2, hc=16)
        wstd3 = wst.rearrange("q (pr ic c) -> q pr ic c", pr=2, ic=64)

        ent = [(s_v, p5_v), (s_mm, p5_mm)]
        t_prev_ev = 0
        for which, seg in ((0, 3), (1, 4)):
            for cc in range(NC):
                w = list(ent) if (which == 0 and cc == 0) else []
                if which == 1 and cc == 0:
                    w = [(s_v, t_prev_ev)]
                p.op("tensor", lambda e, which=which, seg=seg, cc=cc: e.matmul(
                    ps[0:1, cc * 128:(cc + 1) * 128],
                    mod_cols[:, seg * NC + cc:seg * NC + cc + 1], ident,
                    start=(cc % 4 == 0), stop=(cc % 4 == 3)),
                    waits=w, inc=(s_mm, 1) if cc == NC - 1 else None)
            t_row = s_mm.count
            dstrow = slotA if which == 0 else slotB
            p.op("vector", lambda v, dstrow=dstrow: v.tensor_copy(
                out=dstrow, in_=ps[0:1, 0:2048]),
                waits=[(s_mm, t_row)], inc=(s_v, 1))
            t_prev_ev = s_v.count
        p.op("vector", lambda v: v.tensor_scalar(
            out=slotB, in0=slotB, scalar1=1.0, scalar2=None, op0=OP.add),
            inc=(s_v, 1), sep=True)
        t_rows = s_v.count
        evh = []
        for krep in range(8):
            which, piece = divmod(krep, 4)
            src = slotB if which == 0 else slotA
            rdst = repB1 if which == 0 else repsb
            w = [(s_v, t_rows)]
            if krep >= 2:
                w.append((s_v, evh[krep - 2]))
            p.op("tensor", lambda e, src=src, piece=piece, krep=krep: e.matmul(
                bank(krep % 2), ones128, src[:, piece * 512:(piece + 1) * 512],
                start=True, stop=True), waits=w, inc=(s_mm, 1))
            t_k = s_mm.count
            p.op("vector", lambda v, rdst=rdst, piece=piece, krep=krep: v.tensor_copy(
                out=rdst[:, piece * 512:(piece + 1) * 512], in_=bank(krep % 2)),
                waits=[(s_mm, t_k)], inc=(s_v, 1))
            evh.append(s_v.count)
        t_reps = s_v.count

        out_stores = []
        wgl_hist = []
        dbg_mark = [0]
        dbg_rows = [256]
        t_prev_group_done = 0
        for g in range(4):
            gtc = [g * 4 + tc for tc in range(4)]
            ln_ev = []
            x1l_hist = []
            for fc in range(NC):
                lw = []
                if fc >= 2:
                    lw = [(s_mm, x1l_hist[fc - 2])]
                elif g == 0:
                    lw = list(x1_done) + ent
                elif g > 0:
                    lw = [(s_mm, t_prev_group_done)]
                lsem, t_l = xls.load(lambda gg_, fc=fc, g=g: gg_.dma_start(
                    out=x1r[fc % 2], in_=x1_scr[fc * 128:(fc + 1) * 128,
                                                g * 512:(g + 1) * 512]),
                    waits=lw)
                bkk = 4 + (fc % 2)
                tw = [(lsem, t_l)]
                if len(ln_ev) >= 2:
                    tw.append((s_v, ln_ev[-2]))
                if fc < 2 and g == 0:
                    tw.append((s_s, s_s.count))
                for tc in range(4):
                    p.op("tensor", lambda e, tc=tc, bkk=bkk, fc=fc: e.matmul(
                        ps[:, bkk * 512 + tc * 128:bkk * 512 + (tc + 1) * 128],
                        x1r[fc % 2][:, tc * 128:(tc + 1) * 128], ident,
                        start=(tc == 0), stop=(tc == 3), is_transpose=True),
                        waits=tw if tc == 0 else [],
                        inc=(s_mm, 1) if tc == 3 else None)
                t_tr = s_mm.count
                x1l_hist.append(t_tr)
                for tc in range(4):
                    p.op("vector", lambda v, tc=tc, fc=fc, bkk=bkk, g=g: v.tensor_scalar(
                        out=xT4[:, tc * 2048 + fc * 128:tc * 2048 + (fc + 1) * 128],
                        in0=ps[:, bkk * 512 + tc * 128:bkk * 512 + (tc + 1) * 128],
                        scalar1=m2_col[:, g * 4 + tc:g * 4 + tc + 1],
                        scalar2=r2_col[:, g * 4 + tc:g * 4 + tc + 1],
                        op0=OP.subtract, op1=OP.mult),
                        waits=[(s_mm, t_tr)] if tc == 0 else [],
                        inc=(s_v, 1) if tc == 3 else None)
                ln_ev.append(s_v.count)
            t_u = s_v.count
            for tc in range(4):
                p.op("gpsimd", lambda gg_, tc=tc: gg_.tensor_tensor(
                    out=xT4[:, tc * 2048:(tc + 1) * 2048],
                    in0=xT4[:, tc * 2048:(tc + 1) * 2048],
                    in1=repB1, op=OP.mult),
                    waits=([(s_v, t_u), (s_v, t_reps)] if tc == 0 else []),
                    inc=(s_g, 1))
                t_m = s_g.count
                p.op("vector", lambda v, tc=tc: v.tensor_tensor(
                    out=xT4[:, tc * 2048:(tc + 1) * 2048],
                    in0=xT4[:, tc * 2048:(tc + 1) * 2048],
                    in1=repsb, op=OP.add),
                    waits=[(s_g, t_m)], inc=(s_v, 1))
            t_h2add = s_v.count
            hq_ev = 0
            h2q3 = h2q.rearrange("q (fc t) -> q fc t", t=512)
            for tcp in (0, 2):
                mfin = q_pair([dict(src=xT4[:, tc * 2048:(tc + 1) * 2048],
                                    set=tc - tcp, mfrac=3,
                                    w_v=[(s_v, t_h2add)], w_s=[])
                               for tc in (tcp, tcp + 1)])
                for tc in (tcp, tcp + 1):
                    ev_by_b4 = []
                    for b4 in range(4):
                        bkk = 4 + (b4 % 2)
                        tw2 = []
                        if b4 == 0:
                            tw2 = [(s_v, mfin[tc - tcp])]
                            if hq_ev:
                                tw2.append((s_s, hq_ev))
                            if ln_ev:
                                tw2.append((s_v, ln_ev[-1]))
                        elif b4 >= 2:
                            tw2 = [(s_s, ev_by_b4[b4 - 2])]
                        for tt_ in range(4):
                            idx = b4 * 4 + tt_
                            p.op("tensor", lambda e, idx=idx, bkk=bkk, tt_=tt_, tc=tc: e.matmul(
                                ps[:, bkk * 512 + tt_ * 128:bkk * 512 + (tt_ + 1) * 128],
                                xT4[:, tc * 2048 + idx * 128:tc * 2048 + (idx + 1) * 128],
                                ident,
                                start=(tt_ == 0), stop=(tt_ == 3), is_transpose=True),
                                waits=tw2 if tt_ == 0 else [],
                                inc=(s_mm, 1) if tt_ == 3 else None)
                        t_tr2 = s_mm.count
                        h2ev_w = [(s_mm, t_tr2)]
                        if debug == "ffn" and tc == 0 and b4 == 0 and dbg_mark[0]:
                            h2ev_w.append((p.sem("dbgst"), dbg_mark[0]))
                        p.op("scalar", lambda e, b4=b4, tc=tc, bkk=bkk, h2q3=h2q3: e.activation(
                            out=h2q3[:, b4 * 4:(b4 + 1) * 4, tc * 128:(tc + 1) * 128],
                            in_=ps[:, bkk * 512:bkk * 512 + 512]
                                .rearrange("q (i c) -> q i c", c=128),
                            func=AF.Copy, bias=0.0, scale=1.0),
                            waits=h2ev_w, inc=(s_s, 1))
                        ev_by_b4.append(s_s.count)
                        hq_ev = s_s.count
            t_h2q_v = s_v.count
            t_h2q_s = s_s.count
            t_h2q_mm = s_mm.count
            if debug == "ffn":
                p.op("gpsimd", lambda gg_, g=g: gg_.dma_start(
                    out=h2q_dbg[g], in_=h2q), waits=[(s_v, t_h2q_v)],
                    inc=(p.sem("dbgst"), 16))
            dbg_mark[0] = p.sem("dbgst").count

            iq_tr_mm = 0
            iq_ev_v = 0
            ring = [x1r[0], x1r[1], outst[0], outst[1]]
            cp_marks = []
            pm_marks = []
            trtc = {tc: t_h2q_mm for tc in range(4)}
            for oq in range(4):
                for op8 in range(8):
                    opi = oq * 8 + op8
                    gop = g * 32 + opi
                    if opi < 2:
                        lw = (list(wq_done) + [(s_mm, t_h2q_mm)]) if g == 0 else \
                            [(s_mm, t_prev_group_done)]
                    else:
                        lw = [(s_mm, wgl_hist[gop - 2])]
                    par = gop % 2
                    lsg, t_lg = wgl.load(lambda e, opi=opi, par=par: e.dma_start(
                        out=wst3[:, par, 0],
                        in_=wgT3[:, :, opi * 256:(opi + 1) * 256].transpose([1, 0, 2])),
                        waits=lw)
                    lsu, t_lu = wgl.load(lambda e, opi=opi, par=par: e.dma_start(
                        out=wst3[:, par, 1],
                        in_=wuT3[:, :, opi * 256:(opi + 1) * 256].transpose([1, 0, 2])))
                    for tc in range(4):
                        nn = len(cp_marks)
                        bko = 2 * (nn % 2)
                        slot = ring[nn % 4]
                        mm_w = [(lsg, t_lg), (lsu, t_lu), (s_v, t_h2q_v),
                                (s_s, t_h2q_s)]
                        if nn >= 2:
                            mm_w.append((s_s, cp_marks[nn - 2]))
                        for kc in range(NC):
                            p.op("tensor", lambda e, kc=kc, tc=tc, par=par, bko=bko: e.matmul(
                                bank(bko, 256), h2q[:, kc * 512 + tc * 128:kc * 512 + (tc + 1) * 128],
                                wst3[:, par, 0, kc], start=(kc == 0), stop=(kc == NC - 1)),
                                waits=mm_w if kc == 0 else [])
                        for kc in range(NC):
                            p.op("tensor", lambda e, kc=kc, tc=tc, par=par, bko=bko: e.matmul(
                                bank(bko + 1, 256), h2q[:, kc * 512 + tc * 128:kc * 512 + (tc + 1) * 128],
                                wst3[:, par, 1, kc], start=(kc == 0), stop=(kc == NC - 1)),
                                waits=[], inc=(s_mm, 1) if kc == NC - 1 else None)
                        t_gu = s_mm.count
                        si_w = [(s_mm, t_gu)]
                        if nn >= 4:
                            si_w.append((s_g, pm_marks[nn - 4]))
                        elif nn >= 2 and len(out_stores) >= 2:
                            si_w += out_stores[-2:]
                        p.op("scalar", lambda e, bko=bko, slot=slot: e.activation(
                            out=slot[:, 0:256], in_=bank(bko, 256), func=AF.Silu,
                            bias=0.0, scale=1.0),
                            waits=si_w, inc=(s_s, 1))
                        p.op("scalar", lambda e, bko=bko, slot=slot: e.activation(
                            out=slot[:, 256:512], in_=bank(bko + 1, 256),
                            func=AF.Copy, bias=0.0, scale=1.0),
                            inc=(s_s, 1))
                        cp_marks.append(s_s.count)
                        p.op("gpsimd", lambda x, tc=tc, op8=op8, slot=slot: x.tensor_tensor(
                            out=xT4[:, tc * 2048 + op8 * 256:tc * 2048 + (op8 + 1) * 256],
                            in0=slot[:, 0:256], in1=slot[:, 256:512], op=OP.mult),
                            waits=[(s_s, cp_marks[-1]), (s_mm, trtc[tc])],
                            inc=(s_g, 1))
                        pm_marks.append(s_g.count)
                    wgl_hist.append(s_mm.count)
                iq3 = interq.rearrange("q (ic t) -> q ic t", t=512)
                mfin4 = [0] * 4
                for tcp in (0, 2):
                    mm_ = q_pair([dict(src=xT4[:, tc * 2048:(tc + 1) * 2048],
                                       set=tc - tcp, mfrac=3,
                                       w_v=[(s_g, pm_marks[-1])], w_s=[])
                                  for tc in (tcp, tcp + 1)])
                    mfin4[tcp], mfin4[tcp + 1] = mm_
                for tc in range(4):
                    last_v = mfin4[tc]
                    ev_by_b4 = []
                    for b4 in range(4):
                        bkk = 4 + (b4 % 2)
                        tw2 = []
                        if b4 == 0:
                            tw2 = [(s_v, last_v)]
                            if iq_ev_v:
                                tw2.append((s_s, iq_ev_v))
                        elif b4 >= 2:
                            tw2 = [(s_s, ev_by_b4[b4 - 2])]
                        for tt_ in range(4):
                            idx = b4 * 4 + tt_
                            p.op("tensor", lambda e, idx=idx, bkk=bkk, tt_=tt_, tc=tc: e.matmul(
                                ps[:, bkk * 512 + tt_ * 128:bkk * 512 + (tt_ + 1) * 128],
                                xT4[:, tc * 2048 + idx * 128:tc * 2048 + (idx + 1) * 128],
                                ident,
                                start=(tt_ == 0), stop=(tt_ == 3), is_transpose=True),
                                waits=tw2 if tt_ == 0 else [],
                                inc=(s_mm, 1) if tt_ == 3 else None)
                        t_tr2 = s_mm.count
                        iqev_w = [(s_mm, t_tr2)]
                        if debug == "ffn" and oq == 0 and tc == 0 and b4 == 0 and dbg_mark[0]:
                            iqev_w.append((p.sem("dbgst"), dbg_mark[0]))
                        p.op("scalar", lambda e, b4=b4, tc=tc, oq=oq, bkk=bkk, iq3=iq3: e.activation(
                            out=iq3[:, oq * 16 + b4 * 4:oq * 16 + (b4 + 1) * 4,
                                    tc * 128:(tc + 1) * 128],
                            in_=ps[:, bkk * 512:bkk * 512 + 512]
                                .rearrange("q (i c) -> q i c", c=128),
                            func=AF.Copy, bias=0.0, scale=1.0),
                            waits=iqev_w, inc=(s_s, 1))
                        ev_by_b4.append(s_s.count)
                        iq_ev_v = s_s.count
                    iq_tr_mm = s_mm.count
                    trtc[tc] = s_mm.count
            t_stage_g = s_g.count
            t_iq_v = s_v.count
            t_iq_s = s_s.count
            t_iq_mm = s_mm.count
            if debug == "ffn":
                p.op("gpsimd", lambda gg_, g=g: gg_.dma_start(
                    out=iq_dbg[g], in_=interq), waits=[(s_v, t_iq_v)],
                    inc=(p.sem("dbgst"), 16))
            dbg_mark[0] = p.sem("dbgst").count

            dn_stt = {}
            for dm in range(16):
                gdm = g * 16 + dm
                par = gdm % 2
                if dm < 2:
                    lw = (list(wq_done) if g == 0 else []) + [(s_mm, t_iq_mm)]
                else:
                    lw = [(s_mm, dn_mm_hist[gdm - 2])]
                wdT3 = wdqT_scr[dm].rearrange("(ic q) j -> ic q j", q=128)
                lsd, t_ld = wdl.load(lambda e, dm=dm, par=par, wdT3=wdT3: e.dma_start(
                    out=wstd3[:, par], in_=wdT3.transpose([1, 0, 2])), waits=lw)
                xw = [(s_v, dn_stt[par])] if par in dn_stt else []
                if len(out_stores) >= 2:
                    xw.append(out_stores[-2])
                if dm < 2:
                    xw.append((s_g, t_stage_g))
                if gdm < 2 and g == 0:
                    xw = list(x1_done) + [(s_g, t_stage_g)]
                lsx, t_lx = xle.load(lambda gg_, dm=dm, g=g, par=par: gg_.dma_start(
                    out=x1r[par] if False else outst[par],
                    in_=x1_scr[dm * 128:(dm + 1) * 128, g * 512:(g + 1) * 512]),
                    waits=xw)
                bkd = par
                mm_w = [(lsd, t_ld), (s_v, t_iq_v), (s_s, t_iq_s)]
                if bkd in dn_stt:
                    mm_w.append((s_v, dn_stt[bkd]))
                for ic in range(64):
                    p.op("tensor", lambda e, ic=ic, par=par, bkd=bkd: e.matmul(
                        bank(bkd), wstd3[:, par, ic],
                        interq[:, ic * 512:(ic + 1) * 512],
                        start=(ic == 0), stop=(ic == 63)),
                        waits=mm_w if ic == 0 else [],
                        inc=(s_mm, 1) if ic == 63 else None)
                t_dn = s_mm.count
                dn_mm_hist.append(t_dn)
                st_w = [(s_mm, t_dn), (lsx, t_lx)]
                if len(out_stores) >= 2:
                    st_w.append(out_stores[-2])
                p.op("vector", lambda v, dm=dm, par=par, bkd=bkd: v.scalar_tensor_tensor(
                    out=outst[par], in0=bank(bkd), scalar=gb_col[:, dm:dm + 1],
                    in1=outst[par], op0=OP.mult, op1=OP.add),
                    waits=st_w, inc=(s_v, 1))
                dn_stt[par] = s_v.count
                osem, t_os = ous.load(lambda gg_, dm=dm, g=g, par=par: gg_.dma_start(
                    out=outT[dm * 128:(dm + 1) * 128, g * 512:(g + 1) * 512],
                    in_=outst[par]), waits=[(s_v, dn_stt[par])])
                out_stores.append((osem, t_os))
            t_prev_group_done = s_mm.count
        return ous.totals()

    seg_slab_mm = []
    pair_proj_mm = []
    wo_mm_hist = []
    xpc_free_v = []
    stats_done_mm = []
    dn_mm_hist = []
    init_consts()
    mod_v, mod_mm, slab_done = phase_mod()
    t_h_done = []
    val_done = []
    if phases >= 2:
        OV_MAIN.reset()
        t_h_done = phase_ln1_h(mod_v, mod_mm, slab_done)
    x1_done = []
    if phases >= 3:
        OV_MAIN.reset()
        val_done, p3_v, p3_mm = phase_attn(t_h_done, p.sem("s_v").count,
                                           p.sem("s_mm").count)
    if phases >= 4:
        OV_MAIN.reset()
        _, x1_done, p4_v, p4_mm = phase_wo(val_done, p3_v, p3_mm)
    out_done = []
    if phases >= 5:
        wq_done, wq_v, wq_mm = phase_wquant(p4_v, p4_mm)
        OV_MAIN.reset()
        out_done = phase_ffn(x1_done, wq_done, wq_v, wq_mm)

    if debug == "mod":
        p.op("gpsimd", lambda g: g.dma_start(out=dbg[0:128, 0:6 * NC], in_=mod_cols),
             waits=[(s_v, s_v.count)], inc=(s_st, 16))
        p.op("gpsimd", lambda g: g.dma_start(out=dbg[0:128, 96:96 + NC], in_=sc_col),
             inc=(s_st, 16))
        p.op("gpsimd", lambda g: g.dma_start(out=dbg[200:201, 0:2048], in_=prow[:, 0:2048]),
             inc=(s_st, 16))

    if out_done:
        p.op("tensor", lambda e: e.nop(), waits=list(out_done))

    if phases < 5:
        for kc in range(NC):
            p.op("sync", lambda e, kc=kc: e.dma_start(
                out=outT[kc * 128:(kc + 1) * 128, 0:96], in_=mod_cols),
                waits=[(s_v, s_v.count)], inc=(s_out, 16))
        p.op("tensor", lambda e: e.nop(), waits=[(s_out, s_out.count)])
    if t_h_done:
        p.op("tensor", lambda e: e.nop(), waits=list(t_h_done))
    if val_done:
        p.op("tensor", lambda e: e.nop(), waits=list(val_done))
    if x1_done:
        p.op("tensor", lambda e: e.nop(), waits=list(x1_done))

    emit_program(nc, p)
    return nc


def _prep_inputs(L0_attn_x, L0_attn_w_qkv, L0_attn_w_o, L0_ffn_w_gate,
                 L0_ffn_w_up, L0_ffn_w_down, L0_c, L0_wmod):
    x = np.asarray(L0_attn_x, np.float32)
    wqkv_b = np.ascontiguousarray(
        np.asarray(L0_attn_w_qkv, np.float32).reshape(D, 48, 128).transpose(1, 0, 2))
    wo_b = np.ascontiguousarray(
        np.asarray(L0_attn_w_o, np.float32).reshape(D, 16, 128).transpose(1, 0, 2))
    wmod6_ = np.ascontiguousarray(
        np.asarray(L0_wmod, np.float32).reshape(D, 6, D).transpose(1, 0, 2))
    wg = np.ascontiguousarray(np.asarray(L0_ffn_w_gate, np.float32))
    wu = np.ascontiguousarray(np.asarray(L0_ffn_w_up, np.float32))
    wd = np.ascontiguousarray(np.asarray(L0_ffn_w_down, np.float32))
    c = np.asarray(L0_c, np.float32)
    maps = []
    for b in range(8):
        maps.append({
            "xT": np.ascontiguousarray(x[b].T),
            "wqkv": wqkv_b, "wo": wo_b, "wg": wg, "wu": wu, "wd": wd,
            "ccol": np.ascontiguousarray(c[b].reshape(NC, 128).T),
            "wmod6": wmod6_,
        })
    return maps


def kernel(**inputs):
    maps = _prep_inputs(**inputs)
    nc = build_nc(debug=DEBUG, phases=PHASES)
    res = run_bass_kernel_spmd(nc, maps, core_ids=list(range(8)))
    out = np.stack([np.ascontiguousarray(r["outT"].T) for r in res.results])
    return out
